# version 05
# baseline (speedup 1.0000x reference)
import numpy as np
from contextlib import ExitStack

import concourse.bass as bass
import concourse.mybir as mybir
from concourse.bass_utils import run_bass_kernel_spmd

F32 = mybir.dt.float32
BF16 = mybir.dt.bfloat16
ACT = mybir.ActivationFunctionType
ALU = mybir.AluOpType
AX = mybir.AxisListType

MAGIC = 12582912.0
INV1024 = 0.0009765625
INV65536 = 0.0000152587890625
N_CORES = 8
B, N, D, H = 4, 8192, 1024, 8
I3 = 3072
T = 4096
TH = 2048
KD = 8
TCH = 512
NTC = T // TCH
NTCH = TH // TCH
TMC = 128
NTM = T // TMC
OI = 24
LN_EPS = 1e-5

GU_TILE = 128 * 1024
DN_TILE = 128 * 3072
PACK = 12 * GU_TILE + 2 * DN_TILE


def _off_gu(mat, j, part):
    return ((mat * 3 + j) * 2 + part) * GU_TILE


def _off_dn(part):
    return 12 * GU_TILE + part * DN_TILE


class Sem:
    def __init__(self, h):
        self.h = h
        self.n = 0

    def inc(self, k=1):
        self.n += k
        return self.n


def build(detect_races=True):
    nc = bass.Bass("TRN2", target_bir_lowering=False,
                   detect_race_conditions=detect_races)
    dt = nc.dram_tensor
    h_fm = dt("h_fm", [D, T], F32, kind="ExternalInput").ap()
    h_tm = dt("h_tm", [T, D], F32, kind="ExternalInput").ap()
    wr_raw = dt("wr_raw", [D, D], F32, kind="ExternalInput").ap()
    wf_raw = dt("wf_raw", [D, D], F32, kind="ExternalInput").ap()
    attr = dt("attr", [1, 128], F32, kind="ExternalInput").ap()
    gamma = dt("gamma", [1, D], F32, kind="ExternalInput").ap()
    beta = dt("beta", [1, D], F32, kind="ExternalInput").ap()
    wg_sh = dt("wg_sh", [3 * 128, D], F32, kind="ExternalInput").ap()
    wu_sh = dt("wu_sh", [3 * 128, D], F32, kind="ExternalInput").ap()
    wd_sh = dt("wd_sh", [128, I3], F32, kind="ExternalInput").ap()
    bsel = dt("bsel", [4, 1], F32, kind="ExternalInput").ap()
    ident_in = dt("ident_in", [128, 128], F32, kind="ExternalInput").ap()
    out = dt("out", [T, D], F32, kind="ExternalOutput").ap()

    cc0_in = dt("cc0_in", [PACK], BF16)
    cc0_out = dt("cc0_out", [N_CORES, PACK], BF16)
    cc1_in = dt("cc1_in", [H, 130], F32)
    cc1_out = dt("cc1_out", [N_CORES, H, 130], F32)
    ccs_in = [dt(f"ccs_in{i}", [1, 16], F32) for i in range(4)]
    ccs_out = [dt(f"ccs_out{i}", [N_CORES, 16], F32) for i in range(4)]
    e_dram = dt("e_dram", [H, T], F32)
    fh_dram = dt("fh_dram", [1, D], F32)
    x1_dram = dt("x1_dram", [NTM, TMC, D], F32)
    gate_dram = dt("gate_dram", [OI, NTC, 128, TCH], F32)
    up_dram = dt("up_dram", [OI, NTC, 128, TCH], F32)
    g2_dram = dt("g2_dram", [OI, NTC, 128, TCH], F32)
    or_dram = dt("or_dram", [OI, NTC, 128, TCH], F32)

    ctxR = ExitStack()

    def mk(stack):
        sbf = lambda *a, **k: stack.enter_context(nc.sbuf_tensor(*a, **k))
        return sbf

    sbR = mk(ctxR)
    psumR = lambda *a, **k: ctxR.enter_context(nc.psum_tensor(*a, **k))
    mksem = lambda n: Sem(ctxR.enter_context(nc.semaphore(n)))

    sLD = mksem("sLD")
    sST = mksem("sST")
    sGD = mksem("sGD")
    sCC = mksem("sCC")
    sMM = mksem("sMM")
    sVE = mksem("sVE")
    sSC = mksem("sSC")
    sGP = mksem("sGP")

    cst = sbR("cst", [128, 8], F32)
    r7 = sbR("r7", [128, 1], F32)
    r32767 = sbR("r32767", [128, 1], F32)
    ident = sbR("ident", [128, 128], F32)
    attr_c = sbR("attr_c", [128, 1], F32)
    bsel_c = sbR("bsel_c", [4, 1], F32)
    s_s1 = sbR("s_s1", [128, 1], F32)
    s_rs1 = sbR("s_rs1", [128, 1], F32)
    s_s3 = sbR("s_s3", [128, 1], F32)
    s_rs3 = sbR("s_rs3", [128, 1], F32)
    s_s3m = sbR("s_s3m", [128, 1], F32)
    s_s2 = sbR("s_s2", [128, 1], F32)
    s_rs2 = sbR("s_rs2", [128, 1], F32)
    s_s4 = sbR("s_s4", [128, 1], F32)
    s_rs4 = sbR("s_rs4", [128, 1], F32)
    s_s42 = sbR("s_s42", [128, 1], F32)
    s_s5 = sbR("s_s5", [128, 1], F32)
    s_rs5 = sbR("s_rs5", [128, 1], F32)
    s_tmp = sbR("s_tmp", [128, 2], F32)
    s_bc = [sbR(f"s_bc{i}", [128, 128], F32) for i in range(4)]
    numh = [sbR(f"numh{i}", [128, H], F32) for i in range(2)]
    numT_h = [sbR(f"numTh{i}", [H, 128], F32) for i in range(2)]
    m_all = sbR("m_all", [1, 16], F32)
    d_all = sbR("d_all", [1, 16], F32)
    sr_negm = sbR("sr_negm", [1, 1], F32)
    mm8 = sbR("mm8", [1, 8], F32)
    t16 = sbR("t16", [1, 16], F32)
    w01 = sbR("w01", [1, 16], F32)
    dmrg = sbR("dmrg", [1, 8], F32)
    t8a = sbR("t8a", [1, 8], F32)
    mcol = sbR("mcol", [8, 1], F32)
    dcol = sbR("dcol", [8, 1], F32)
    w0c = sbR("w0c", [8, 1], F32)
    w1c = sbR("w1c", [8, 1], F32)
    numT = sbR("numT", [H, 128], F32)
    pack_sb = sbR("pack_sb", [H, 130], F32)
    gA = sbR("gA", [32, 130], F32)
    gB = sbR("gB", [32, 130], F32)
    mx32 = sbR("mx32", [32, 1], F32)
    ngm32 = sbR("ngm32", [32, 1], F32)
    wA32 = sbR("wA32", [32, 1], F32)
    wB32 = sbR("wB32", [32, 1], F32)
    t129 = sbR("t129", [32, 129], F32)
    cn129 = sbR("cn129", [32, 129], F32)
    dcb = sbR("dcb", [32, 1], F32)
    rdcb = sbR("rdcb", [32, 1], F32)
    ctx32 = sbR("ctx32", [32, 128], F32)
    ctxT = sbR("ctxT", [128, 32], F32)
    ctxT_h = sbR("ctxT_h", [128, 32], BF16)
    ctxT_l = sbR("ctxT_l", [128, 32], BF16)
    fh4 = sbR("fh4", [4, D], F32)
    fh1 = sbR("fh1", [1, D], F32)
    lnsum = sbR("lnsum", [128, 1], F32)
    negmu = sbR("negmu", [128, 1], F32)
    ssq = sbR("ssq", [128, 1], F32)
    lnsd = sbR("lnsd", [128, 1], F32)
    lnr = sbR("lnr", [128, 1], F32)
    sqscr = sbR("sqscr", [128, D], F32)
    xm32 = sbR("xm32", [128, NTM], F32)
    gmax = sbR("gmax", [128, OI * NTC], F32)
    umax = sbR("umax", [128, OI * NTC], F32)
    g2max = sbR("g2max", [128, OI * NTC], F32)
    omax = sbR("omax", [128, OI * NTC], F32)
    red1 = sbR("red1", [128, 1], F32)
    redT = sbR("redT", [1, 128], F32)
    red0 = sbR("red0", [1, 1], F32)
    zz16 = sbR("zz16", [1, 16], F32)

    nc.const_aps.register(F32, MAGIC, cst[:, 0:1])
    nc.const_aps.register(F32, -MAGIC, cst[:, 1:2])

    pMM = [psumR(f"pMM{i}", [128, TCH], F32) for i in range(4)]
    pTR = [psumR(f"pTR{i}", [128, TCH], F32) for i in range(2)]
    pSR = pTR[0][0:1, 0:TCH]
    pSEL = pTR[1][0:1, 0:TCH]
    pFH = [pMM[0][0:4, 0:TCH], pMM[1][0:4, 0:TCH]]

    EQ = {k: [] for k in ("sync", "tensor", "vector", "scalar", "gpsimd")}

    def em(eng, fn):
        EQ[eng].append(fn)

    def p0(sync):
        sync.dma_start(out=ident[:, :], in_=ident_in[:, :]).then_inc(sLD.h, 16)
        sync.dma_start(out=attr_c[:, 0:1],
                       in_=attr.rearrange("a f -> f a")).then_inc(sLD.h, 16)
        sync.dma_start(out=bsel_c[:, :], in_=bsel[:, :]).then_inc(sLD.h, 16)
    em("sync", p0)
    sLD.inc(16 * 3)

    def p0v(vector):
        vector.memset(cst[:, 0:1], MAGIC).then_inc(sVE.h, 1)
        vector.memset(cst[:, 1:2], -MAGIC).then_inc(sVE.h, 1)
        vector.memset(cst[:, 2:3], 65536.0).then_inc(sVE.h, 1)
        vector.memset(cst[:, 3:4], LN_EPS).then_inc(sVE.h, 1)
        vector.memset(cst[:, 4:5], 7.0).then_inc(sVE.h, 1)
        vector.memset(cst[:, 5:6], 32767.0).then_inc(sVE.h, 1)
        vector.memset(cst[:, 6:7], INV1024).then_inc(sVE.h, 1)
        vector.wait_ge(sVE.h, 7)
        vector.reciprocal(r7[:, :], cst[:, 4:5])
        vector.reciprocal(r32767[:, :], cst[:, 5:6]).then_inc(sVE.h, 1)
    em("vector", p0v)
    sVE.inc(7)
    ev_consts = sVE.inc(1)

    def p0z(vector):
        vector.memset(zz16[:, :], 0.0).then_inc(sVE.h, 1)
    em("vector", p0z)
    ev_zz = sVE.inc(1)

    def p0zd(gpsimd):
        gpsimd.wait_ge(sVE.h, ev_zz)
        for i in range(4):
            gpsimd.dma_start(out=ccs_in[i][:, :], in_=zz16[:, :]).then_inc(sGD.h, 16)
    em("gpsimd", p0zd)
    ev_ccz = sGD.inc(64)

    ctxA = ExitStack()
    sbA = mk(ctxA)
    wr_stage = sbA("wr_stage", [128, D], F32)
    sr0 = sbA("sr0", [1, TH], F32)
    wrt_hi = [sbA(f"wrt_hi{k}", [128, D], BF16) for k in range(KD)]
    wrt_lo = [sbA(f"wrt_lo{k}", [128, D], BF16) for k in range(KD)]
    hst = [sbA("hst0", [128, KD * TCH], F32)]
    hh = sbA("hh", [128, KD * TCH], BF16)
    hl = sbA("hl", [128, KD * TCH], BF16)
    fr_sb = [sbA(f"fr{k}", [128, TH], F32) for k in range(KD)]
    lf_sb = [sbA(f"lf_sb{i}", [128, TCH], F32) for i in range(2)]
    ebc = sbA("ebc", [128, TH], F32)

    pt_free = [None, None]
    wr_v_done = {}
    for i in range(KD):
        def ldf(sync, i=i):
            if i >= 1:
                sync.wait_ge(sVE.h, wr_v_done[i - 1])
            sync.dma_start(out=wr_stage[:, :],
                           in_=wr_raw[i * 128:(i + 1) * 128, :]).then_inc(sLD.h, 16)
        em("sync", ldf)
        ev_ld = sLD.inc(16)
        for j in range(KD):
            ps = pTR[j % 2]
            pf = pt_free[j % 2]

            def tfn(tensor, i=i, j=j, ps=ps, ev_ld=ev_ld, pf=pf):
                tensor.wait_ge(sLD.h, ev_ld)
                if i == 0 and j == 0:
                    tensor.wait_ge(sVE.h, ev_consts)
                if pf is not None:
                    tensor.wait_ge(sVE.h, pf)
                tensor.transpose(ps[0:128, 0:128],
                                 wr_stage[:, j * 128:(j + 1) * 128],
                                 ident[:, :]).then_inc(sMM.h, 1)
            em("tensor", tfn)
            ev_mm = sMM.inc(1)

            def vfn(vector, i=i, j=j, ps=ps, ev_mm=ev_mm):
                vector.wait_ge(sMM.h, ev_mm)
                vector.tensor_copy(wrt_hi[j][:, i * 128:(i + 1) * 128],
                                   ps[0:128, 0:128])
                vector.tensor_tensor(wrt_lo[j][:, i * 128:(i + 1) * 128],
                                     ps[0:128, 0:128],
                                     wrt_hi[j][:, i * 128:(i + 1) * 128],
                                     op=ALU.subtract).then_inc(sVE.h, 1)
            em("vector", vfn)
            pt_free[j % 2] = sVE.inc(1)
        wr_v_done[i] = sVE.n
    ev_wrt_v = sVE.n

    wsh_stage = sbA("wsh_stage", [128, D], F32)
    sblk = sbA("sblk", [128, 32], F32)
    sinv = sbA("sinv", [128, 32], F32)
    wdq = sbA("wdq", [128, D], F32)
    pk_hi = sbA("pk_hi", [128, D], BF16)
    pk_lo = sbA("pk_lo", [128, D], BF16)

    sh_ptfree = [None, None]
    ev_pk_prev = [0]

    def shard_tile(raw_ap, off_hi, off_lo):
        def ldf(sync, prev=ev_pk_prev[0]):
            sync.wait_ge(sGD.h, prev)
            sync.dma_start(out=wsh_stage[:, :], in_=raw_ap).then_inc(sLD.h, 16)
        em("sync", ldf)
        ev_ld = sLD.inc(16)

        e1 = sVE.n + 1
        e2 = sVE.n + 2
        e3 = sVE.n + 3

        def vfn(vector, ev_ld=ev_ld, e1=e1, e2=e2, e3=e3):
            vector.wait_ge(sLD.h, ev_ld)
            vector.wait_ge(sVE.h, ev_consts)
            vector.tensor_reduce(sblk[:, :],
                                 wsh_stage[:, :].rearrange("p (b c) -> p b c", c=32),
                                 axis=AX.X, op=ALU.max,
                                 apply_absolute_value=True).then_inc(sVE.h, 1)
            vector.wait_ge(sVE.h, e1)
            vector.tensor_scalar(sblk[:, :], sblk[:, :], r7[:, 0:1], 1e-12,
                                 op0=ALU.mult, op1=ALU.max).then_inc(sVE.h, 1)
            vector.wait_ge(sVE.h, e2)
            vector.reciprocal(sinv[:, :], sblk[:, :]).then_inc(sVE.h, 1)
            vector.wait_ge(sVE.h, e3)
            vector.tensor_tensor(wdq[:, :].rearrange("p (b c) -> p b c", c=32),
                                 wsh_stage[:, :].rearrange("p (b c) -> p b c", c=32),
                                 sinv[:, :].unsqueeze(2).broadcast_to((128, 32, 32)), op=ALU.mult)
            vector.tensor_scalar(wdq[:, :], wdq[:, :], MAGIC, MAGIC,
                                 op0=ALU.add, op1=ALU.subtract)
            vector.tensor_tensor(wdq[:, :].rearrange("p (b c) -> p b c", c=32),
                                 wdq[:, :].rearrange("p (b c) -> p b c", c=32),
                                 sblk[:, :].unsqueeze(2).broadcast_to((128, 32, 32)), op=ALU.mult).then_inc(sVE.h, 1)
        em("vector", vfn)
        sVE.inc(3)
        ev_dq = sVE.inc(1)

        for j in range(KD):
            ps = pTR[j % 2]
            pf = sh_ptfree[j % 2]

            def tfn(tensor, j=j, ps=ps, ev_dq=ev_dq, pf=pf):
                tensor.wait_ge(sVE.h, ev_dq)
                if pf is not None:
                    tensor.wait_ge(sVE.h, pf)
                tensor.transpose(ps[0:128, 0:128], wdq[:, j * 128:(j + 1) * 128],
                                 ident[:, :]).then_inc(sMM.h, 1)
            em("tensor", tfn)
            ev_mm = sMM.inc(1)

            def vfn2(vector, j=j, ps=ps, ev_mm=ev_mm):
                vector.wait_ge(sMM.h, ev_mm)
                vector.tensor_copy(pk_hi[:, j * 128:(j + 1) * 128], ps[0:128, 0:128])
                vector.tensor_tensor(pk_lo[:, j * 128:(j + 1) * 128],
                                     ps[0:128, 0:128],
                                     pk_hi[:, j * 128:(j + 1) * 128],
                                     op=ALU.subtract).then_inc(sVE.h, 1)
            em("vector", vfn2)
            sh_ptfree[j % 2] = sVE.inc(1)
        ev_tiles = sVE.n

        def gdf(gpsimd, off_hi=off_hi, off_lo=off_lo, ev_tiles=ev_tiles):
            gpsimd.wait_ge(sVE.h, ev_tiles)
            gpsimd.dma_start(
                out=cc0_in.ap()[off_hi:off_hi + GU_TILE].rearrange("(p f) -> p f", p=128),
                in_=pk_hi[:, :]).then_inc(sGD.h, 16)
            gpsimd.dma_start(
                out=cc0_in.ap()[off_lo:off_lo + GU_TILE].rearrange("(p f) -> p f", p=128),
                in_=pk_lo[:, :]).then_inc(sGD.h, 16)
        em("gpsimd", gdf)
        ev_pk_prev[0] = sGD.inc(32)

    for mat, raw in ((0, wg_sh), (1, wu_sh)):
        for j in range(3):
            shard_tile(raw[j * 128:(j + 1) * 128, :],
                       _off_gu(mat, j, 0), _off_gu(mat, j, 1))
    for jj in range(3):
        shard_tile(wd_sh[:, jj * 1024:(jj + 1) * 1024],
                   _off_dn(0) + jj * 8 * 128 * 128,
                   _off_dn(1) + jj * 8 * 128 * 128)
    ev_pk_all = ev_pk_prev[0]

    def cc0(gpsimd):
        gpsimd.wait_ge(sGD.h, ev_pk_all)
        gpsimd.collective_compute(
            "AllGather", ALU.bypass, replica_groups=[list(range(N_CORES))],
            ins=[cc0_in.ap()], outs=[cc0_out.ap()]).then_inc(sCC.h, 1)
    em("gpsimd", cc0)
    ev_cc0 = sCC.inc(1)

    h_cast_done = {}
    fr_mm_done = {}
    fr_ps_free = [None, None, None, None]
    sr_ps_free = [None]
    lf_mm_prev = [None, None]
    hstat_prev = [None]
    num_prev = [None]
    ev_half_stats = {}

    for half in range(2):
        for tci in range(NTCH):
            tc = half * NTCH + tci
            buf = hst[0]

            def ldh(sync, tc=tc, buf=buf):
                if tc >= 1:
                    sync.wait_ge(sVE.h, h_cast_done[tc - 1])
                for k in range(KD):
                    sync.dma_start(
                        out=buf[:, k * TCH:(k + 1) * TCH],
                        in_=h_fm[k * 128:(k + 1) * 128, tc * TCH:(tc + 1) * TCH]
                    ).then_inc(sLD.h, 16)
            em("sync", ldh)
            ev_hld = sLD.inc(16 * KD)

            def castf(vector, tc=tc, tci=tci, buf=buf, ev_hld=ev_hld):
                vector.wait_ge(sLD.h, ev_hld)
                if tc >= 1:
                    vector.wait_ge(sMM.h, fr_mm_done[tc - 1])
                vector.tensor_copy(hh[:, :], buf[:, :])
                vector.tensor_tensor(hl[:, :], buf[:, :], hh[:, :],
                                     op=ALU.subtract).then_inc(sVE.h, 1)
            em("vector", castf)
            h_cast_done[tc] = sVE.inc(1)

            for oc in range(KD):
                ps = pMM[oc % 4]
                pf = fr_ps_free[oc % 4]

                def mmf(tensor, tc=tc, tci=tci, oc=oc, ps=ps, pf=pf):
                    if tc == 0 and oc == 0:
                        tensor.wait_ge(sVE.h, ev_wrt_v)
                    tensor.wait_ge(sVE.h, h_cast_done[tc])
                    if pf is not None:
                        tensor.wait_ge(sVE.h, pf)
                    mm = None
                    for k in range(KD):
                        tensor.matmul(ps[:, :], wrt_hi[k][:, oc * 128:(oc + 1) * 128],
                                      hh[:, k * TCH:(k + 1) * TCH],
                                      start=(k == 0), stop=False)
                        tensor.matmul(ps[:, :], wrt_hi[k][:, oc * 128:(oc + 1) * 128],
                                      hl[:, k * TCH:(k + 1) * TCH],
                                      start=False, stop=False)
                        mm = tensor.matmul(ps[:, :], wrt_lo[k][:, oc * 128:(oc + 1) * 128],
                                           hh[:, k * TCH:(k + 1) * TCH],
                                           start=False, stop=(k == KD - 1))
                    mm.then_inc(sMM.h, 1)
                em("tensor", mmf)
                ev_mm = sMM.inc(1)
                if oc == KD - 1:
                    fr_mm_done[tc] = ev_mm

                def evf(vector, tci=tci, oc=oc, ps=ps, ev_mm=ev_mm):
                    vector.wait_ge(sMM.h, ev_mm)
                    vector.tensor_copy(fr_sb[oc][:, tci * TCH:(tci + 1) * TCH],
                                       ps[:, :]).then_inc(sVE.h, 1)
                em("vector", evf)
                fr_ps_free[oc % 4] = sVE.inc(1)
        ev_fr_half_v = sVE.n

        for hc in range(H):
            for tci in range(NTCH):
                gi = hc * NTCH + tci
                lb = lf_sb[gi % 2]

                def lff(vector, hc=hc, tci=tci, lb=lb,
                        ev_fr_half_v=ev_fr_half_v, pmm=lf_mm_prev[gi % 2]):
                    vector.wait_ge(sVE.h, ev_fr_half_v)
                    if pmm is not None:
                        vector.wait_ge(sMM.h, pmm)
                    vector.scalar_tensor_tensor(
                        lb[:, :], fr_sb[hc][:, tci * TCH:(tci + 1) * TCH], 0.01,
                        fr_sb[hc][:, tci * TCH:(tci + 1) * TCH],
                        op0=ALU.mult, op1=ALU.max).then_inc(sVE.h, 1)
                em("vector", lff)
                ev_lf = sVE.inc(1)

                def srm(tensor, ev_lf=ev_lf, lb=lb, pf=sr_ps_free[0]):
                    tensor.wait_ge(sVE.h, ev_lf)
                    if pf is not None:
                        tensor.wait_ge(sSC.h, pf)
                    tensor.matmul(pSR, attr_c[:, 0:1], lb[:, :],
                                  start=True, stop=True).then_inc(sMM.h, 1)
                em("tensor", srm)
                ev_srm = sMM.inc(1)
                lf_mm_prev[gi % 2] = ev_srm

                def sre(scalar, tci=tci, ev_srm=ev_srm, prev_head=hstat_prev[0]):
                    scalar.wait_ge(sMM.h, ev_srm)
                    if tci == 0 and prev_head is not None:
                        scalar.wait_ge(sGD.h, prev_head)
                    scalar.copy(sr0[0:1, tci * TCH:(tci + 1) * TCH],
                                pSR).then_inc(sSC.h, 1)
                em("scalar", sre)
                sr_ps_free[0] = sSC.inc(1)
            ev_sr_head = sSC.n
            slot = half * 8 + hc

            ea = sVE.n + 1

            def hstat_v(vector, slot=slot, ev_sr_head=ev_sr_head, ea=ea):
                vector.wait_ge(sSC.h, ev_sr_head)
                vector.tensor_reduce(m_all[0:1, slot:slot + 1], sr0[:, :],
                                     axis=AX.X, op=ALU.max).then_inc(sVE.h, 1)
                vector.wait_ge(sVE.h, ea)
                vector.tensor_scalar(sr_negm[:, :], m_all[0:1, slot:slot + 1],
                                     -1.0, None, op0=ALU.mult).then_inc(sVE.h, 1)
            em("vector", hstat_v)
            sVE.inc(1)
            ev_hm = sVE.inc(1)

            def hstat_s(scalar, ev_hm=ev_hm):
                scalar.wait_ge(sVE.h, ev_hm)
                scalar.activation(sr0[:, :], sr0[:, :], ACT.Exp,
                                  bias=sr_negm[0:1, 0:1], scale=1.0).then_inc(sSC.h, 1)
            em("scalar", hstat_s)
            ev_he = sSC.inc(1)

            def hstat_v2(vector, slot=slot, ev_he=ev_he):
                vector.wait_ge(sSC.h, ev_he)
                vector.tensor_reduce(d_all[0:1, slot:slot + 1], sr0[:, :],
                                     axis=AX.X, op=ALU.add).then_inc(sVE.h, 1)
            em("vector", hstat_v2)
            sVE.inc(1)

            def e_out(gpsimd, hc=hc, half=half, ev_he=ev_he):
                gpsimd.wait_ge(sSC.h, ev_he)
                gpsimd.dma_start(out=e_dram[hc:hc + 1, half * TH:(half + 1) * TH],
                                 in_=sr0[:, :]).then_inc(sGD.h, 16)
            em("gpsimd", e_out)
            hstat_prev[0] = sGD.inc(16)
            ev_eh_dram = hstat_prev[0]

            def ebl(gpsimd, hc=hc, half=half, ev_eh_dram=ev_eh_dram,
                    prev=num_prev[0]):
                gpsimd.wait_ge(sGD.h, ev_eh_dram)
                if prev is not None:
                    gpsimd.wait_ge(sVE.h, prev)
                gpsimd.dma_start(
                    out=ebc[:, :],
                    in_=e_dram[hc:hc + 1, half * TH:(half + 1) * TH]
                    .partition_broadcast(128)).then_inc(sGD.h, 16)
            em("gpsimd", ebl)
            ev_ebl = sGD.inc(16)

            def numf(vector, hc=hc, half=half, ev_ebl=ev_ebl):
                vector.wait_ge(sGD.h, ev_ebl)
                vector.scalar_tensor_tensor(
                    ebc[:, :], fr_sb[hc][:, :], 1.0, ebc[:, :],
                    op0=ALU.mult, op1=ALU.mult,
                    accum_out=numh[half][:, hc:hc + 1]).then_inc(sVE.h, 1)
            em("vector", numf)
            num_prev[0] = sVE.inc(1)
        ev_half_stats[half] = sVE.n

        def numt_t(tensor, half=half, ev=ev_half_stats[half]):
            tensor.wait_ge(sVE.h, ev)
            tensor.transpose(pTR[0][0:H, 0:128], numh[half][:, :],
                             ident[:, :]).then_inc(sMM.h, 1)
        em("tensor", numt_t)
        ev_ntm = sMM.inc(1)

        def numt_v(vector, half=half, ev_ntm=ev_ntm):
            vector.wait_ge(sMM.h, ev_ntm)
            vector.tensor_copy(numT_h[half][:, :], pTR[0][0:H, 0:128]).then_inc(sVE.h, 1)
        em("vector", numt_v)
        sVE.inc(1)
    ev_both_halves = sVE.n

    _ea = sVE.n + 1

    def merge_v(vector, _ea=_ea):
        vector.wait_ge(sVE.h, ev_both_halves)
        vector.tensor_tensor(mm8[:, :], m_all[0:1, 0:8], m_all[0:1, 8:16],
                             op=ALU.max).then_inc(sVE.h, 1)
        vector.wait_ge(sVE.h, _ea)
        vector.tensor_tensor(t16[0:1, 0:8], m_all[0:1, 0:8], mm8[:, :],
                             op=ALU.subtract)
        vector.tensor_tensor(t16[0:1, 8:16], m_all[0:1, 8:16], mm8[:, :],
                             op=ALU.subtract).then_inc(sVE.h, 1)
    em("vector", merge_v)
    sVE.inc(1)
    ev_mM = sVE.inc(1)

    def merge_s(scalar):
        scalar.wait_ge(sVE.h, ev_mM)
        scalar.activation(w01[:, :], t16[:, :], ACT.Exp, bias=0.0,
                          scale=1.0).then_inc(sSC.h, 1)
    em("scalar", merge_s)
    ev_w2 = sSC.inc(1)

    _eb = sVE.n + 1

    def merge_d(vector, _eb=_eb):
        vector.wait_ge(sSC.h, ev_w2)
        vector.tensor_tensor(t8a[:, :], w01[0:1, 0:8], d_all[0:1, 0:8],
                             op=ALU.mult)
        vector.tensor_tensor(dmrg[:, :], w01[0:1, 8:16], d_all[0:1, 8:16],
                             op=ALU.mult).then_inc(sVE.h, 1)
        vector.wait_ge(sVE.h, _eb)
        vector.tensor_tensor(dmrg[:, :], dmrg[:, :], t8a[:, :],
                             op=ALU.add).then_inc(sVE.h, 1)
    em("vector", merge_d)
    sVE.inc(1)
    ev_dm = sVE.inc(1)

    mg_tp_free = [None]
    mg_cols = []
    for src_ap, dst in ((mm8[0:1, 0:8], mcol), (dmrg[0:1, 0:8], dcol),
                        (w01[0:1, 0:8], w0c), (w01[0:1, 8:16], w1c)):
        def mgt(tensor, src_ap=src_ap, pf=mg_tp_free[0]):
            tensor.wait_ge(sVE.h, ev_dm)
            if pf is not None:
                tensor.wait_ge(sVE.h, pf)
            tensor.transpose(pTR[0][0:8, 0:1], src_ap,
                             ident[0:1, 0:1]).then_inc(sMM.h, 1)
        em("tensor", mgt)
        ev_tp = sMM.inc(1)

        def mge(vector, dst=dst, ev_tp=ev_tp):
            vector.wait_ge(sMM.h, ev_tp)
            vector.tensor_copy(dst[:, :], pTR[0][0:8, 0:1]).then_inc(sVE.h, 1)
        em("vector", mge)
        mg_tp_free[0] = sVE.inc(1)
    ev_cols = sVE.n

    _e1 = sVE.n + 1
    _e2 = sVE.n + 2

    def merge_num(vector, _e1=_e1, _e2=_e2):
        vector.wait_ge(sVE.h, ev_cols)
        vector.tensor_scalar(numT[:, :], numT_h[0][:, :], w0c[:, 0:1], None,
                             op0=ALU.mult).then_inc(sVE.h, 1)
        vector.wait_ge(sVE.h, _e1)
        vector.scalar_tensor_tensor(numT[:, :], numT_h[1][:, :], w1c[:, 0:1],
                                    numT[:, :], op0=ALU.mult,
                                    op1=ALU.add).then_inc(sVE.h, 1)
        vector.wait_ge(sVE.h, _e2)
        vector.tensor_copy(pack_sb[:, 0:1], mcol[:, :])
        vector.tensor_copy(pack_sb[:, 1:2], dcol[:, :])
        vector.tensor_copy(pack_sb[:, 2:130], numT[:, :]).then_inc(sVE.h, 1)
    em("vector", merge_num)
    sVE.inc(2)
    ev_pack = sVE.inc(1)

    def cc1_pack(gpsimd):
        gpsimd.wait_ge(sVE.h, ev_pack)
        gpsimd.dma_start(out=cc1_in.ap(), in_=pack_sb[:, :]).then_inc(sGD.h, 16)
    em("gpsimd", cc1_pack)
    ev_cc1_in = sGD.inc(16)

    def cc1_run(gpsimd):
        gpsimd.wait_ge(sGD.h, ev_cc1_in)
        gpsimd.collective_compute(
            "AllGather", ALU.bypass, replica_groups=[list(range(N_CORES))],
            ins=[cc1_in.ap()], outs=[cc1_out.ap()]).then_inc(sCC.h, 1)
    em("gpsimd", cc1_run)
    ev_cc1 = sCC.inc(1)

    wf_ptfree = [None, None]
    wf_v_done = {}
    for i in range(KD):
        def ldwf(sync, i=i):
            if i >= 1:
                sync.wait_ge(sVE.h, wf_v_done[i - 1])
            else:
                sync.wait_ge(sVE.h, ev_both_halves)
            sync.dma_start(out=wr_stage[:, :],
                           in_=wf_raw[i * 128:(i + 1) * 128, :]).then_inc(sLD.h, 16)
        em("sync", ldwf)
        ev_ld = sLD.inc(16)
        for j in range(KD):
            ps = pTR[(j % 2)]
            pf = wf_ptfree[j % 2]

            def tfn(tensor, i=i, j=j, ps=ps, ev_ld=ev_ld, pf=pf):
                tensor.wait_ge(sLD.h, ev_ld)
                if pf is not None:
                    tensor.wait_ge(sVE.h, pf)
                tensor.transpose(ps[0:128, 0:128],
                                 wr_stage[:, j * 128:(j + 1) * 128],
                                 ident[:, :]).then_inc(sMM.h, 1)
            em("tensor", tfn)
            ev_mm = sMM.inc(1)

            def vfn(vector, i=i, j=j, ps=ps, ev_mm=ev_mm):
                vector.wait_ge(sMM.h, ev_mm)
                vector.tensor_copy(wrt_hi[j][:, i * 128:(i + 1) * 128],
                                   ps[0:128, 0:128])
                vector.tensor_tensor(wrt_lo[j][:, i * 128:(i + 1) * 128],
                                     ps[0:128, 0:128],
                                     wrt_hi[j][:, i * 128:(i + 1) * 128],
                                     op=ALU.subtract).then_inc(sVE.h, 1)
            em("vector", vfn)
            wf_ptfree[j % 2] = sVE.inc(1)
        wf_v_done[i] = sVE.n
    ev_wft = sVE.n

    def comb_ld(gpsimd):
        gpsimd.wait_ge(sCC.h, ev_cc1)
        v = cc1_out.ap().rearrange("(b a) h c -> b a h c", a=2)
        gpsimd.dma_start(out=gA[:, :], in_=v[:, 0:1]).then_inc(sGD.h, 16)
        gpsimd.dma_start(out=gB[:, :], in_=v[:, 1:2]).then_inc(sGD.h, 16)
    em("gpsimd", comb_ld)
    ev_gAB = sGD.inc(32)

    _ec = sVE.n + 1

    def comb_v(vector, _ec=_ec):
        vector.wait_ge(sGD.h, ev_gAB)
        vector.tensor_tensor(mx32[:, :], gA[:, 0:1], gB[:, 0:1],
                             op=ALU.max).then_inc(sVE.h, 1)
        vector.wait_ge(sVE.h, _ec)
        vector.tensor_scalar(ngm32[:, :], mx32[:, :], -1.0, None,
                             op0=ALU.mult).then_inc(sVE.h, 1)
    em("vector", comb_v)
    sVE.inc(1)
    ev_mx = sVE.inc(1)

    def comb_s(scalar):
        scalar.wait_ge(sVE.h, ev_mx)
        scalar.activation(wA32[:, :], gA[:, 0:1], ACT.Exp, bias=ngm32[:, 0:1],
                          scale=1.0)
        scalar.activation(wB32[:, :], gB[:, 0:1], ACT.Exp, bias=ngm32[:, 0:1],
                          scale=1.0).then_inc(sSC.h, 1)
    em("scalar", comb_s)
    ev_w32 = sSC.inc(1)

    _g = [sVE.n + k for k in range(1, 5)]

    def comb_v2(vector, _g=_g):
        vector.wait_ge(sSC.h, ev_w32)
        vector.tensor_scalar(t129[:, :], gA[:, 1:130], wA32[:, 0:1], None,
                             op0=ALU.mult).then_inc(sVE.h, 1)
        vector.wait_ge(sVE.h, _g[0])
        vector.scalar_tensor_tensor(cn129[:, :], gB[:, 1:130], wB32[:, 0:1],
                                    t129[:, :], op0=ALU.mult,
                                    op1=ALU.add).then_inc(sVE.h, 1)
        vector.wait_ge(sVE.h, _g[1])
        vector.tensor_copy(dcb[:, :], cn129[:, 0:1]).then_inc(sVE.h, 1)
        vector.wait_ge(sVE.h, _g[2])
        vector.reciprocal(rdcb[:, :], dcb[:, :]).then_inc(sVE.h, 1)
        vector.wait_ge(sVE.h, _g[3])
        vector.tensor_scalar(ctx32[:, :], cn129[:, 1:129], rdcb[:, 0:1], None,
                             op0=ALU.mult).then_inc(sVE.h, 1)
    em("vector", comb_v2)
    sVE.inc(4)
    ev_ctx = sVE.inc(1)

    def ctxt_t(tensor):
        tensor.wait_ge(sVE.h, ev_ctx)
        tensor.transpose(pTR[0][0:128, 0:32], ctx32[:, :], ident[0:32, 0:32]).then_inc(sMM.h, 1)
    em("tensor", ctxt_t)
    ev_ctxT_mm = sMM.inc(1)

    _h = [sVE.n + 1, sVE.n + 2]

    def ctxt_v(vector, _h=_h):
        vector.wait_ge(sMM.h, ev_ctxT_mm)
        vector.tensor_copy(ctxT[:, :], pTR[0][0:128, 0:32]).then_inc(sVE.h, 1)
        vector.wait_ge(sVE.h, _h[0])
        vector.tensor_copy(ctxT_h[:, :], ctxT[:, :]).then_inc(sVE.h, 1)
        vector.wait_ge(sVE.h, _h[1])
        vector.tensor_tensor(ctxT_l[:, :], ctxT[:, :], ctxT_h[:, :],
                             op=ALU.subtract).then_inc(sVE.h, 1)
    em("vector", ctxt_v)
    sVE.inc(2)
    ev_ctxT = sVE.inc(1)

    fh_mm_evts = []
    for half in range(2):
        def fhm(tensor, half=half):
            tensor.wait_ge(sVE.h, max(ev_ctxT, ev_wft))
            mm = None
            for kc in range(KD):
                tensor.matmul(pFH[half], ctxT_h[:, kc:kc + 25:8],
                              wrt_hi[kc][:, half * TCH:(half + 1) * TCH],
                              start=(kc == 0), stop=False)
                tensor.matmul(pFH[half], ctxT_h[:, kc:kc + 25:8],
                              wrt_lo[kc][:, half * TCH:(half + 1) * TCH],
                              start=False, stop=False)
                mm = tensor.matmul(pFH[half], ctxT_l[:, kc:kc + 25:8],
                                   wrt_hi[kc][:, half * TCH:(half + 1) * TCH],
                                   start=False, stop=(kc == KD - 1))
            mm.then_inc(sMM.h, 1)
        em("tensor", fhm)
        ev_fhm = sMM.inc(1)
        fh_mm_evts.append(ev_fhm)

        def fhe(vector, half=half, ev_fhm=ev_fhm):
            vector.wait_ge(sMM.h, ev_fhm)
            vector.tensor_copy(fh4[:, half * TCH:(half + 1) * TCH],
                               pFH[half]).then_inc(sVE.h, 1)
        em("vector", fhe)
        sVE.inc(1)
    ev_fh4 = sVE.n

    sel_prev = [None]
    for half in range(2):
        def self_m(tensor, half=half, pf=sel_prev[0]):
            tensor.wait_ge(sVE.h, ev_fh4)
            if pf is not None:
                tensor.wait_ge(sSC.h, pf)
            tensor.matmul(pSEL, bsel_c[:, 0:1],
                          fh4[:, half * TCH:(half + 1) * TCH],
                          start=True, stop=True).then_inc(sMM.h, 1)
        em("tensor", self_m)
        ev_selm = sMM.inc(1)

        def sel_e(scalar, half=half, ev_selm=ev_selm):
            scalar.wait_ge(sMM.h, ev_selm)
            scalar.copy(fh1[0:1, half * TCH:(half + 1) * TCH], pSEL).then_inc(sSC.h, 1)
        em("scalar", sel_e)
        sel_prev[0] = sSC.inc(1)
    ev_fh1 = sSC.n
    ev_fh_mm_last = sMM.n

    ctxA.close()

    ctxM = ExitStack()
    sbM = mk(ctxM)
    x1c = [sbM(f"x1c{i}", [128, D], F32) for i in range(2)]
    fhb = sbM("fhb", [128, D], F32)

    def fh_out(gpsimd):
        gpsimd.wait_ge(sSC.h, ev_fh1)
        gpsimd.dma_start(out=fh_dram[:, :], in_=fh1[:, :]).then_inc(sGD.h, 16)
    em("gpsimd", fh_out)
    ev_fh_dram = sGD.inc(16)

    def fh_bc(gpsimd):
        gpsimd.wait_ge(sGD.h, ev_fh_dram)
        gpsimd.wait_ge(sMM.h, ev_fh_mm_last)
        gpsimd.dma_start(out=fhb[:, :],
                         in_=fh_dram[0:1, :].partition_broadcast(128)).then_inc(sGD.h, 16)
    em("gpsimd", fh_bc)
    ev_fhb = sGD.inc(16)

    ctxP2 = ExitStack()
    sbP = mk(ctxP2)
    htm_st = [sbP(f"htm{i}", [128, D], F32) for i in range(2)]
    p2y = sbP("p2y", [128, D], F32)
    p2t = sbP("p2t", [128, D], F32)
    p2a = sbP("p2a", [128, D], F32)
    gam_b = sbP("gam_b", [128, D], F32)
    bet_b = sbP("bet_b", [128, D], F32)

    def p2_gb_ld(sync):
        sync.wait_ge(sMM.h, ev_fh_mm_last)
        sync.wait_ge(sSC.h, ev_fh1)
        sync.dma_start(out=gam_b[:, :],
                       in_=gamma[0:1, :].partition_broadcast(128)).then_inc(sLD.h, 16)
        sync.dma_start(out=bet_b[:, :],
                       in_=beta[0:1, :].partition_broadcast(128)).then_inc(sLD.h, 16)
    em("sync", p2_gb_ld)
    ev_gb_ld = sLD.inc(32)

    p2_done = {}
    x1_spill_done = {}
    for tcm in range(NTM):
        buf = htm_st[tcm % 2]
        xb = x1c[tcm % 2]

        def ldh2(sync, tcm=tcm, buf=buf):
            if tcm < 2:
                sync.wait_ge(sMM.h, ev_fh_mm_last)
                sync.wait_ge(sSC.h, ev_fh1)
            else:
                sync.wait_ge(sVE.h, p2_done[tcm - 2])
            sync.dma_start(out=buf[:, :],
                           in_=h_tm[tcm * TMC:(tcm + 1) * TMC, :]).then_inc(sLD.h, 16)
        em("sync", ldh2)
        ev_ld = sLD.inc(16)

        _i1 = sVE.n + 1

        def p2v(vector, tcm=tcm, buf=buf, ev_ld=ev_ld, _i1=_i1):
            vector.wait_ge(sLD.h, ev_ld)
            if tcm == 0:
                vector.wait_ge(sGD.h, ev_fhb)
                vector.wait_ge(sLD.h, ev_gb_ld)
            if tcm >= 2:
                vector.wait_ge(sST.h, x1_spill_done[tcm - 2])
            vector.tensor_tensor(p2y[:, :], buf[:, :], fhb[:, :], op=ALU.add)
            vector.tensor_reduce(lnsum[:, :], p2y[:, :], axis=AX.X,
                                 op=ALU.add).then_inc(sVE.h, 1)
            vector.wait_ge(sVE.h, _i1)
            vector.tensor_scalar(negmu[:, :], lnsum[:, :], -INV1024, None,
                                 op0=ALU.mult).then_inc(sVE.h, 1)
            vector.wait_ge(sVE.h, _i1 + 1)
            vector.tensor_scalar(p2t[:, :], p2y[:, :], negmu[:, 0:1], None,
                                 op0=ALU.add).then_inc(sVE.h, 1)
        em("vector", p2v)
        sVE.inc(2)
        ev_t1 = sVE.inc(1)

        def p2s(scalar, ev_t1=ev_t1):
            scalar.wait_ge(sVE.h, ev_t1)
            scalar.activation(sqscr[:, :], p2t[:, :], ACT.Square,
                              accum_out=ssq[:, :])
            scalar.activation(lnsd[:, :], ssq[:, :], ACT.Sqrt,
                              bias=cst[:, 3:4], scale=cst[:, 6:7]).then_inc(sSC.h, 1)
        em("scalar", p2s)
        ev_sd1 = sSC.inc(1)

        _j1 = sVE.n + 1

        def p2v2(vector, buf=buf, ev_sd1=ev_sd1, _j1=_j1):
            vector.wait_ge(sSC.h, ev_sd1)
            vector.reciprocal(lnr[:, :], lnsd[:, :]).then_inc(sVE.h, 1)
            vector.wait_ge(sVE.h, _j1)
            vector.scalar_tensor_tensor(p2a[:, :], p2t[:, :], lnr[:, 0:1],
                                        gam_b[:, :], op0=ALU.mult, op1=ALU.mult)
            vector.tensor_tensor(p2a[:, :], p2a[:, :], bet_b[:, :], op=ALU.add)
            vector.tensor_tensor(p2y[:, :], buf[:, :], p2a[:, :], op=ALU.add)
            vector.tensor_reduce(lnsum[:, :], p2y[:, :], axis=AX.X,
                                 op=ALU.add).then_inc(sVE.h, 1)
            vector.wait_ge(sVE.h, _j1 + 1)
            vector.tensor_scalar(negmu[:, :], lnsum[:, :], -INV1024, None,
                                 op0=ALU.mult).then_inc(sVE.h, 1)
            vector.wait_ge(sVE.h, _j1 + 2)
            vector.tensor_scalar(p2t[:, :], p2y[:, :], negmu[:, 0:1], None,
                                 op0=ALU.add).then_inc(sVE.h, 1)
        em("vector", p2v2)
        sVE.inc(3)
        ev_t2 = sVE.inc(1)

        def p2s2(scalar, ev_t2=ev_t2):
            scalar.wait_ge(sVE.h, ev_t2)
            scalar.activation(sqscr[:, :], p2t[:, :], ACT.Square,
                              accum_out=ssq[:, :])
            scalar.activation(lnsd[:, :], ssq[:, :], ACT.Sqrt,
                              bias=cst[:, 3:4], scale=cst[:, 6:7]).then_inc(sSC.h, 1)
        em("scalar", p2s2)
        ev_sd2 = sSC.inc(1)

        _k1 = sVE.n + 1

        def p2v3(vector, tcm=tcm, xb=xb, ev_sd2=ev_sd2, _k1=_k1):
            vector.wait_ge(sSC.h, ev_sd2)
            vector.reciprocal(lnr[:, :], lnsd[:, :]).then_inc(sVE.h, 1)
            vector.wait_ge(sVE.h, _k1)
            vector.tensor_scalar(xb[:, :], p2t[:, :], lnr[:, 0:1], None,
                                 op0=ALU.mult)
            vector.tensor_reduce(xm32[:, tcm:tcm + 1], xb[:, :], axis=AX.X,
                                 op=ALU.max, apply_absolute_value=True).then_inc(sVE.h, 1)
        em("vector", p2v3)
        sVE.inc(1)
        p2_done[tcm] = sVE.inc(1)

        def x1sp(scalar, tcm=tcm, xb=xb):
            scalar.wait_ge(sVE.h, p2_done[tcm])
            scalar.dma_start(out=x1_dram.ap()[tcm], in_=xb[:, :]).then_inc(sST.h, 16)
        em("scalar", x1sp)
        x1_spill_done[tcm] = sST.inc(16)
    ev_p2_all = sVE.n
    ev_x1_sp_all = sST.n

    ctxP2.close()

    last_gf = [0]
    last_red_user = [None]

    def lmax_emit(max_cols_ap, cc_idx, slot, ready_sem, ev_ready):
        def vf(vector, prev_gf=last_gf[0]):
            vector.wait_ge(ready_sem.h, ev_ready)
            if prev_gf:
                vector.wait_ge(sGD.h, prev_gf)
            vector.tensor_reduce(red1[:, :], max_cols_ap, axis=AX.X,
                                 op=ALU.max).then_inc(sVE.h, 1)
        em("vector", vf)
        ev_r1 = sVE.inc(1)

        def tf(tensor):
            tensor.wait_ge(sVE.h, ev_r1)
            tensor.transpose(pTR[0][0:1, 0:128], red1[:, 0:1],
                             ident[:, :]).then_inc(sMM.h, 1)
        em("tensor", tf)
        ev_tm = sMM.inc(1)

        _o1 = sVE.n + 1

        def vf2(vector, ev_tm=ev_tm, _o1=_o1):
            vector.wait_ge(sMM.h, ev_tm)
            vector.tensor_copy(redT[:, :], pTR[0][0:1, 0:128]).then_inc(sVE.h, 1)
            vector.wait_ge(sVE.h, _o1)
            vector.tensor_reduce(red0[:, :], redT[:, :], axis=AX.X,
                                 op=ALU.max).then_inc(sVE.h, 1)
        em("vector", vf2)
        sVE.inc(1)
        ev_r0 = sVE.inc(1)

        def gf(gpsimd, cc_idx=cc_idx, slot=slot, ev_r0=ev_r0):
            gpsimd.wait_ge(sVE.h, ev_r0)
            gpsimd.wait_ge(sGD.h, ev_ccz)
            gpsimd.dma_start(out=ccs_in[cc_idx][0:1, slot:slot + 1],
                             in_=red0[:, :]).then_inc(sGD.h, 16)
        em("gpsimd", gf)
        last_gf[0] = sGD.inc(16)
        return last_gf[0]

    def cc_run(cc_idx, ev_in_ready):
        def cf(gpsimd, cc_idx=cc_idx, ev_in_ready=ev_in_ready):
            gpsimd.wait_ge(sGD.h, ev_in_ready)
            gpsimd.collective_compute(
                "AllGather", ALU.bypass, replica_groups=[list(range(N_CORES))],
                ins=[ccs_in[cc_idx].ap()], outs=[ccs_out[cc_idx].ap()]).then_inc(sCC.h, 1)
        em("gpsimd", cf)
        return sCC.inc(1)

    def cc_bcast(cc_idx, ev_cc):
        def bf(gpsimd, cc_idx=cc_idx, ev_cc=ev_cc):
            gpsimd.wait_ge(sCC.h, ev_cc)
            gpsimd.dma_start(
                out=s_bc[cc_idx][:, :],
                in_=ccs_out[cc_idx].ap().rearrange("r c -> (r c)")
                .rearrange("(a f) -> a f", a=1).partition_broadcast(128)
            ).then_inc(sGD.h, 16)
        em("gpsimd", bf)
        return sGD.inc(16)

    def scale_from_slot(cc_idx, slot, s_tile, rs_tile, ev_bc):
        _n1 = sVE.n + 1

        def vf(vector, cc_idx=cc_idx, slot=slot, ev_bc=ev_bc, _n1=_n1):
            vector.wait_ge(sGD.h, ev_bc)
            vector.wait_ge(sVE.h, ev_consts)
            vector.tensor_reduce(s_tmp[:, 0:1],
                                 s_bc[cc_idx][:, slot:slot + 113:16],
                                 axis=AX.X, op=ALU.max).then_inc(sVE.h, 1)
            vector.wait_ge(sVE.h, _n1)
            vector.tensor_scalar(s_tile[:, :], s_tmp[:, 0:1], r32767[:, 0:1],
                                 1e-12, op0=ALU.mult, op1=ALU.max).then_inc(sVE.h, 1)
            vector.wait_ge(sVE.h, _n1 + 1)
            vector.reciprocal(rs_tile[:, :], s_tile[:, :]).then_inc(sVE.h, 1)
        em("vector", vf)
        sVE.inc(2)
        return sVE.inc(1)

    ev_in2 = lmax_emit(xm32[:, :], 0, 0, sVE, ev_p2_all)
    ev_cc2 = cc_run(0, ev_in2)
    ev_bc2 = cc_bcast(0, ev_cc2)
    ev_s1 = scale_from_slot(0, 0, s_s1, s_rs1, ev_bc2)

    ctxRB = ExitStack()
    sbRB = mk(ctxRB)
    rb_in = [sbRB(f"rb_in{i}", [128, TCH], F32) for i in range(2)]
    rb_ig = sbRB("rb_ig", [128, TCH], F32)
    rb_t1 = sbRB("rb_t1", [128, TCH], F32)
    rb_t2 = sbRB("rb_t2", [128, TCH], F32)
    rb_o = [sbRB(f"rb_o{i}", [128, TCH], F32) for i in range(2)]
    rb2_up = rb_in
    rb2_g2 = [sbRB(f"rb2g{i}", [128, TCH], F32) for i in range(2)]
    rb2_i2 = sbRB("rb2_i2", [128, TCH], F32)
    rb2_i4 = sbRB("rb2_i4", [128, TCH], F32)

    ctxF1 = ExitStack()
    sbF = mk(ctxF1)
    q16 = sbF("q16", [128, D], F32)
    qh_fm = [sbF(f"qh{k}", [128, T], BF16) for k in range(KD)]
    ql_fm = [sbF(f"ql{k}", [128, T], BF16) for k in range(KD)]
    wt_hi = [sbF(f"wt_hi{i}", [128, D], BF16) for i in range(2)]
    wt_lo = [sbF(f"wt_lo{i}", [128, D], BF16) for i in range(2)]
    ev_tile = [sbF(f"ev_tile{i}", [128, TCH], F32) for i in range(4)]

    q_ptfree = [None, None]
    q16_done = {}
    for tcm in range(NTM):
        xb = x1c[tcm % 2]

        def x1rd(sync, tcm=tcm, xb=xb):
            if tcm == 0:
                sync.wait_ge(sST.h, ev_x1_sp_all)
            if tcm >= 2:
                sync.wait_ge(sVE.h, q16_done[tcm - 2])
            sync.dma_start(out=xb[:, :], in_=x1_dram.ap()[tcm]).then_inc(sLD.h, 16)
        em("sync", x1rd)
        ev_ld = sLD.inc(16)

        def q16v(vector, tcm=tcm, xb=xb, ev_ld=ev_ld):
            vector.wait_ge(sLD.h, ev_ld)
            if tcm == 0:
                vector.wait_ge(sVE.h, ev_s1)
            if tcm >= 1:
                vector.wait_ge(sMM.h, q_mm_done[tcm - 1])
            vector.tensor_scalar(q16[:, :], xb[:, :], s_rs1[:, 0:1], MAGIC,
                                 op0=ALU.mult, op1=ALU.add).then_inc(sVE.h, 1)
        if tcm == 0:
            q16s_done = {}
            q_mm_done = {}
        em("vector", q16v)
        ev_q16a = sVE.inc(1)

        def q16s(scalar, ev_q16a=ev_q16a):
            scalar.wait_ge(sVE.h, ev_q16a)
            scalar.activation(q16[:, :], q16[:, :], ACT.Copy,
                              bias=-MAGIC, scale=1.0).then_inc(sSC.h, 1)
        em("scalar", q16s)
        q16s_done[tcm] = sSC.inc(1)

        for j in range(KD):
            ps = pTR[j % 2]
            pf = q_ptfree[j % 2]

            def qtt(tensor, tcm=tcm, j=j, ps=ps, pf=pf):
                tensor.wait_ge(sSC.h, q16s_done[tcm])
                if pf is not None:
                    tensor.wait_ge(sVE.h, pf)
                tensor.transpose(ps[0:128, 0:128], q16[:, j * 128:(j + 1) * 128],
                                 ident[:, :]).then_inc(sMM.h, 1)
            em("tensor", qtt)
            ev_mm = sMM.inc(1)

            def qte(vector, tcm=tcm, j=j, ps=ps, ev_mm=ev_mm):
                vector.wait_ge(sMM.h, ev_mm)
                vector.tensor_copy(qh_fm[j][:, tcm * TMC:(tcm + 1) * TMC],
                                   ps[0:128, 0:128])
                vector.tensor_tensor(ql_fm[j][:, tcm * TMC:(tcm + 1) * TMC],
                                     ps[0:128, 0:128],
                                     qh_fm[j][:, tcm * TMC:(tcm + 1) * TMC],
                                     op=ALU.subtract).then_inc(sVE.h, 1)
            em("vector", qte)
            q_ptfree[j % 2] = sVE.inc(1)
        q_mm_done[tcm] = sMM.n
        q16_done[tcm] = sVE.n
    ev_q_all = sVE.n

    gu_wfree = [None, None]
    gu_psfree = [None, None, None, None]
    gu_ebfree = [None, None, None, None]

    def emit_gu_phase(mat, dst_dram, max_cols, tail_cb=None):
        for oc in range(OI):
            wb = oc % 2
            rank, j = oc % 8, oc // 8
            wfree = gu_wfree[wb]

            def wld(sync, mat=mat, wb=wb, rank=rank, j=j, wfree=wfree):
                if mat == 0 and j == 0 and rank < 2:
                    sync.wait_ge(sCC.h, ev_cc0)
                if wfree is not None:
                    sync.wait_ge(sMM.h, wfree)
                off_hi = _off_gu(mat, j, 0)
                off_lo = _off_gu(mat, j, 1)
                sync.dma_start(
                    out=wt_hi[wb][:, :],
                    in_=cc0_out.ap()[rank, off_hi:off_hi + GU_TILE]
                    .rearrange("(p f) -> p f", p=128)).then_inc(sLD.h, 16)
                sync.dma_start(
                    out=wt_lo[wb][:, :],
                    in_=cc0_out.ap()[rank, off_lo:off_lo + GU_TILE]
                    .rearrange("(p f) -> p f", p=128)).then_inc(sLD.h, 16)
            em("sync", wld)
            ev_wld = sLD.inc(32)

            for tc in range(NTC):
                pb = (oc * NTC + tc) % 4
                ps = pMM[pb]
                idx = oc * NTC + tc
                psfree = gu_psfree[pb]

                def gmm(tensor, mat=mat, oc=oc, tc=tc, wb=wb, ps=ps,
                        ev_wld=ev_wld, psfree=psfree):
                    if mat == 0 and oc == 0 and tc == 0:
                        tensor.wait_ge(sVE.h, ev_q_all)
                    tensor.wait_ge(sLD.h, ev_wld)
                    if psfree is not None:
                        tensor.wait_ge(sSC.h, psfree)
                    mm = None
                    for k in range(KD):
                        tensor.matmul(ps[:, :], wt_hi[wb][:, k * 128:(k + 1) * 128],
                                      qh_fm[k][:, tc * TCH:(tc + 1) * TCH],
                                      start=(k == 0), stop=False)
                        tensor.matmul(ps[:, :], wt_hi[wb][:, k * 128:(k + 1) * 128],
                                      ql_fm[k][:, tc * TCH:(tc + 1) * TCH],
                                      start=False, stop=False)
                        mm = tensor.matmul(ps[:, :], wt_lo[wb][:, k * 128:(k + 1) * 128],
                                           qh_fm[k][:, tc * TCH:(tc + 1) * TCH],
                                           start=False, stop=(k == KD - 1))
                    mm.then_inc(sMM.h, 1)
                em("tensor", gmm)
                ev_mm = sMM.inc(1)
                if tc == NTC - 1:
                    gu_wfree[wb] = ev_mm

                eb = ev_tile[idx % 4]
                ebfree = gu_ebfree[idx % 4]

                def gev(scalar, ps=ps, eb=eb, ev_mm=ev_mm, ebfree=ebfree):
                    scalar.wait_ge(sMM.h, ev_mm)
                    if ebfree is not None:
                        scalar.wait_ge(sST.h, ebfree[0])
                        scalar.wait_ge(sVE.h, ebfree[1])
                    scalar.activation(eb[:, :], ps[:, :], ACT.Copy,
                                      scale=s_s1[:, 0:1]).then_inc(sSC.h, 1)
                em("scalar", gev)
                ev_ev = sSC.inc(1)
                gu_psfree[pb] = ev_ev

                def gsp(scalar, oc=oc, tc=tc, eb=eb, dst=dst_dram):
                    scalar.dma_start(out=dst.ap()[oc, tc], in_=eb[:, :]).then_inc(sST.h, 16)
                em("scalar", gsp)
                ev_sp = sST.inc(16)

                def gmx(vector, idx=idx, eb=eb, mc=max_cols, ev_ev=ev_ev):
                    vector.wait_ge(sSC.h, ev_ev)
                    vector.tensor_reduce(mc[:, idx:idx + 1], eb[:, :], axis=AX.X,
                                         op=ALU.max,
                                         apply_absolute_value=True).then_inc(sVE.h, 1)
                em("vector", gmx)
                ev_mx = sVE.inc(1)
                gu_ebfree[idx % 4] = (ev_sp, ev_mx)
            if tail_cb is not None:
                tail_cb(oc)
        return sST.n, sVE.n

    ev_gate_sp, ev_gate_mx = emit_gu_phase(0, gate_dram, gmax)

    ev_in3 = lmax_emit(gmax[:, :], 1, 0, sVE, ev_gate_mx)
    ev_cc3 = cc_run(1, ev_in3)
    ev_bc3 = cc_bcast(1, ev_cc3)
    ev_s3 = scale_from_slot(1, 0, s_s3, s_rs3, ev_bc3)

    def s3m_f(vector):
        vector.wait_ge(sVE.h, ev_s3)
        vector.tensor_scalar(s_s3m[:, :], s_s3[:, :], INV65536, None,
                             op0=ALU.mult).then_inc(sVE.h, 1)
    em("vector", s3m_f)
    ev_s3m = sVE.inc(1)

    r1_done = {}
    r1_s_done = {}
    r1_sp_done = {}

    def emit_r1(idx):
        oc, tc = idx // NTC, idx % NTC
        ib = rb_in[idx % 2]
        ob = rb_o[idx % 2]

        def r1ld(sync, oc=oc, tc=tc, ib=ib, idx=idx):
            if idx == 0:
                sync.wait_ge(sST.h, ev_gate_sp)
            if idx >= 2:
                sync.wait_ge(sVE.h, r1_done[idx - 2])
            sync.dma_start(out=ib[:, :], in_=gate_dram.ap()[oc, tc]).then_inc(sLD.h, 16)
        em("sync", r1ld)
        ev_ld = sLD.inc(16)

        def r1v(vector, ib=ib, idx=idx, ev_ld=ev_ld):
            vector.wait_ge(sLD.h, ev_ld)
            if idx == 0:
                vector.wait_ge(sVE.h, ev_s3m)
            if idx >= 1:
                vector.wait_ge(sSC.h, r1_s_done[idx - 1])
            vector.tensor_scalar(rb_t1[:, :], ib[:, :], s_rs3[:, 0:1], MAGIC,
                                 op0=ALU.mult, op1=ALU.add).then_inc(sVE.h, 1)
        em("vector", r1v)
        ev_a = sVE.inc(1)

        def r1s(scalar, ev_a=ev_a, idx=idx):
            scalar.wait_ge(sVE.h, ev_a)
            if idx >= 1:
                scalar.wait_ge(sVE.h, r1_done.get(idx - 1, 0))
            scalar.activation(rb_ig[:, :], rb_t1[:, :], ACT.Copy,
                              bias=-MAGIC, scale=1.0)
            scalar.activation(rb_t2[:, :], rb_ig[:, :], ACT.Sigmoid,
                              bias=0.0, scale=s_s3[:, 0:1])
            scalar.activation(rb_t2[:, :], rb_t2[:, :], ACT.Copy,
                              bias=MAGIC, scale=cst[:, 2:3]).then_inc(sSC.h, 1)
        em("scalar", r1s)
        r1_s_done[idx] = sSC.inc(1)

        def r1v2(vector, ob=ob, idx=idx):
            vector.wait_ge(sSC.h, r1_s_done[idx])
            if idx >= 2:
                vector.wait_ge(sGD.h, r1_sp_done[idx - 2])
            vector.tensor_scalar(rb_t2[:, :], rb_t2[:, :], MAGIC, 65535.0,
                                 op0=ALU.subtract, op1=ALU.min)
            vector.scalar_tensor_tensor(ob[:, :], rb_ig[:, :], s_s3m[:, 0:1],
                                        rb_t2[:, :], op0=ALU.mult, op1=ALU.mult)
            vector.tensor_reduce(g2max[:, idx:idx + 1], ob[:, :], axis=AX.X,
                                 op=ALU.max, apply_absolute_value=True).then_inc(sVE.h, 1)
        em("vector", r1v2)
        r1_done[idx] = sVE.inc(1)

        def r1sp(gpsimd, oc=oc, tc=tc, ob=ob, idx=idx):
            gpsimd.wait_ge(sVE.h, r1_done[idx])
            gpsimd.dma_start(out=g2_dram.ap()[oc, tc], in_=ob[:, :]).then_inc(sGD.h, 16)
        em("gpsimd", r1sp)
        r1_sp_done[idx] = sGD.inc(16)

    def up_tail(oc):
        for tc in range(NTC):
            emit_r1(oc * NTC + tc)

    ev_up_sp, ev_up_mx = emit_gu_phase(1, up_dram, umax, tail_cb=up_tail)
    ev_r1_all = sVE.n
    ev_r1_sp = sGD.n

    lmax_emit(umax[:, :], 2, 0, sVE, ev_up_mx)
    ev_in4 = lmax_emit(g2max[:, :], 2, 1, sVE, ev_r1_all)
    ev_cc4 = cc_run(2, ev_in4)
    ev_bc4 = cc_bcast(2, ev_cc4)
    ev_s2 = scale_from_slot(2, 0, s_s2, s_rs2, ev_bc4)
    ev_s4 = scale_from_slot(2, 1, s_s4, s_rs4, ev_bc4)

    def s42_f(vector):
        vector.wait_ge(sVE.h, max(ev_s2, ev_s4))
        vector.tensor_tensor(s_s42[:, :], s_s4[:, :], s_s2[:, :],
                             op=ALU.mult).then_inc(sVE.h, 1)
    em("vector", s42_f)
    ev_s42 = sVE.inc(1)

    ctxF1.close()

    ctxWD = ExitStack()
    sbW = mk(ctxWD)
    wdh = [sbW(f"wdh{ic}", [128, D], BF16) for ic in range(OI)]
    wdl = [sbW(f"wdl{ic}", [128, D], BF16) for ic in range(OI)]
    oin = sbW("oin", [128, I3], F32)
    o16 = sbW("o16", [128, I3], F32)
    ohh = sbW("ohh", [128, I3], BF16)
    oll = sbW("oll", [128, I3], BF16)
    yx1 = sbW("yx1", [128, D], F32)
    y2 = sbW("y2", [128, D], F32)
    outc = [sbW(f"outc{i}", [128, D], F32) for i in range(2)]

    def wdld(sync):
        sync.wait_ge(sCC.h, ev_cc0)
        sync.wait_ge(sST.h, ev_up_sp)
        sync.wait_ge(sVE.h, ev_up_mx)
        for ic in range(OI):
            jj, j = ic // 8, ic % 8
            for r in range(8):
                for part, dst in ((0, wdh), (1, wdl)):
                    base = _off_dn(part) + jj * (128 * 1024)
                    sync.dma_start(
                        out=dst[ic][:, r * 128:(r + 1) * 128],
                        in_=cc0_out.ap()[r, base:base + 128 * 1024]
                        .rearrange("(p j o) -> p j o", j=8, o=128)[:, j:j + 1, :]
                    ).then_inc(sLD.h, 16)
    em("sync", wdld)
    ev_wd_ld = sLD.inc(16 * OI * 8 * 2)

    r2_done = {}
    r2_s_done = {}
    r2_g_done = {}
    r2_sp_done = {}
    for idx in range(OI * NTC):
        oc, tc = idx // NTC, idx % NTC
        ub = rb2_up[idx % 2]
        gb = rb2_g2[idx % 2]
        ob = rb_o[idx % 2]

        def r2ld(sync, oc=oc, tc=tc, ub=ub, gb=gb, idx=idx):
            if idx == 0:
                sync.wait_ge(sST.h, ev_up_sp)
                sync.wait_ge(sGD.h, ev_r1_sp)
            if idx >= 2:
                sync.wait_ge(sVE.h, r2_done[idx - 2])
            sync.dma_start(out=ub[:, :], in_=up_dram.ap()[oc, tc]).then_inc(sLD.h, 16)
            sync.dma_start(out=gb[:, :], in_=g2_dram.ap()[oc, tc]).then_inc(sLD.h, 16)
        em("sync", r2ld)
        ev_ld = sLD.inc(32)

        def r2v(vector, ub=ub, gb=gb, idx=idx, ev_ld=ev_ld):
            vector.wait_ge(sLD.h, ev_ld)
            if idx == 0:
                vector.wait_ge(sVE.h, ev_s42)
            if idx >= 1:
                vector.wait_ge(sSC.h, r2_s_done[idx - 1])
            vector.tensor_scalar(rb_t1[:, :], ub[:, :], s_rs2[:, 0:1], MAGIC,
                                 op0=ALU.mult, op1=ALU.add)
            vector.tensor_scalar(rb_t2[:, :], gb[:, :], s_rs4[:, 0:1], MAGIC,
                                 op0=ALU.mult, op1=ALU.add).then_inc(sVE.h, 1)
        em("vector", r2v)
        ev_a = sVE.inc(1)

        def r2s(scalar, ev_a=ev_a, idx=idx):
            scalar.wait_ge(sVE.h, ev_a)
            if idx >= 1:
                scalar.wait_ge(sGP.h, r2_g_done[idx - 1])
            scalar.activation(rb2_i2[:, :], rb_t1[:, :], ACT.Copy,
                              bias=-MAGIC, scale=1.0)
            scalar.activation(rb2_i4[:, :], rb_t2[:, :], ACT.Copy,
                              bias=-MAGIC, scale=1.0)
            scalar.activation(rb2_i4[:, :], rb2_i4[:, :], ACT.Copy,
                              bias=0.0, scale=s_s42[:, 0:1]).then_inc(sSC.h, 1)
        em("scalar", r2s)
        r2_s_done[idx] = sSC.inc(1)

        def r2g(gpsimd, ob=ob, oc=oc, tc=tc, idx=idx,
                r2_g_done_self=sGP.n + 1):
            gpsimd.wait_ge(sSC.h, r2_s_done[idx])
            if idx >= 2:
                gpsimd.wait_ge(sVE.h, r2_done[idx - 2])
                gpsimd.wait_ge(sGD.h, r2_sp_done[idx - 2])
            gpsimd.tensor_tensor(ob[:, :], rb2_i4[:, :], rb2_i2[:, :],
                                 op=ALU.mult).then_inc(sGP.h, 1)
            gpsimd.wait_ge(sGP.h, r2_g_done_self)
            gpsimd.dma_start(out=or_dram.ap()[oc, tc], in_=ob[:, :]).then_inc(sGD.h, 16)
        em("gpsimd", r2g)
        r2_g_done[idx] = sGP.inc(1)
        r2_sp_done[idx] = sGD.inc(16)

        def r2vr(vector, ob=ob, idx=idx):
            vector.wait_ge(sGP.h, r2_g_done[idx])
            vector.tensor_reduce(omax[:, idx:idx + 1], ob[:, :], axis=AX.X,
                                 op=ALU.max, apply_absolute_value=True).then_inc(sVE.h, 1)
        em("vector", r2vr)
        r2_done[idx] = sVE.inc(1)
    ev_r2_all = sVE.n
    ev_r2_sp = sGD.n

    ev_in5 = lmax_emit(omax[:, :], 3, 0, sVE, ev_r2_all)
    ev_cc5 = cc_run(3, ev_in5)
    ev_bc5 = cc_bcast(3, ev_cc5)
    ev_s5 = scale_from_slot(3, 0, s_s5, s_rs5, ev_bc5)

    d_psfree = [None, None, None, None]
    dq_done = {}
    out_sp = {}
    for tt in range(NTM):
        oc_out = outc[tt % 2]
        tc512, sub = tt // 4, tt % 4

        def d_ld(sync, tt=tt, tc512=tc512, sub=sub):
            if tt == 0:
                sync.wait_ge(sGD.h, ev_r2_sp)
            if tt >= 1:
                sync.wait_ge(sVE.h, d_y2done[tt - 1])
            for ic in range(OI):
                sync.dma_start(
                    out=oin[:, ic * 128:(ic + 1) * 128],
                    in_=or_dram.ap()[ic, tc512, :, sub * 128:(sub + 1) * 128]
                ).then_inc(sLD.h, 16)
            sync.dma_start(out=yx1[:, :], in_=x1_dram.ap()[tt]).then_inc(sLD.h, 16)
        em("sync", d_ld)
        ev_ld = sLD.inc(16 * (OI + 1))

        def d_v(vector, tt=tt, ev_ld=ev_ld):
            vector.wait_ge(sLD.h, ev_ld)
            if tt == 0:
                vector.wait_ge(sVE.h, ev_s5)
            if tt >= 1:
                vector.wait_ge(sMM.h, d_mm_done[tt - 1])
            vector.tensor_scalar(o16[:, :], oin[:, :], s_rs5[:, 0:1], MAGIC,
                                 op0=ALU.mult, op1=ALU.add).then_inc(sVE.h, 1)
        if tt == 0:
            d_mm_done = {}
            d_y2done = {}
        em("vector", d_v)
        ev_a = sVE.inc(1)

        def d_s(scalar, ev_a=ev_a):
            scalar.wait_ge(sVE.h, ev_a)
            scalar.activation(o16[:, :], o16[:, :], ACT.Copy,
                              bias=-MAGIC, scale=1.0).then_inc(sSC.h, 1)
        em("scalar", d_s)
        ev_o16 = sSC.inc(1)

        def d_v2(vector, ev_o16=ev_o16):
            vector.wait_ge(sSC.h, ev_o16)
            vector.tensor_copy(ohh[:, :], o16[:, :])
            vector.tensor_tensor(oll[:, :], o16[:, :], ohh[:, :],
                                 op=ALU.subtract).then_inc(sVE.h, 1)
        em("vector", d_v2)
        ev_ohl = sVE.inc(1)
        dq_done[tt] = ev_ohl

        for half in range(2):
            pb = (tt * 2 + half) % 4
            ps = pMM[pb]
            psfree = d_psfree[pb]

            def dmm(tensor, tt=tt, half=half, ps=ps, ev_ohl=ev_ohl,
                    psfree=psfree):
                if tt == 0 and half == 0:
                    tensor.wait_ge(sLD.h, ev_wd_ld)
                tensor.wait_ge(sVE.h, ev_ohl)
                if psfree is not None:
                    tensor.wait_ge(sVE.h, psfree)
                mm = None
                for ic in range(OI):
                    tensor.matmul(ps[:, :], ohh[:, ic * 128:(ic + 1) * 128],
                                  wdh[ic][:, half * TCH:(half + 1) * TCH],
                                  start=(ic == 0), stop=False)
                    tensor.matmul(ps[:, :], ohh[:, ic * 128:(ic + 1) * 128],
                                  wdl[ic][:, half * TCH:(half + 1) * TCH],
                                  start=False, stop=False)
                    mm = tensor.matmul(ps[:, :], oll[:, ic * 128:(ic + 1) * 128],
                                       wdh[ic][:, half * TCH:(half + 1) * TCH],
                                       start=False, stop=(ic == OI - 1))
                mm.then_inc(sMM.h, 1)
            em("tensor", dmm)
            ev_mm = sMM.inc(1)
            if half == 1:
                d_mm_done[tt] = ev_mm

            def dev(vector, half=half, ps=ps, ev_mm=ev_mm):
                vector.wait_ge(sMM.h, ev_mm)
                vector.scalar_tensor_tensor(
                    y2[:, half * TCH:(half + 1) * TCH], ps[:, :], s_s5[:, 0:1],
                    yx1[:, half * TCH:(half + 1) * TCH],
                    op0=ALU.mult, op1=ALU.add).then_inc(sVE.h, 1)
            em("vector", dev)
            d_psfree[pb] = sVE.inc(1)
        ev_y2 = sVE.n
        d_y2done[tt] = ev_y2

        _l1 = sVE.n + 1

        def f_v(vector, ev_y2=ev_y2, _l1=_l1):
            vector.wait_ge(sVE.h, ev_y2)
            vector.tensor_reduce(lnsum[:, :], y2[:, :], axis=AX.X,
                                 op=ALU.add).then_inc(sVE.h, 1)
            vector.wait_ge(sVE.h, _l1)
            vector.tensor_scalar(negmu[:, :], lnsum[:, :], -INV1024, None,
                                 op0=ALU.mult).then_inc(sVE.h, 1)
            vector.wait_ge(sVE.h, _l1 + 1)
            vector.tensor_scalar(y2[:, :], y2[:, :], negmu[:, 0:1], None,
                                 op0=ALU.add).then_inc(sVE.h, 1)
        em("vector", f_v)
        sVE.inc(2)
        ev_t3 = sVE.inc(1)

        def f_s(scalar, ev_t3=ev_t3):
            scalar.wait_ge(sVE.h, ev_t3)
            scalar.activation(sqscr[:, :], y2[:, :], ACT.Square,
                              accum_out=ssq[:, :])
            scalar.activation(lnsd[:, :], ssq[:, :], ACT.Sqrt,
                              bias=cst[:, 3:4], scale=cst[:, 6:7]).then_inc(sSC.h, 1)
        em("scalar", f_s)
        ev_sd3 = sSC.inc(1)

        _m1 = sVE.n + 1

        def f_v2(vector, tt=tt, oc_out=oc_out, ev_sd3=ev_sd3, _m1=_m1):
            vector.wait_ge(sSC.h, ev_sd3)
            if tt >= 2:
                vector.wait_ge(sST.h, out_sp[tt - 2])
            vector.reciprocal(lnr[:, :], lnsd[:, :]).then_inc(sVE.h, 1)
            vector.wait_ge(sVE.h, _m1)
            vector.tensor_scalar(oc_out[:, :], y2[:, :], lnr[:, 0:1], None,
                                 op0=ALU.mult).then_inc(sVE.h, 1)
        em("vector", f_v2)
        sVE.inc(1)
        ev_oc = sVE.inc(1)

        def f_st(scalar, tt=tt, oc_out=oc_out, ev_oc=ev_oc):
            scalar.wait_ge(sVE.h, ev_oc)
            scalar.dma_start(out=out[tt * TMC:(tt + 1) * TMC, :],
                             in_=oc_out[:, :]).then_inc(sST.h, 16)
        em("scalar", f_st)
        out_sp[tt] = sST.inc(16)
    ev_out_all = sST.n

    def fin(sync):
        sync.wait_ge(sST.h, ev_out_all)
    em("sync", fin)

    ctxWD.close()
    ctxRB.close()
    ctxM.close()

    with nc.Block() as block:
        @block.sync
        def _(sync):
            for fn in EQ["sync"]:
                fn(sync)

        @block.tensor
        def _(tensor):
            for fn in EQ["tensor"]:
                fn(tensor)

        @block.vector
        def _(vector):
            for fn in EQ["vector"]:
                fn(vector)

        @block.scalar
        def _(scalar):
            for fn in EQ["scalar"]:
                fn(scalar)

        @block.gpsimd
        def _(gpsimd):
            for fn in EQ["gpsimd"]:
                fn(gpsimd)

    ctxR.close()
    return nc


_NC_CACHE = None


def kernel(L0_attn_h, L0_attn_Wl, L0_attn_Wr, L0_attn_att_l, L0_attn_att_r,
           L0_attn_Wf, L0_attn_ln_gamma, L0_attn_ln_beta, L0_ffn_w_gate,
           L0_ffn_w_up, L0_ffn_w_down):
    global _NC_CACHE
    h = np.ascontiguousarray(np.asarray(L0_attn_h, dtype=np.float32))
    wr = np.ascontiguousarray(np.asarray(L0_attn_Wr, dtype=np.float32))
    wf = np.ascontiguousarray(np.asarray(L0_attn_Wf, dtype=np.float32))
    attr = np.ascontiguousarray(np.asarray(L0_attn_att_r, dtype=np.float32))
    gamma = np.asarray(L0_attn_ln_gamma, dtype=np.float32).reshape(1, D)
    beta = np.asarray(L0_attn_ln_beta, dtype=np.float32).reshape(1, D)
    wg = np.ascontiguousarray(np.asarray(L0_ffn_w_gate, dtype=np.float32))
    wu = np.ascontiguousarray(np.asarray(L0_ffn_w_up, dtype=np.float32))
    wd = np.ascontiguousarray(np.asarray(L0_ffn_w_down, dtype=np.float32))
    ident = np.eye(128, dtype=np.float32)

    if _NC_CACHE is None:
        _NC_CACHE = build()
    nc = _NC_CACHE

    in_maps = []
    for c in range(N_CORES):
        b, halfseq = c // 2, c % 2
        tok = slice(halfseq * T, (halfseq + 1) * T)
        h_tm = np.ascontiguousarray(h[b, tok, :])
        h_fm = np.ascontiguousarray(h_tm.T)
        wg_sh = np.ascontiguousarray(
            np.concatenate([wg[128 * (c + 8 * j):128 * (c + 8 * j) + 128]
                            for j in range(3)], axis=0))
        wu_sh = np.ascontiguousarray(
            np.concatenate([wu[128 * (c + 8 * j):128 * (c + 8 * j) + 128]
                            for j in range(3)], axis=0))
        wd_sh = np.ascontiguousarray(wd[128 * c:128 * (c + 1)])
        bsel_np = np.zeros((4, 1), dtype=np.float32)
        bsel_np[b, 0] = 1.0
        in_maps.append({
            "h_fm": h_fm, "h_tm": h_tm, "wr_raw": wr, "wf_raw": wf,
            "attr": attr, "gamma": gamma, "beta": beta,
            "wg_sh": wg_sh, "wu_sh": wu_sh, "wd_sh": wd_sh,
            "bsel": bsel_np, "ident_in": ident,
        })

    res = run_bass_kernel_spmd(nc, in_maps, core_ids=list(range(N_CORES)))
    full = np.empty((B, N, D), dtype=np.float32)
    for c in range(N_CORES):
        b, halfseq = c // 2, c % 2
        full[b, halfseq * T:(halfseq + 1) * T, :] = res.results[c]["out"]
    return full


# version 11
# speedup vs baseline: 1.0472x; 1.0472x over previous
import numpy as np
from contextlib import ExitStack

import concourse.bass as bass
import concourse.mybir as mybir
from concourse.bass_utils import run_bass_kernel_spmd

F32 = mybir.dt.float32
BF16 = mybir.dt.bfloat16
ACT = mybir.ActivationFunctionType
ALU = mybir.AluOpType
AX = mybir.AxisListType

MAGIC = 12582912.0
INV1024 = 0.0009765625
INV65536 = 0.0000152587890625
N_CORES = 8
B, N, D, H = 4, 8192, 1024, 8
I3 = 3072
T = 4096
TH = 2048
KD = 8
TCH = 512
NTC = T // TCH
NTCH = TH // TCH
TMC = 128
NTM = T // TMC
OI = 24
LN_EPS = 1e-5

GU_TILE = 128 * 1024
DN_TILE = 128 * 3072
PACK = 12 * GU_TILE + 2 * DN_TILE


def _off_gu(mat, j, part):
    return ((mat * 3 + j) * 2 + part) * GU_TILE


def _off_dn(part):
    return 12 * GU_TILE + part * DN_TILE


class Sem:
    def __init__(self, h):
        self.h = h
        self.n = 0

    def inc(self, k=1):
        self.n += k
        return self.n


def build(detect_races=True):
    nc = bass.Bass("TRN2", target_bir_lowering=False,
                   detect_race_conditions=detect_races)
    dt = nc.dram_tensor
    h_fm = dt("h_fm", [D, T], F32, kind="ExternalInput").ap()
    h_tm = dt("h_tm", [T, D], F32, kind="ExternalInput").ap()
    wr_raw = dt("wr_raw", [D, D], F32, kind="ExternalInput").ap()
    wf_raw = dt("wf_raw", [D, D], F32, kind="ExternalInput").ap()
    attr = dt("attr", [1, 128], F32, kind="ExternalInput").ap()
    gamma = dt("gamma", [1, D], F32, kind="ExternalInput").ap()
    beta = dt("beta", [1, D], F32, kind="ExternalInput").ap()
    wg_sh = dt("wg_sh", [3 * 128, D], F32, kind="ExternalInput").ap()
    wu_sh = dt("wu_sh", [3 * 128, D], F32, kind="ExternalInput").ap()
    wd_sh = dt("wd_sh", [128, I3], F32, kind="ExternalInput").ap()
    bsel = dt("bsel", [4, 1], F32, kind="ExternalInput").ap()
    ident_in = dt("ident_in", [128, 128], F32, kind="ExternalInput").ap()
    out = dt("out", [T, D], F32, kind="ExternalOutput").ap()

    cc0_in = dt("cc0_in", [PACK], BF16)
    cc0_out = dt("cc0_out", [N_CORES, PACK], BF16)
    cc1_in = dt("cc1_in", [H, 130], F32)
    cc1_out = dt("cc1_out", [N_CORES, H, 130], F32)
    ccs_in = [dt(f"ccs_in{i}", [1, 16], F32) for i in range(4)]
    ccs_out = [dt(f"ccs_out{i}", [N_CORES, 16], F32) for i in range(4)]
    e_dram = dt("e_dram", [H, T], F32)
    fh_dram = dt("fh_dram", [1, D], F32)
    x1_dram = dt("x1_dram", [NTM, TMC, D], F32)
    gate_dram = dt("gate_dram", [OI, NTC, 128, TCH], F32)
    up_dram = dt("up_dram", [OI, NTC, 128, TCH], F32)
    g2_dram = dt("g2_dram", [OI, NTC, 128, TCH], F32)
    or_dram = dt("or_dram", [OI, NTC, 128, TCH], F32)

    ctxR = ExitStack()

    def mk(stack):
        sbf = lambda *a, **k: stack.enter_context(nc.sbuf_tensor(*a, **k))
        return sbf

    sbR = mk(ctxR)
    psumR = lambda *a, **k: ctxR.enter_context(nc.psum_tensor(*a, **k))
    mksem = lambda n: Sem(ctxR.enter_context(nc.semaphore(n)))

    sLD = mksem("sLD")
    sST = mksem("sST")
    sGD = mksem("sGD")
    sCC = mksem("sCC")
    sMM = mksem("sMM")
    sVE = mksem("sVE")
    sSC = mksem("sSC")
    sGP = mksem("sGP")
    sWT = [mksem("sWT0"), mksem("sWT1")]

    cst = sbR("cst", [128, 8], F32)
    r7 = sbR("r7", [128, 1], F32)
    r32767 = sbR("r32767", [128, 1], F32)
    ident = sbR("ident", [128, 128], F32)
    attr_c = sbR("attr_c", [128, 1], F32)
    bsel_c = sbR("bsel_c", [4, 1], F32)
    s_s1 = sbR("s_s1", [128, 1], F32)
    s_rs1 = sbR("s_rs1", [128, 1], F32)
    s_s3 = sbR("s_s3", [128, 1], F32)
    s_rs3 = sbR("s_rs3", [128, 1], F32)
    s_s3m = sbR("s_s3m", [128, 1], F32)
    s_s2 = sbR("s_s2", [128, 1], F32)
    s_rs2 = sbR("s_rs2", [128, 1], F32)
    s_s4 = sbR("s_s4", [128, 1], F32)
    s_rs4 = sbR("s_rs4", [128, 1], F32)
    s_s42 = sbR("s_s42", [128, 1], F32)
    s_s5 = sbR("s_s5", [128, 1], F32)
    s_rs5 = sbR("s_rs5", [128, 1], F32)
    s_tmp = sbR("s_tmp", [128, 2], F32)
    s_bc = [sbR(f"s_bc{i}", [128, 128], F32) for i in range(4)]
    numh = [sbR(f"numh{i}", [128, H], F32) for i in range(2)]
    numT_h = [sbR(f"numTh{i}", [H, 128], F32) for i in range(2)]
    m_all = sbR("m_all", [1, 16], F32)
    d_all = sbR("d_all", [1, 16], F32)
    sr_negm = sbR("sr_negm", [1, 1], F32)
    mm8 = sbR("mm8", [1, 8], F32)
    t16 = sbR("t16", [1, 16], F32)
    w01 = sbR("w01", [1, 16], F32)
    dmrg = sbR("dmrg", [1, 8], F32)
    t8a = sbR("t8a", [1, 8], F32)
    mcol = sbR("mcol", [8, 1], F32)
    dcol = sbR("dcol", [8, 1], F32)
    w0c = sbR("w0c", [8, 1], F32)
    w1c = sbR("w1c", [8, 1], F32)
    numT = sbR("numT", [H, 128], F32)
    pack_sb = sbR("pack_sb", [H, 130], F32)
    gA = sbR("gA", [32, 130], F32)
    gB = sbR("gB", [32, 130], F32)
    mx32 = sbR("mx32", [32, 1], F32)
    ngm32 = sbR("ngm32", [32, 1], F32)
    wA32 = sbR("wA32", [32, 1], F32)
    wB32 = sbR("wB32", [32, 1], F32)
    t129 = sbR("t129", [32, 129], F32)
    cn129 = sbR("cn129", [32, 129], F32)
    dcb = sbR("dcb", [32, 1], F32)
    rdcb = sbR("rdcb", [32, 1], F32)
    ctx32 = sbR("ctx32", [32, 128], F32)
    ctxT = sbR("ctxT", [128, 32], F32)
    ctxT_h = sbR("ctxT_h", [128, 32], BF16)
    ctxT_l = sbR("ctxT_l", [128, 32], BF16)
    lnsum = sbR("lnsum", [128, 1], F32)
    negmu = sbR("negmu", [128, 1], F32)
    ssq = sbR("ssq", [128, 1], F32)
    lnsd = sbR("lnsd", [128, 1], F32)
    lnr = sbR("lnr", [128, 1], F32)
    sqscr = sbR("sqscr", [128, D], F32)
    xm32 = sbR("xm32", [128, NTM], F32)
    gmax = sbR("gmax", [128, OI * NTC], F32)
    umax = sbR("umax", [128, OI * NTC], F32)
    g2max = sbR("g2max", [128, OI * NTC], F32)
    omax = sbR("omax", [128, OI * NTC], F32)
    red1 = sbR("red1", [128, 1], F32)
    redT = sbR("redT", [1, 128], F32)
    red0 = sbR("red0", [1, 1], F32)
    zz16 = sbR("zz16", [1, 16], F32)

    nc.const_aps.register(F32, MAGIC, cst[:, 0:1])
    nc.const_aps.register(F32, -MAGIC, cst[:, 1:2])

    pMM = [psumR(f"pMM{i}", [128, TCH], F32) for i in range(4)]
    pTR = [psumR(f"pTR{i}", [128, TCH], F32) for i in range(2)]
    pSR = pTR[0][0:1, 0:TCH]
    pSEL = pTR[1][0:1, 0:TCH]
    pFH = [pMM[0][0:4, 0:TCH], pMM[1][0:4, 0:TCH]]

    EQ = {k: [] for k in ("sync", "tensor", "vector", "scalar", "gpsimd")}

    def em(eng, fn):
        EQ[eng].append(fn)

    def p0(sync):
        sync.dma_start(out=ident[:, :], in_=ident_in[:, :]).then_inc(sLD.h, 16)
        sync.dma_start(out=attr_c[:, 0:1],
                       in_=attr.rearrange("a f -> f a")).then_inc(sLD.h, 16)
        sync.dma_start(out=bsel_c[:, :], in_=bsel[:, :]).then_inc(sLD.h, 16)
    em("sync", p0)
    sLD.inc(16 * 3)

    def p0v(vector):
        vector.memset(cst[:, 0:1], MAGIC).then_inc(sVE.h, 1)
        vector.memset(cst[:, 1:2], -MAGIC).then_inc(sVE.h, 1)
        vector.memset(cst[:, 2:3], 65536.0).then_inc(sVE.h, 1)
        vector.memset(cst[:, 3:4], LN_EPS).then_inc(sVE.h, 1)
        vector.memset(cst[:, 4:5], 7.0).then_inc(sVE.h, 1)
        vector.memset(cst[:, 5:6], 32767.0).then_inc(sVE.h, 1)
        vector.memset(cst[:, 6:7], INV1024).then_inc(sVE.h, 1)
        vector.wait_ge(sVE.h, 7)
        vector.reciprocal(r7[:, :], cst[:, 4:5])
        vector.reciprocal(r32767[:, :], cst[:, 5:6]).then_inc(sVE.h, 1)
    em("vector", p0v)
    sVE.inc(7)
    ev_consts = sVE.inc(1)

    def p0z(vector):
        vector.memset(zz16[:, :], 0.0).then_inc(sVE.h, 1)
    em("vector", p0z)
    ev_zz = sVE.inc(1)

    def p0zd(gpsimd):
        gpsimd.wait_ge(sVE.h, ev_zz)
        for i in range(4):
            gpsimd.dma_start(out=ccs_in[i][:, :], in_=zz16[:, :]).then_inc(sGD.h, 16)
    em("gpsimd", p0zd)
    ev_ccz = sGD.inc(64)

    ctxA = ExitStack()
    sbA = mk(ctxA)
    wr_stage = sbA("wr_stage", [128, D], F32)
    sr0 = sbA("sr0", [1, TH], F32)
    fh4 = sbA("fh4", [4, D], F32)
    fh1 = sbA("fh1", [1, D], F32)
    wrt_hi = [sbA(f"wrt_hi{k}", [128, D], BF16) for k in range(KD)]
    wrt_lo = [sbA(f"wrt_lo{k}", [128, D], BF16) for k in range(KD)]
    hst = [sbA("hst0", [128, KD * TCH], F32)]
    hh = sbA("hh", [128, KD * TCH], BF16)
    hl = sbA("hl", [128, KD * TCH], BF16)
    fr_sb = [sbA(f"fr{k}", [128, TH], F32) for k in range(KD)]
    lf_sb = [sbA(f"lf_sb{i}", [128, TCH], F32) for i in range(2)]
    ebc = sbA("ebc", [128, TH], F32)

    pt_free = [None, None]
    wr_v_done = {}
    for i in range(KD):
        def ldf(sync, i=i):
            if i >= 1:
                sync.wait_ge(sVE.h, wr_v_done[i - 1])
            sync.dma_start(out=wr_stage[:, :],
                           in_=wr_raw[i * 128:(i + 1) * 128, :]).then_inc(sLD.h, 16)
        em("sync", ldf)
        ev_ld = sLD.inc(16)
        for j in range(KD):
            ps = pTR[j % 2]
            pf = pt_free[j % 2]

            def tfn(tensor, i=i, j=j, ps=ps, ev_ld=ev_ld, pf=pf):
                tensor.wait_ge(sLD.h, ev_ld)
                if i == 0 and j == 0:
                    tensor.wait_ge(sVE.h, ev_consts)
                if pf is not None:
                    tensor.wait_ge(sVE.h, pf)
                tensor.transpose(ps[0:128, 0:128],
                                 wr_stage[:, j * 128:(j + 1) * 128],
                                 ident[:, :]).then_inc(sMM.h, 1)
            em("tensor", tfn)
            ev_mm = sMM.inc(1)

            def vfn(vector, i=i, j=j, ps=ps, ev_mm=ev_mm):
                vector.wait_ge(sMM.h, ev_mm)
                vector.tensor_copy(wrt_hi[j][:, i * 128:(i + 1) * 128],
                                   ps[0:128, 0:128])
                vector.tensor_tensor(wrt_lo[j][:, i * 128:(i + 1) * 128],
                                     ps[0:128, 0:128],
                                     wrt_hi[j][:, i * 128:(i + 1) * 128],
                                     op=ALU.subtract).then_inc(sVE.h, 1)
            em("vector", vfn)
            pt_free[j % 2] = sVE.inc(1)
        wr_v_done[i] = sVE.n
    ev_wrt_v = sVE.n

    wsh_stage = sbA("wsh_stage", [128, D], F32)
    sblk = sbA("sblk", [128, 32], F32)
    sinv = sbA("sinv", [128, 32], F32)
    wdq = sbA("wdq", [128, D], F32)
    pk_hi = sbA("pk_hi", [128, D], BF16)
    pk_lo = sbA("pk_lo", [128, D], BF16)

    sh_ptfree = [None, None]
    ev_pk_prev = [0]

    def shard_tile(raw_ap, off_hi, off_lo):
        def ldf(sync, prev=ev_pk_prev[0]):
            sync.wait_ge(sGD.h, prev)
            sync.dma_start(out=wsh_stage[:, :], in_=raw_ap).then_inc(sLD.h, 16)
        em("sync", ldf)
        ev_ld = sLD.inc(16)

        e1 = sVE.n + 1
        e2 = sVE.n + 2
        e3 = sVE.n + 3

        def vfn(vector, ev_ld=ev_ld, e1=e1, e2=e2, e3=e3):
            vector.wait_ge(sLD.h, ev_ld)
            vector.wait_ge(sVE.h, ev_consts)
            vector.tensor_reduce(sblk[:, :],
                                 wsh_stage[:, :].rearrange("p (b c) -> p b c", c=32),
                                 axis=AX.X, op=ALU.max,
                                 apply_absolute_value=True).then_inc(sVE.h, 1)
            vector.wait_ge(sVE.h, e1)
            vector.tensor_scalar(sblk[:, :], sblk[:, :], r7[:, 0:1], 1e-12,
                                 op0=ALU.mult, op1=ALU.max).then_inc(sVE.h, 1)
            vector.wait_ge(sVE.h, e2)
            vector.reciprocal(sinv[:, :], sblk[:, :]).then_inc(sVE.h, 1)
            vector.wait_ge(sVE.h, e3)
            vector.tensor_tensor(wdq[:, :].rearrange("p (b c) -> p b c", c=32),
                                 wsh_stage[:, :].rearrange("p (b c) -> p b c", c=32),
                                 sinv[:, :].unsqueeze(2).broadcast_to((128, 32, 32)), op=ALU.mult)
            vector.tensor_scalar(wdq[:, :], wdq[:, :], MAGIC, MAGIC,
                                 op0=ALU.add, op1=ALU.subtract)
            vector.tensor_tensor(wdq[:, :].rearrange("p (b c) -> p b c", c=32),
                                 wdq[:, :].rearrange("p (b c) -> p b c", c=32),
                                 sblk[:, :].unsqueeze(2).broadcast_to((128, 32, 32)), op=ALU.mult).then_inc(sVE.h, 1)
        em("vector", vfn)
        sVE.inc(3)
        ev_dq = sVE.inc(1)

        for j in range(KD):
            ps = pTR[j % 2]
            pf = sh_ptfree[j % 2]

            def tfn(tensor, j=j, ps=ps, ev_dq=ev_dq, pf=pf):
                tensor.wait_ge(sVE.h, ev_dq)
                if pf is not None:
                    tensor.wait_ge(sVE.h, pf)
                tensor.transpose(ps[0:128, 0:128], wdq[:, j * 128:(j + 1) * 128],
                                 ident[:, :]).then_inc(sMM.h, 1)
            em("tensor", tfn)
            ev_mm = sMM.inc(1)

            def vfn2(vector, j=j, ps=ps, ev_mm=ev_mm):
                vector.wait_ge(sMM.h, ev_mm)
                vector.tensor_copy(pk_hi[:, j * 128:(j + 1) * 128], ps[0:128, 0:128])
                vector.tensor_tensor(pk_lo[:, j * 128:(j + 1) * 128],
                                     ps[0:128, 0:128],
                                     pk_hi[:, j * 128:(j + 1) * 128],
                                     op=ALU.subtract).then_inc(sVE.h, 1)
            em("vector", vfn2)
            sh_ptfree[j % 2] = sVE.inc(1)
        ev_tiles = sVE.n

        def gdf(gpsimd, off_hi=off_hi, off_lo=off_lo, ev_tiles=ev_tiles):
            gpsimd.wait_ge(sVE.h, ev_tiles)
            gpsimd.dma_start(
                out=cc0_in.ap()[off_hi:off_hi + GU_TILE].rearrange("(p f) -> p f", p=128),
                in_=pk_hi[:, :]).then_inc(sGD.h, 16)
            gpsimd.dma_start(
                out=cc0_in.ap()[off_lo:off_lo + GU_TILE].rearrange("(p f) -> p f", p=128),
                in_=pk_lo[:, :]).then_inc(sGD.h, 16)
        em("gpsimd", gdf)
        ev_pk_prev[0] = sGD.inc(32)

    for mat, raw in ((0, wg_sh), (1, wu_sh)):
        for j in range(3):
            shard_tile(raw[j * 128:(j + 1) * 128, :],
                       _off_gu(mat, j, 0), _off_gu(mat, j, 1))
    for jj in range(3):
        shard_tile(wd_sh[:, jj * 1024:(jj + 1) * 1024],
                   _off_dn(0) + jj * 8 * 128 * 128,
                   _off_dn(1) + jj * 8 * 128 * 128)
    ev_pk_all = ev_pk_prev[0]

    def cc0(gpsimd):
        gpsimd.wait_ge(sGD.h, ev_pk_all)
        gpsimd.collective_compute(
            "AllGather", ALU.bypass, replica_groups=[list(range(N_CORES))],
            ins=[cc0_in.ap()], outs=[cc0_out.ap()]).then_inc(sCC.h, 1)
    em("gpsimd", cc0)
    ev_cc0 = sCC.inc(1)

    h_cast_done = {}
    fr_mm_done = {}
    fr_ps_free = [None, None, None, None]
    sr_ps_free = [None]
    lf_mm_prev = [None, None]
    hstat_prev = [None]
    num_prev = [None]
    ev_half_stats = {}

    for half in range(2):
        for tci in range(NTCH):
            tc = half * NTCH + tci
            buf = hst[0]

            def ldh(sync, tc=tc, buf=buf):
                if tc >= 1:
                    sync.wait_ge(sVE.h, h_cast_done[tc - 1])
                for k in range(KD):
                    sync.dma_start(
                        out=buf[:, k * TCH:(k + 1) * TCH],
                        in_=h_fm[k * 128:(k + 1) * 128, tc * TCH:(tc + 1) * TCH]
                    ).then_inc(sLD.h, 16)
            em("sync", ldh)
            ev_hld = sLD.inc(16 * KD)

            def castf(vector, tc=tc, tci=tci, buf=buf, ev_hld=ev_hld):
                vector.wait_ge(sLD.h, ev_hld)
                if tc >= 1:
                    vector.wait_ge(sMM.h, fr_mm_done[tc - 1])
                vector.tensor_copy(hh[:, :], buf[:, :])
                vector.tensor_tensor(hl[:, :], buf[:, :], hh[:, :],
                                     op=ALU.subtract).then_inc(sVE.h, 1)
            em("vector", castf)
            h_cast_done[tc] = sVE.inc(1)

            for oc in range(KD):
                ps = pMM[oc % 4]
                pf = fr_ps_free[oc % 4]

                def mmf(tensor, tc=tc, tci=tci, oc=oc, ps=ps, pf=pf):
                    if tc == 0 and oc == 0:
                        tensor.wait_ge(sVE.h, ev_wrt_v)
                    tensor.wait_ge(sVE.h, h_cast_done[tc])
                    if pf is not None:
                        tensor.wait_ge(sVE.h, pf)
                    mm = None
                    for k in range(KD):
                        tensor.matmul(ps[:, :], wrt_hi[k][:, oc * 128:(oc + 1) * 128],
                                      hh[:, k * TCH:(k + 1) * TCH],
                                      start=(k == 0), stop=False)
                        tensor.matmul(ps[:, :], wrt_hi[k][:, oc * 128:(oc + 1) * 128],
                                      hl[:, k * TCH:(k + 1) * TCH],
                                      start=False, stop=False)
                        mm = tensor.matmul(ps[:, :], wrt_lo[k][:, oc * 128:(oc + 1) * 128],
                                           hh[:, k * TCH:(k + 1) * TCH],
                                           start=False, stop=(k == KD - 1))
                    mm.then_inc(sMM.h, 1)
                em("tensor", mmf)
                ev_mm = sMM.inc(1)
                if oc == KD - 1:
                    fr_mm_done[tc] = ev_mm

                def evf(vector, tci=tci, oc=oc, ps=ps, ev_mm=ev_mm):
                    vector.wait_ge(sMM.h, ev_mm)
                    vector.tensor_copy(fr_sb[oc][:, tci * TCH:(tci + 1) * TCH],
                                       ps[:, :]).then_inc(sVE.h, 1)
                em("vector", evf)
                fr_ps_free[oc % 4] = sVE.inc(1)
        ev_fr_half_v = sVE.n

        for hc in range(H):
            for tci in range(NTCH):
                gi = hc * NTCH + tci
                lb = lf_sb[gi % 2]

                def lff(vector, hc=hc, tci=tci, lb=lb,
                        ev_fr_half_v=ev_fr_half_v, pmm=lf_mm_prev[gi % 2]):
                    vector.wait_ge(sVE.h, ev_fr_half_v)
                    if pmm is not None:
                        vector.wait_ge(sMM.h, pmm)
                    vector.scalar_tensor_tensor(
                        lb[:, :], fr_sb[hc][:, tci * TCH:(tci + 1) * TCH], 0.01,
                        fr_sb[hc][:, tci * TCH:(tci + 1) * TCH],
                        op0=ALU.mult, op1=ALU.max).then_inc(sVE.h, 1)
                em("vector", lff)
                ev_lf = sVE.inc(1)

                def srm(tensor, ev_lf=ev_lf, lb=lb, pf=sr_ps_free[0]):
                    tensor.wait_ge(sVE.h, ev_lf)
                    if pf is not None:
                        tensor.wait_ge(sSC.h, pf)
                    tensor.matmul(pSR, attr_c[:, 0:1], lb[:, :],
                                  start=True, stop=True).then_inc(sMM.h, 1)
                em("tensor", srm)
                ev_srm = sMM.inc(1)
                lf_mm_prev[gi % 2] = ev_srm

                def sre(scalar, tci=tci, ev_srm=ev_srm, prev_head=hstat_prev[0]):
                    scalar.wait_ge(sMM.h, ev_srm)
                    if tci == 0 and prev_head is not None:
                        scalar.wait_ge(sGD.h, prev_head)
                    scalar.copy(sr0[0:1, tci * TCH:(tci + 1) * TCH],
                                pSR).then_inc(sSC.h, 1)
                em("scalar", sre)
                sr_ps_free[0] = sSC.inc(1)
            ev_sr_head = sSC.n
            slot = half * 8 + hc

            ea = sVE.n + 1

            def hstat_v(vector, slot=slot, ev_sr_head=ev_sr_head, ea=ea):
                vector.wait_ge(sSC.h, ev_sr_head)
                vector.tensor_reduce(m_all[0:1, slot:slot + 1], sr0[:, :],
                                     axis=AX.X, op=ALU.max).then_inc(sVE.h, 1)
                vector.wait_ge(sVE.h, ea)
                vector.tensor_scalar(sr_negm[:, :], m_all[0:1, slot:slot + 1],
                                     -1.0, None, op0=ALU.mult).then_inc(sVE.h, 1)
            em("vector", hstat_v)
            sVE.inc(1)
            ev_hm = sVE.inc(1)

            def hstat_s(scalar, ev_hm=ev_hm):
                scalar.wait_ge(sVE.h, ev_hm)
                scalar.activation(sr0[:, :], sr0[:, :], ACT.Exp,
                                  bias=sr_negm[0:1, 0:1], scale=1.0).then_inc(sSC.h, 1)
            em("scalar", hstat_s)
            ev_he = sSC.inc(1)

            def hstat_v2(vector, slot=slot, ev_he=ev_he):
                vector.wait_ge(sSC.h, ev_he)
                vector.tensor_reduce(d_all[0:1, slot:slot + 1], sr0[:, :],
                                     axis=AX.X, op=ALU.add).then_inc(sVE.h, 1)
            em("vector", hstat_v2)
            sVE.inc(1)

            def e_out(gpsimd, hc=hc, half=half, ev_he=ev_he):
                gpsimd.wait_ge(sSC.h, ev_he)
                gpsimd.dma_start(out=e_dram[hc:hc + 1, half * TH:(half + 1) * TH],
                                 in_=sr0[:, :]).then_inc(sGD.h, 16)
            em("gpsimd", e_out)
            hstat_prev[0] = sGD.inc(16)
            ev_eh_dram = hstat_prev[0]

            def ebl(gpsimd, hc=hc, half=half, ev_eh_dram=ev_eh_dram,
                    prev=num_prev[0]):
                gpsimd.wait_ge(sGD.h, ev_eh_dram)
                if prev is not None:
                    gpsimd.wait_ge(sVE.h, prev)
                gpsimd.dma_start(
                    out=ebc[:, :],
                    in_=e_dram[hc:hc + 1, half * TH:(half + 1) * TH]
                    .partition_broadcast(128)).then_inc(sGD.h, 16)
            em("gpsimd", ebl)
            ev_ebl = sGD.inc(16)

            def numf(vector, hc=hc, half=half, ev_ebl=ev_ebl):
                vector.wait_ge(sGD.h, ev_ebl)
                vector.scalar_tensor_tensor(
                    ebc[:, :], fr_sb[hc][:, :], 1.0, ebc[:, :],
                    op0=ALU.mult, op1=ALU.mult,
                    accum_out=numh[half][:, hc:hc + 1]).then_inc(sVE.h, 1)
            em("vector", numf)
            num_prev[0] = sVE.inc(1)
        ev_half_stats[half] = sVE.n

        def numt_t(tensor, half=half, ev=ev_half_stats[half]):
            tensor.wait_ge(sVE.h, ev)
            tensor.transpose(pTR[0][0:H, 0:128], numh[half][:, :],
                             ident[:, :]).then_inc(sMM.h, 1)
        em("tensor", numt_t)
        ev_ntm = sMM.inc(1)

        def numt_v(vector, half=half, ev_ntm=ev_ntm):
            vector.wait_ge(sMM.h, ev_ntm)
            vector.tensor_copy(numT_h[half][:, :], pTR[0][0:H, 0:128]).then_inc(sVE.h, 1)
        em("vector", numt_v)
        sVE.inc(1)
    ev_both_halves = sVE.n

    _ea = sVE.n + 1

    def merge_v(vector, _ea=_ea):
        vector.wait_ge(sVE.h, ev_both_halves)
        vector.tensor_tensor(mm8[:, :], m_all[0:1, 0:8], m_all[0:1, 8:16],
                             op=ALU.max).then_inc(sVE.h, 1)
        vector.wait_ge(sVE.h, _ea)
        vector.tensor_tensor(t16[0:1, 0:8], m_all[0:1, 0:8], mm8[:, :],
                             op=ALU.subtract)
        vector.tensor_tensor(t16[0:1, 8:16], m_all[0:1, 8:16], mm8[:, :],
                             op=ALU.subtract).then_inc(sVE.h, 1)
    em("vector", merge_v)
    sVE.inc(1)
    ev_mM = sVE.inc(1)

    def merge_s(scalar):
        scalar.wait_ge(sVE.h, ev_mM)
        scalar.activation(w01[:, :], t16[:, :], ACT.Exp, bias=0.0,
                          scale=1.0).then_inc(sSC.h, 1)
    em("scalar", merge_s)
    ev_w2 = sSC.inc(1)

    _eb = sVE.n + 1

    def merge_d(vector, _eb=_eb):
        vector.wait_ge(sSC.h, ev_w2)
        vector.tensor_tensor(t8a[:, :], w01[0:1, 0:8], d_all[0:1, 0:8],
                             op=ALU.mult)
        vector.tensor_tensor(dmrg[:, :], w01[0:1, 8:16], d_all[0:1, 8:16],
                             op=ALU.mult).then_inc(sVE.h, 1)
        vector.wait_ge(sVE.h, _eb)
        vector.tensor_tensor(dmrg[:, :], dmrg[:, :], t8a[:, :],
                             op=ALU.add).then_inc(sVE.h, 1)
    em("vector", merge_d)
    sVE.inc(1)
    ev_dm = sVE.inc(1)

    mg_tp_free = [None]
    mg_cols = []
    for src_ap, dst in ((mm8[0:1, 0:8], mcol), (dmrg[0:1, 0:8], dcol),
                        (w01[0:1, 0:8], w0c), (w01[0:1, 8:16], w1c)):
        def mgt(tensor, src_ap=src_ap, pf=mg_tp_free[0]):
            tensor.wait_ge(sVE.h, ev_dm)
            if pf is not None:
                tensor.wait_ge(sVE.h, pf)
            tensor.transpose(pTR[0][0:8, 0:1], src_ap,
                             ident[0:1, 0:1]).then_inc(sMM.h, 1)
        em("tensor", mgt)
        ev_tp = sMM.inc(1)

        def mge(vector, dst=dst, ev_tp=ev_tp):
            vector.wait_ge(sMM.h, ev_tp)
            vector.tensor_copy(dst[:, :], pTR[0][0:8, 0:1]).then_inc(sVE.h, 1)
        em("vector", mge)
        mg_tp_free[0] = sVE.inc(1)
    ev_cols = sVE.n

    _e1 = sVE.n + 1
    _e2 = sVE.n + 2

    def merge_num(vector, _e1=_e1, _e2=_e2):
        vector.wait_ge(sVE.h, ev_cols)
        vector.tensor_scalar(numT[:, :], numT_h[0][:, :], w0c[:, 0:1], None,
                             op0=ALU.mult).then_inc(sVE.h, 1)
        vector.wait_ge(sVE.h, _e1)
        vector.scalar_tensor_tensor(numT[:, :], numT_h[1][:, :], w1c[:, 0:1],
                                    numT[:, :], op0=ALU.mult,
                                    op1=ALU.add).then_inc(sVE.h, 1)
        vector.wait_ge(sVE.h, _e2)
        vector.tensor_copy(pack_sb[:, 0:1], mcol[:, :])
        vector.tensor_copy(pack_sb[:, 1:2], dcol[:, :])
        vector.tensor_copy(pack_sb[:, 2:130], numT[:, :]).then_inc(sVE.h, 1)
    em("vector", merge_num)
    sVE.inc(2)
    ev_pack = sVE.inc(1)

    def cc1_pack(gpsimd):
        gpsimd.wait_ge(sVE.h, ev_pack)
        gpsimd.dma_start(out=cc1_in.ap(), in_=pack_sb[:, :]).then_inc(sGD.h, 16)
    em("gpsimd", cc1_pack)
    ev_cc1_in = sGD.inc(16)

    def cc1_run(gpsimd):
        gpsimd.wait_ge(sGD.h, ev_cc1_in)
        gpsimd.collective_compute(
            "AllGather", ALU.bypass, replica_groups=[list(range(N_CORES))],
            ins=[cc1_in.ap()], outs=[cc1_out.ap()]).then_inc(sCC.h, 1)
    em("gpsimd", cc1_run)
    ev_cc1 = sCC.inc(1)

    wf_ptfree = [None, None]
    wf_v_done = {}
    for i in range(KD):
        def ldwf(sync, i=i):
            if i >= 1:
                sync.wait_ge(sVE.h, wf_v_done[i - 1])
            else:
                sync.wait_ge(sVE.h, ev_both_halves)
            sync.dma_start(out=wr_stage[:, :],
                           in_=wf_raw[i * 128:(i + 1) * 128, :]).then_inc(sLD.h, 16)
        em("sync", ldwf)
        ev_ld = sLD.inc(16)
        for j in range(KD):
            ps = pTR[(j % 2)]
            pf = wf_ptfree[j % 2]

            def tfn(tensor, i=i, j=j, ps=ps, ev_ld=ev_ld, pf=pf):
                tensor.wait_ge(sLD.h, ev_ld)
                if pf is not None:
                    tensor.wait_ge(sVE.h, pf)
                tensor.transpose(ps[0:128, 0:128],
                                 wr_stage[:, j * 128:(j + 1) * 128],
                                 ident[:, :]).then_inc(sMM.h, 1)
            em("tensor", tfn)
            ev_mm = sMM.inc(1)

            def vfn(vector, i=i, j=j, ps=ps, ev_mm=ev_mm):
                vector.wait_ge(sMM.h, ev_mm)
                vector.tensor_copy(wrt_hi[j][:, i * 128:(i + 1) * 128],
                                   ps[0:128, 0:128])
                vector.tensor_tensor(wrt_lo[j][:, i * 128:(i + 1) * 128],
                                     ps[0:128, 0:128],
                                     wrt_hi[j][:, i * 128:(i + 1) * 128],
                                     op=ALU.subtract).then_inc(sVE.h, 1)
            em("vector", vfn)
            wf_ptfree[j % 2] = sVE.inc(1)
        wf_v_done[i] = sVE.n
    ev_wft = sVE.n

    def comb_ld(gpsimd):
        gpsimd.wait_ge(sCC.h, ev_cc1)
        v = cc1_out.ap().rearrange("(b a) h c -> b a h c", a=2)
        gpsimd.dma_start(out=gA[:, :], in_=v[:, 0:1]).then_inc(sGD.h, 16)
        gpsimd.dma_start(out=gB[:, :], in_=v[:, 1:2]).then_inc(sGD.h, 16)
    em("gpsimd", comb_ld)
    ev_gAB = sGD.inc(32)

    _ec = sVE.n + 1

    def comb_v(vector, _ec=_ec):
        vector.wait_ge(sGD.h, ev_gAB)
        vector.tensor_tensor(mx32[:, :], gA[:, 0:1], gB[:, 0:1],
                             op=ALU.max).then_inc(sVE.h, 1)
        vector.wait_ge(sVE.h, _ec)
        vector.tensor_scalar(ngm32[:, :], mx32[:, :], -1.0, None,
                             op0=ALU.mult).then_inc(sVE.h, 1)
    em("vector", comb_v)
    sVE.inc(1)
    ev_mx = sVE.inc(1)

    def comb_s(scalar):
        scalar.wait_ge(sVE.h, ev_mx)
        scalar.activation(wA32[:, :], gA[:, 0:1], ACT.Exp, bias=ngm32[:, 0:1],
                          scale=1.0)
        scalar.activation(wB32[:, :], gB[:, 0:1], ACT.Exp, bias=ngm32[:, 0:1],
                          scale=1.0).then_inc(sSC.h, 1)
    em("scalar", comb_s)
    ev_w32 = sSC.inc(1)

    _g = [sVE.n + k for k in range(1, 5)]

    def comb_v2(vector, _g=_g):
        vector.wait_ge(sSC.h, ev_w32)
        vector.tensor_scalar(t129[:, :], gA[:, 1:130], wA32[:, 0:1], None,
                             op0=ALU.mult).then_inc(sVE.h, 1)
        vector.wait_ge(sVE.h, _g[0])
        vector.scalar_tensor_tensor(cn129[:, :], gB[:, 1:130], wB32[:, 0:1],
                                    t129[:, :], op0=ALU.mult,
                                    op1=ALU.add).then_inc(sVE.h, 1)
        vector.wait_ge(sVE.h, _g[1])
        vector.tensor_copy(dcb[:, :], cn129[:, 0:1]).then_inc(sVE.h, 1)
        vector.wait_ge(sVE.h, _g[2])
        vector.reciprocal(rdcb[:, :], dcb[:, :]).then_inc(sVE.h, 1)
        vector.wait_ge(sVE.h, _g[3])
        vector.tensor_scalar(ctx32[:, :], cn129[:, 1:129], rdcb[:, 0:1], None,
                             op0=ALU.mult).then_inc(sVE.h, 1)
    em("vector", comb_v2)
    sVE.inc(4)
    ev_ctx = sVE.inc(1)

    def ctxt_t(tensor):
        tensor.wait_ge(sVE.h, ev_ctx)
        tensor.transpose(pTR[0][0:128, 0:32], ctx32[:, :], ident[0:32, 0:32]).then_inc(sMM.h, 1)
    em("tensor", ctxt_t)
    ev_ctxT_mm = sMM.inc(1)

    _h = [sVE.n + 1, sVE.n + 2]

    def ctxt_v(vector, _h=_h):
        vector.wait_ge(sMM.h, ev_ctxT_mm)
        vector.tensor_copy(ctxT[:, :], pTR[0][0:128, 0:32]).then_inc(sVE.h, 1)
        vector.wait_ge(sVE.h, _h[0])
        vector.tensor_copy(ctxT_h[:, :], ctxT[:, :]).then_inc(sVE.h, 1)
        vector.wait_ge(sVE.h, _h[1])
        vector.tensor_tensor(ctxT_l[:, :], ctxT[:, :], ctxT_h[:, :],
                             op=ALU.subtract).then_inc(sVE.h, 1)
    em("vector", ctxt_v)
    sVE.inc(2)
    ev_ctxT = sVE.inc(1)

    fh_mm_evts = []
    for half in range(2):
        def fhm(tensor, half=half):
            tensor.wait_ge(sVE.h, max(ev_ctxT, ev_wft))
            mm = None
            for kc in range(KD):
                tensor.matmul(pFH[half], ctxT_h[:, kc:kc + 25:8],
                              wrt_hi[kc][:, half * TCH:(half + 1) * TCH],
                              start=(kc == 0), stop=False)
                tensor.matmul(pFH[half], ctxT_h[:, kc:kc + 25:8],
                              wrt_lo[kc][:, half * TCH:(half + 1) * TCH],
                              start=False, stop=False)
                mm = tensor.matmul(pFH[half], ctxT_l[:, kc:kc + 25:8],
                                   wrt_hi[kc][:, half * TCH:(half + 1) * TCH],
                                   start=False, stop=(kc == KD - 1))
            mm.then_inc(sMM.h, 1)
        em("tensor", fhm)
        ev_fhm = sMM.inc(1)
        fh_mm_evts.append(ev_fhm)

        def fhe(vector, half=half, ev_fhm=ev_fhm):
            vector.wait_ge(sMM.h, ev_fhm)
            vector.tensor_copy(fh4[:, half * TCH:(half + 1) * TCH],
                               pFH[half]).then_inc(sVE.h, 1)
        em("vector", fhe)
        sVE.inc(1)
    ev_fh4 = sVE.n

    sel_prev = [None]
    for half in range(2):
        def self_m(tensor, half=half, pf=sel_prev[0]):
            tensor.wait_ge(sVE.h, ev_fh4)
            if pf is not None:
                tensor.wait_ge(sSC.h, pf)
            tensor.matmul(pSEL, bsel_c[:, 0:1],
                          fh4[:, half * TCH:(half + 1) * TCH],
                          start=True, stop=True).then_inc(sMM.h, 1)
        em("tensor", self_m)
        ev_selm = sMM.inc(1)

        def sel_e(scalar, half=half, ev_selm=ev_selm):
            scalar.wait_ge(sMM.h, ev_selm)
            scalar.copy(fh1[0:1, half * TCH:(half + 1) * TCH], pSEL).then_inc(sSC.h, 1)
        em("scalar", sel_e)
        sel_prev[0] = sSC.inc(1)
    ev_fh1 = sSC.n
    ev_fh_mm_last = sMM.n

    ctxA.close()

    ctxM = ExitStack()
    sbM = mk(ctxM)
    x1c = [sbM(f"x1c{i}", [128, D], F32) for i in range(2)]
    fhb = sbM("fhb", [128, D], F32)

    def fh_out(gpsimd):
        gpsimd.wait_ge(sSC.h, ev_fh1)
        gpsimd.dma_start(out=fh_dram[:, :], in_=fh1[:, :]).then_inc(sGD.h, 16)
    em("gpsimd", fh_out)
    ev_fh_dram = sGD.inc(16)

    def fh_bc(gpsimd):
        gpsimd.wait_ge(sGD.h, ev_fh_dram)
        gpsimd.wait_ge(sMM.h, ev_fh_mm_last)
        gpsimd.dma_start(out=fhb[:, :],
                         in_=fh_dram[0:1, :].partition_broadcast(128)).then_inc(sGD.h, 16)
    em("gpsimd", fh_bc)
    ev_fhb = sGD.inc(16)

    ctxP2 = ExitStack()
    sbP = mk(ctxP2)
    htm_st = [sbP(f"htm{i}", [128, D], F32) for i in range(2)]
    p2y = sbP("p2y", [128, D], F32)
    p2t = sbP("p2t", [128, D], F32)
    p2a = sbP("p2a", [128, D], F32)
    gam_b = sbP("gam_b", [128, D], F32)
    bet_b = sbP("bet_b", [128, D], F32)

    def p2_gb_ld(sync):
        sync.wait_ge(sMM.h, ev_fh_mm_last)
        sync.wait_ge(sSC.h, ev_fh1)
        sync.dma_start(out=gam_b[:, :],
                       in_=gamma[0:1, :].partition_broadcast(128)).then_inc(sLD.h, 16)
        sync.dma_start(out=bet_b[:, :],
                       in_=beta[0:1, :].partition_broadcast(128)).then_inc(sLD.h, 16)
    em("sync", p2_gb_ld)
    ev_gb_ld = sLD.inc(32)

    p2_done = {}
    x1_spill_done = {}
    for tcm in range(NTM):
        buf = htm_st[tcm % 2]
        xb = x1c[tcm % 2]

        def ldh2(sync, tcm=tcm, buf=buf):
            if tcm < 2:
                sync.wait_ge(sMM.h, ev_fh_mm_last)
                sync.wait_ge(sSC.h, ev_fh1)
                sync.wait_ge(sGD.h, ev_fhb)
            else:
                sync.wait_ge(sVE.h, p2_done[tcm - 2])
            sync.dma_start(out=buf[:, :],
                           in_=h_tm[tcm * TMC:(tcm + 1) * TMC, :]).then_inc(sLD.h, 16)
        em("sync", ldh2)
        ev_ld = sLD.inc(16)

        _i1 = sVE.n + 1

        def p2v(vector, tcm=tcm, buf=buf, ev_ld=ev_ld, _i1=_i1):
            vector.wait_ge(sLD.h, ev_ld)
            if tcm == 0:
                vector.wait_ge(sGD.h, ev_fhb)
                vector.wait_ge(sLD.h, ev_gb_ld)
            if tcm >= 2:
                vector.wait_ge(sST.h, x1_spill_done[tcm - 2])
            vector.tensor_tensor(p2y[:, :], buf[:, :], fhb[:, :], op=ALU.add)
            vector.tensor_reduce(lnsum[:, :], p2y[:, :], axis=AX.X,
                                 op=ALU.add).then_inc(sVE.h, 1)
            vector.wait_ge(sVE.h, _i1)
            vector.tensor_scalar(negmu[:, :], lnsum[:, :], -INV1024, None,
                                 op0=ALU.mult).then_inc(sVE.h, 1)
            vector.wait_ge(sVE.h, _i1 + 1)
            vector.tensor_scalar(p2t[:, :], p2y[:, :], negmu[:, 0:1], None,
                                 op0=ALU.add).then_inc(sVE.h, 1)
        em("vector", p2v)
        sVE.inc(2)
        ev_t1 = sVE.inc(1)

        def p2s(scalar, ev_t1=ev_t1):
            scalar.wait_ge(sVE.h, ev_t1)
            scalar.activation(sqscr[:, :], p2t[:, :], ACT.Square,
                              accum_out=ssq[:, :])
            scalar.activation(lnsd[:, :], ssq[:, :], ACT.Sqrt,
                              bias=cst[:, 3:4], scale=cst[:, 6:7]).then_inc(sSC.h, 1)
        em("scalar", p2s)
        ev_sd1 = sSC.inc(1)

        _j1 = sVE.n + 1

        def p2v2(vector, buf=buf, ev_sd1=ev_sd1, _j1=_j1):
            vector.wait_ge(sSC.h, ev_sd1)
            vector.reciprocal(lnr[:, :], lnsd[:, :]).then_inc(sVE.h, 1)
            vector.wait_ge(sVE.h, _j1)
            vector.scalar_tensor_tensor(p2a[:, :], p2t[:, :], lnr[:, 0:1],
                                        gam_b[:, :], op0=ALU.mult, op1=ALU.mult)
            vector.tensor_tensor(p2a[:, :], p2a[:, :], bet_b[:, :], op=ALU.add)
            vector.tensor_tensor(p2y[:, :], buf[:, :], p2a[:, :], op=ALU.add)
            vector.tensor_reduce(lnsum[:, :], p2y[:, :], axis=AX.X,
                                 op=ALU.add).then_inc(sVE.h, 1)
            vector.wait_ge(sVE.h, _j1 + 1)
            vector.tensor_scalar(negmu[:, :], lnsum[:, :], -INV1024, None,
                                 op0=ALU.mult).then_inc(sVE.h, 1)
            vector.wait_ge(sVE.h, _j1 + 2)
            vector.tensor_scalar(p2t[:, :], p2y[:, :], negmu[:, 0:1], None,
                                 op0=ALU.add).then_inc(sVE.h, 1)
        em("vector", p2v2)
        sVE.inc(3)
        ev_t2 = sVE.inc(1)

        def p2s2(scalar, ev_t2=ev_t2):
            scalar.wait_ge(sVE.h, ev_t2)
            scalar.activation(sqscr[:, :], p2t[:, :], ACT.Square,
                              accum_out=ssq[:, :])
            scalar.activation(lnsd[:, :], ssq[:, :], ACT.Sqrt,
                              bias=cst[:, 3:4], scale=cst[:, 6:7]).then_inc(sSC.h, 1)
        em("scalar", p2s2)
        ev_sd2 = sSC.inc(1)

        _k1 = sVE.n + 1

        def p2v3(vector, tcm=tcm, xb=xb, ev_sd2=ev_sd2, _k1=_k1):
            vector.wait_ge(sSC.h, ev_sd2)
            vector.reciprocal(lnr[:, :], lnsd[:, :]).then_inc(sVE.h, 1)
            vector.wait_ge(sVE.h, _k1)
            vector.tensor_scalar(xb[:, :], p2t[:, :], lnr[:, 0:1], None,
                                 op0=ALU.mult)
            vector.tensor_reduce(xm32[:, tcm:tcm + 1], xb[:, :], axis=AX.X,
                                 op=ALU.max, apply_absolute_value=True).then_inc(sVE.h, 1)
        em("vector", p2v3)
        sVE.inc(1)
        p2_done[tcm] = sVE.inc(1)

        def x1sp(scalar, tcm=tcm, xb=xb):
            scalar.wait_ge(sVE.h, p2_done[tcm])
            scalar.dma_start(out=x1_dram.ap()[tcm], in_=xb[:, :]).then_inc(sST.h, 16)
        em("scalar", x1sp)
        x1_spill_done[tcm] = sST.inc(16)
    ev_p2_all = sVE.n
    ev_x1_sp_all = sST.n

    ctxP2.close()

    last_gf = [0]
    last_red_user = [None]

    def lmax_emit(max_cols_ap, cc_idx, slot, ready_sem, ev_ready):
        def vf(vector, prev_gf=last_gf[0]):
            vector.wait_ge(ready_sem.h, ev_ready)
            if prev_gf:
                vector.wait_ge(sGD.h, prev_gf)
            vector.tensor_reduce(red1[:, :], max_cols_ap, axis=AX.X,
                                 op=ALU.max).then_inc(sVE.h, 1)
        em("vector", vf)
        ev_r1 = sVE.inc(1)

        def tf(tensor):
            tensor.wait_ge(sVE.h, ev_r1)
            tensor.transpose(pTR[0][0:1, 0:128], red1[:, 0:1],
                             ident[:, :]).then_inc(sMM.h, 1)
        em("tensor", tf)
        ev_tm = sMM.inc(1)

        _o1 = sVE.n + 1

        def vf2(vector, ev_tm=ev_tm, _o1=_o1):
            vector.wait_ge(sMM.h, ev_tm)
            vector.tensor_copy(redT[:, :], pTR[0][0:1, 0:128]).then_inc(sVE.h, 1)
            vector.wait_ge(sVE.h, _o1)
            vector.tensor_reduce(red0[:, :], redT[:, :], axis=AX.X,
                                 op=ALU.max).then_inc(sVE.h, 1)
        em("vector", vf2)
        sVE.inc(1)
        ev_r0 = sVE.inc(1)

        def gf(gpsimd, cc_idx=cc_idx, slot=slot, ev_r0=ev_r0):
            gpsimd.wait_ge(sVE.h, ev_r0)
            gpsimd.wait_ge(sGD.h, ev_ccz)
            gpsimd.dma_start(out=ccs_in[cc_idx][0:1, slot:slot + 1],
                             in_=red0[:, :]).then_inc(sGD.h, 16)
        em("gpsimd", gf)
        last_gf[0] = sGD.inc(16)
        return last_gf[0]

    def cc_run(cc_idx, ev_in_ready):
        def cf(gpsimd, cc_idx=cc_idx, ev_in_ready=ev_in_ready):
            gpsimd.wait_ge(sGD.h, ev_in_ready)
            gpsimd.collective_compute(
                "AllGather", ALU.bypass, replica_groups=[list(range(N_CORES))],
                ins=[ccs_in[cc_idx].ap()], outs=[ccs_out[cc_idx].ap()]).then_inc(sCC.h, 1)
        em("gpsimd", cf)
        return sCC.inc(1)

    def cc_bcast(cc_idx, ev_cc):
        def bf(gpsimd, cc_idx=cc_idx, ev_cc=ev_cc):
            gpsimd.wait_ge(sCC.h, ev_cc)
            gpsimd.dma_start(
                out=s_bc[cc_idx][:, :],
                in_=ccs_out[cc_idx].ap().rearrange("r c -> (r c)")
                .rearrange("(a f) -> a f", a=1).partition_broadcast(128)
            ).then_inc(sGD.h, 16)
        em("gpsimd", bf)
        return sGD.inc(16)

    def scale_from_slot(cc_idx, slot, s_tile, rs_tile, ev_bc):
        _n1 = sVE.n + 1

        def vf(vector, cc_idx=cc_idx, slot=slot, ev_bc=ev_bc, _n1=_n1):
            vector.wait_ge(sGD.h, ev_bc)
            vector.wait_ge(sVE.h, ev_consts)
            vector.tensor_reduce(s_tmp[:, 0:1],
                                 s_bc[cc_idx][:, slot:slot + 113:16],
                                 axis=AX.X, op=ALU.max).then_inc(sVE.h, 1)
            vector.wait_ge(sVE.h, _n1)
            vector.tensor_scalar(s_tile[:, :], s_tmp[:, 0:1], r32767[:, 0:1],
                                 1e-12, op0=ALU.mult, op1=ALU.max).then_inc(sVE.h, 1)
            vector.wait_ge(sVE.h, _n1 + 1)
            vector.reciprocal(rs_tile[:, :], s_tile[:, :]).then_inc(sVE.h, 1)
        em("vector", vf)
        sVE.inc(2)
        return sVE.inc(1)

    ev_in2 = lmax_emit(xm32[:, :], 0, 0, sVE, ev_p2_all)
    ev_cc2 = cc_run(0, ev_in2)
    ev_bc2 = cc_bcast(0, ev_cc2)
    ev_s1 = scale_from_slot(0, 0, s_s1, s_rs1, ev_bc2)

    ctxRB = ExitStack()
    sbRB = mk(ctxRB)
    rb_in = [sbRB(f"rb_in{i}", [128, TCH], F32) for i in range(2)]
    rb_ig = sbRB("rb_ig", [128, TCH], F32)
    rb_t1 = sbRB("rb_t1", [128, TCH], F32)
    rb_t2 = sbRB("rb_t2", [128, TCH], F32)
    rb_o = [sbRB(f"rb_o{i}", [128, TCH], F32) for i in range(2)]
    rb2_up = rb_in
    rb2_g2 = [sbRB(f"rb2g{i}", [128, TCH], F32) for i in range(2)]
    rb2_i2 = [sbRB(f"rb2_i2{i}", [128, TCH], F32) for i in range(2)]
    rb2_i4 = [sbRB(f"rb2_i4{i}", [128, TCH], F32) for i in range(2)]

    ctxF1 = ExitStack()
    sbF = mk(ctxF1)
    q16 = sbF("q16", [128, D], F32)
    qh_fm = [sbF(f"qh{k}", [128, T], BF16) for k in range(KD)]
    ql_fm = [sbF(f"ql{k}", [128, T], BF16) for k in range(KD)]
    wt_hi = [sbF(f"wt_hi{i}", [128, D], BF16) for i in range(2)]
    wt_lo = [sbF(f"wt_lo{i}", [128, D], BF16) for i in range(2)]
    ev_tile = [sbF(f"ev_tile{i}", [128, TCH], F32) for i in range(2)]

    q_ptfree = [None, None]
    q16_done = {}
    for tcm in range(NTM):
        xb = x1c[tcm % 2]

        def x1rd(sync, tcm=tcm, xb=xb):
            if tcm == 0:
                sync.wait_ge(sST.h, ev_x1_sp_all)
            if tcm >= 2:
                sync.wait_ge(sVE.h, q16_done[tcm - 2])
            sync.dma_start(out=xb[:, :], in_=x1_dram.ap()[tcm]).then_inc(sLD.h, 16)
        em("sync", x1rd)
        ev_ld = sLD.inc(16)

        def q16v(vector, tcm=tcm, xb=xb, ev_ld=ev_ld):
            vector.wait_ge(sLD.h, ev_ld)
            if tcm == 0:
                vector.wait_ge(sVE.h, ev_s1)
            if tcm >= 1:
                vector.wait_ge(sMM.h, q_mm_done[tcm - 1])
            vector.tensor_scalar(q16[:, :], xb[:, :], s_rs1[:, 0:1], MAGIC,
                                 op0=ALU.mult, op1=ALU.add).then_inc(sVE.h, 1)
        if tcm == 0:
            q16s_done = {}
            q_mm_done = {}
        em("vector", q16v)
        ev_q16a = sVE.inc(1)

        def q16s(scalar, ev_q16a=ev_q16a):
            scalar.wait_ge(sVE.h, ev_q16a)
            scalar.activation(q16[:, :], q16[:, :], ACT.Copy,
                              bias=-MAGIC, scale=1.0).then_inc(sSC.h, 1)
        em("scalar", q16s)
        q16s_done[tcm] = sSC.inc(1)

        for j in range(KD):
            ps = pTR[j % 2]
            pf = q_ptfree[j % 2]

            def qtt(tensor, tcm=tcm, j=j, ps=ps, pf=pf):
                tensor.wait_ge(sSC.h, q16s_done[tcm])
                if pf is not None:
                    tensor.wait_ge(sVE.h, pf)
                tensor.transpose(ps[0:128, 0:128], q16[:, j * 128:(j + 1) * 128],
                                 ident[:, :]).then_inc(sMM.h, 1)
            em("tensor", qtt)
            ev_mm = sMM.inc(1)

            def qte(vector, tcm=tcm, j=j, ps=ps, ev_mm=ev_mm):
                vector.wait_ge(sMM.h, ev_mm)
                vector.tensor_copy(qh_fm[j][:, tcm * TMC:(tcm + 1) * TMC],
                                   ps[0:128, 0:128])
                vector.tensor_tensor(ql_fm[j][:, tcm * TMC:(tcm + 1) * TMC],
                                     ps[0:128, 0:128],
                                     qh_fm[j][:, tcm * TMC:(tcm + 1) * TMC],
                                     op=ALU.subtract).then_inc(sVE.h, 1)
            em("vector", qte)
            q_ptfree[j % 2] = sVE.inc(1)
        q_mm_done[tcm] = sMM.n
        q16_done[tcm] = sVE.n
    ev_q_all = sVE.n

    gu_wfree = [None, None]
    gu_psfree = [None, None, None, None]
    gu_ebfree = [None, None]

    def emit_gu_phase(mat, dst_dram, max_cols, tail_cb=None):
        ev_wld_map = {}

        def emit_wld(oc, mat=mat):
            wb = oc % 2
            rank, j = oc % 8, oc // 8
            wfree = gu_wfree[wb]

            def wld(sync, mat=mat, wb=wb, rank=rank, j=j, wfree=wfree):
                if mat == 0 and j == 0 and rank < 2:
                    sync.wait_ge(sCC.h, ev_cc0)
                if wfree is not None:
                    sync.wait_ge(sMM.h, wfree)
                off_hi = _off_gu(mat, j, 0)
                off_lo = _off_gu(mat, j, 1)
                sync.dma_start(
                    out=wt_hi[wb][:, :],
                    in_=cc0_out.ap()[rank, off_hi:off_hi + GU_TILE]
                    .rearrange("(p f) -> p f", p=128)).then_inc(sWT[wb].h, 16)
                sync.dma_start(
                    out=wt_lo[wb][:, :],
                    in_=cc0_out.ap()[rank, off_lo:off_lo + GU_TILE]
                    .rearrange("(p f) -> p f", p=128)).then_inc(sWT[wb].h, 16)
            em("sync", wld)
            ev_wld_map[oc] = (wb, sWT[wb].inc(32))

        emit_wld(0)
        for oc in range(OI):
            wb = oc % 2
            if oc + 1 < OI:
                emit_wld(oc + 1)
            ev_wld = ev_wld_map[oc]

            for tc in range(NTC):
                pb = (oc * NTC + tc) % 4
                ps = pMM[pb]
                idx = oc * NTC + tc
                psfree = gu_psfree[pb]

                def gmm(tensor, mat=mat, oc=oc, tc=tc, wb=wb, ps=ps,
                        ev_wld=ev_wld, psfree=psfree):
                    if mat == 0 and oc == 0 and tc == 0:
                        tensor.wait_ge(sVE.h, ev_q_all)
                    tensor.wait_ge(sWT[ev_wld[0]].h, ev_wld[1])
                    if psfree is not None:
                        tensor.wait_ge(sSC.h, psfree)
                    mm = None
                    for k in range(KD):
                        tensor.matmul(ps[:, :], wt_hi[wb][:, k * 128:(k + 1) * 128],
                                      qh_fm[k][:, tc * TCH:(tc + 1) * TCH],
                                      start=(k == 0), stop=False)
                        tensor.matmul(ps[:, :], wt_hi[wb][:, k * 128:(k + 1) * 128],
                                      ql_fm[k][:, tc * TCH:(tc + 1) * TCH],
                                      start=False, stop=False)
                        mm = tensor.matmul(ps[:, :], wt_lo[wb][:, k * 128:(k + 1) * 128],
                                           qh_fm[k][:, tc * TCH:(tc + 1) * TCH],
                                           start=False, stop=(k == KD - 1))
                    mm.then_inc(sMM.h, 1)
                em("tensor", gmm)
                ev_mm = sMM.inc(1)
                if tc == NTC - 1:
                    gu_wfree[wb] = ev_mm

                eb = ev_tile[idx % 2]
                ebfree = gu_ebfree[idx % 2]

                def gev(scalar, ps=ps, eb=eb, ev_mm=ev_mm, ebfree=ebfree):
                    scalar.wait_ge(sMM.h, ev_mm)
                    if ebfree is not None:
                        scalar.wait_ge(sST.h, ebfree[0])
                        scalar.wait_ge(sVE.h, ebfree[1])
                    scalar.activation(eb[:, :], ps[:, :], ACT.Copy,
                                      scale=s_s1[:, 0:1]).then_inc(sSC.h, 1)
                em("scalar", gev)
                ev_ev = sSC.inc(1)
                gu_psfree[pb] = ev_ev

                def gsp(scalar, oc=oc, tc=tc, eb=eb, dst=dst_dram):
                    scalar.dma_start(out=dst.ap()[oc, tc], in_=eb[:, :]).then_inc(sST.h, 16)
                em("scalar", gsp)
                ev_sp = sST.inc(16)

                def gmx(vector, idx=idx, eb=eb, mc=max_cols, ev_ev=ev_ev):
                    vector.wait_ge(sSC.h, ev_ev)
                    vector.tensor_reduce(mc[:, idx:idx + 1], eb[:, :], axis=AX.X,
                                         op=ALU.max,
                                         apply_absolute_value=True).then_inc(sVE.h, 1)
                em("vector", gmx)
                ev_mx = sVE.inc(1)
                gu_ebfree[idx % 2] = (ev_sp, ev_mx)
            if tail_cb is not None:
                tail_cb(oc)
        return sST.n, sVE.n

    ev_gate_sp, ev_gate_mx = emit_gu_phase(0, gate_dram, gmax)

    ev_in3 = lmax_emit(gmax[:, :], 1, 0, sVE, ev_gate_mx)
    ev_cc3 = cc_run(1, ev_in3)
    ev_bc3 = cc_bcast(1, ev_cc3)
    ev_s3 = scale_from_slot(1, 0, s_s3, s_rs3, ev_bc3)

    def s3m_f(vector):
        vector.wait_ge(sVE.h, ev_s3)
        vector.tensor_scalar(s_s3m[:, :], s_s3[:, :], INV65536, None,
                             op0=ALU.mult).then_inc(sVE.h, 1)
    em("vector", s3m_f)
    ev_s3m = sVE.inc(1)

    r1_done = {}
    r1_s_done = {}
    r1_sp_done = {}

    def emit_r1(idx):
        oc, tc = idx // NTC, idx % NTC
        ib = rb_in[idx % 2]
        ob = rb_o[idx % 2]

        def r1ld(sync, oc=oc, tc=tc, ib=ib, idx=idx):
            if idx == 0:
                sync.wait_ge(sST.h, ev_gate_sp)
            if idx >= 2:
                sync.wait_ge(sVE.h, r1_done[idx - 2])
            sync.dma_start(out=ib[:, :], in_=gate_dram.ap()[oc, tc]).then_inc(sLD.h, 16)
        em("sync", r1ld)
        ev_ld = sLD.inc(16)

        def r1v(vector, ib=ib, idx=idx, ev_ld=ev_ld):
            vector.wait_ge(sLD.h, ev_ld)
            if idx == 0:
                vector.wait_ge(sVE.h, ev_s3m)
            if idx >= 1:
                vector.wait_ge(sSC.h, r1_s_done[idx - 1])
            vector.tensor_scalar(rb_t1[:, :], ib[:, :], s_rs3[:, 0:1], MAGIC,
                                 op0=ALU.mult, op1=ALU.add).then_inc(sVE.h, 1)
        em("vector", r1v)
        ev_a = sVE.inc(1)

        def r1s(scalar, ev_a=ev_a, idx=idx):
            scalar.wait_ge(sVE.h, ev_a)
            if idx >= 1:
                scalar.wait_ge(sVE.h, r1_done.get(idx - 1, 0))
            scalar.activation(rb_ig[:, :], rb_t1[:, :], ACT.Copy,
                              bias=-MAGIC, scale=1.0)
            scalar.activation(rb_t2[:, :], rb_ig[:, :], ACT.Sigmoid,
                              bias=0.0, scale=s_s3[:, 0:1])
            scalar.activation(rb_t2[:, :], rb_t2[:, :], ACT.Copy,
                              bias=MAGIC, scale=cst[:, 2:3]).then_inc(sSC.h, 1)
        em("scalar", r1s)
        r1_s_done[idx] = sSC.inc(1)

        def r1v2(vector, ob=ob, idx=idx):
            vector.wait_ge(sSC.h, r1_s_done[idx])
            if idx >= 2:
                vector.wait_ge(sGD.h, r1_sp_done[idx - 2])
            vector.tensor_scalar(rb_t2[:, :], rb_t2[:, :], MAGIC, 65535.0,
                                 op0=ALU.subtract, op1=ALU.min)
            vector.scalar_tensor_tensor(ob[:, :], rb_ig[:, :], s_s3m[:, 0:1],
                                        rb_t2[:, :], op0=ALU.mult, op1=ALU.mult)
            vector.tensor_reduce(g2max[:, idx:idx + 1], ob[:, :], axis=AX.X,
                                 op=ALU.max, apply_absolute_value=True).then_inc(sVE.h, 1)
        em("vector", r1v2)
        r1_done[idx] = sVE.inc(1)

        def r1sp(gpsimd, oc=oc, tc=tc, ob=ob, idx=idx):
            gpsimd.wait_ge(sVE.h, r1_done[idx])
            gpsimd.dma_start(out=g2_dram.ap()[oc, tc], in_=ob[:, :]).then_inc(sGD.h, 16)
        em("gpsimd", r1sp)
        r1_sp_done[idx] = sGD.inc(16)

    ev_up_sp, ev_up_mx = emit_gu_phase(1, up_dram, umax)
    for _idx in range(OI * NTC):
        emit_r1(_idx)
    ev_r1_all = sVE.n
    ev_r1_sp = sGD.n

    lmax_emit(umax[:, :], 2, 0, sVE, ev_up_mx)
    ev_in4 = lmax_emit(g2max[:, :], 2, 1, sVE, ev_r1_all)
    ev_cc4 = cc_run(2, ev_in4)
    ev_bc4 = cc_bcast(2, ev_cc4)
    ev_s2 = scale_from_slot(2, 0, s_s2, s_rs2, ev_bc4)
    ev_s4 = scale_from_slot(2, 1, s_s4, s_rs4, ev_bc4)

    def s42_f(vector):
        vector.wait_ge(sVE.h, max(ev_s2, ev_s4))
        vector.tensor_tensor(s_s42[:, :], s_s4[:, :], s_s2[:, :],
                             op=ALU.mult).then_inc(sVE.h, 1)
    em("vector", s42_f)
    ev_s42 = sVE.inc(1)

    ctxF1.close()

    ctxWD = ExitStack()
    sbW = mk(ctxWD)
    wdh = [sbW(f"wdh{ic}", [128, D], BF16) for ic in range(OI)]
    wdl = [sbW(f"wdl{ic}", [128, D], BF16) for ic in range(OI)]
    oin = [sbW(f"oin{i}", [128, I3], F32) for i in range(2)]
    ohh = [sbW(f"ohh{i}", [128, I3], BF16) for i in range(2)]
    oll = [sbW(f"oll{i}", [128, I3], BF16) for i in range(2)]
    yx1 = [sbW("yx10", [128, D], F32)]
    y2 = sbW("y2", [128, D], F32)

    def wdld(sync):
        sync.wait_ge(sCC.h, ev_cc0)
        sync.wait_ge(sST.h, ev_up_sp)
        sync.wait_ge(sVE.h, ev_up_mx)
        for ic in range(OI):
            jj, j = ic // 8, ic % 8
            for r in range(8):
                for part, dst in ((0, wdh), (1, wdl)):
                    base = _off_dn(part) + jj * (128 * 1024)
                    sync.dma_start(
                        out=dst[ic][:, r * 128:(r + 1) * 128],
                        in_=cc0_out.ap()[r, base:base + 128 * 1024]
                        .rearrange("(p j o) -> p j o", j=8, o=128)[:, j:j + 1, :]
                    ).then_inc(sLD.h, 16)
    em("sync", wdld)
    ev_wd_ld = sLD.inc(16 * OI * 8 * 2)

    r2_done = {}
    r2_s_done = {}
    r2_v_done = {}
    r2_g_done = {}
    r2_sp_done = {}
    for idx in range(OI * NTC):
        oc, tc = idx // NTC, idx % NTC
        ub = rb2_up[idx % 2]
        gb = rb2_g2[idx % 2]
        ob = rb_o[idx % 2]
        i2 = rb2_i2[idx % 2]
        i4 = rb2_i4[idx % 2]

        def r2ld(sync, oc=oc, tc=tc, ub=ub, gb=gb, idx=idx):
            if idx == 0:
                sync.wait_ge(sST.h, ev_up_sp)
                sync.wait_ge(sGD.h, ev_r1_sp)
            if idx >= 2:
                sync.wait_ge(sVE.h, r2_v_done[idx - 2])
                sync.wait_ge(sSC.h, r2_s_done[idx - 2])
            sync.dma_start(out=ub[:, :], in_=up_dram.ap()[oc, tc]).then_inc(sLD.h, 16)
            sync.dma_start(out=gb[:, :], in_=g2_dram.ap()[oc, tc]).then_inc(sLD.h, 16)
        em("sync", r2ld)
        ev_ld = sLD.inc(32)

        def r2v(vector, ub=ub, i2=i2, idx=idx, ev_ld=ev_ld):
            vector.wait_ge(sLD.h, ev_ld)
            if idx == 0:
                vector.wait_ge(sVE.h, ev_s42)
            if idx >= 2:
                vector.wait_ge(sGP.h, r2_g_done[idx - 2])
            vector.tensor_scalar(i2[:, :], ub[:, :], s_rs2[:, 0:1], MAGIC,
                                 op0=ALU.mult, op1=ALU.add)
            vector.tensor_scalar(i2[:, :], i2[:, :], MAGIC, None,
                                 op0=ALU.subtract).then_inc(sVE.h, 1)
        em("vector", r2v)
        r2_v_done[idx] = sVE.inc(1)

        def r2s(scalar, gb=gb, i4=i4, idx=idx, ev_ld=ev_ld):
            scalar.wait_ge(sLD.h, ev_ld)
            if idx >= 2:
                scalar.wait_ge(sGP.h, r2_g_done[idx - 2])
            scalar.activation(i4[:, :], gb[:, :], ACT.Copy,
                              bias=MAGIC, scale=s_rs4[:, 0:1])
            scalar.activation(i4[:, :], i4[:, :], ACT.Copy,
                              bias=-MAGIC, scale=1.0)
            scalar.activation(i4[:, :], i4[:, :], ACT.Copy,
                              bias=0.0, scale=s_s42[:, 0:1]).then_inc(sSC.h, 1)
        em("scalar", r2s)
        r2_s_done[idx] = sSC.inc(1)

        def r2g(gpsimd, ob=ob, oc=oc, tc=tc, i2=i2, i4=i4, idx=idx,
                self_ev=sGP.n + 1):
            gpsimd.wait_ge(sSC.h, r2_s_done[idx])
            gpsimd.wait_ge(sVE.h, r2_v_done[idx])
            if idx >= 2:
                gpsimd.wait_ge(sVE.h, r2_done[idx - 2])
                gpsimd.wait_ge(sGD.h, r2_sp_done[idx - 2])
            gpsimd.tensor_tensor(ob[:, :], i4[:, :], i2[:, :],
                                 op=ALU.mult).then_inc(sGP.h, 1)
            gpsimd.wait_ge(sGP.h, self_ev)
            gpsimd.dma_start(out=or_dram.ap()[oc, tc], in_=ob[:, :]).then_inc(sGD.h, 16)
        em("gpsimd", r2g)
        r2_g_done[idx] = sGP.inc(1)
        r2_sp_done[idx] = sGD.inc(16)

        def r2vr(vector, ob=ob, idx=idx):
            vector.wait_ge(sGP.h, r2_g_done[idx])
            vector.tensor_reduce(omax[:, idx:idx + 1], ob[:, :], axis=AX.X,
                                 op=ALU.max, apply_absolute_value=True).then_inc(sVE.h, 1)
        em("vector", r2vr)
        r2_done[idx] = sVE.inc(1)
    ev_r2_all = sVE.n
    ev_r2_sp = sGD.n

    ev_in5 = lmax_emit(omax[:, :], 3, 0, sVE, ev_r2_all)
    ev_cc5 = cc_run(3, ev_in5)
    ev_bc5 = cc_bcast(3, ev_cc5)
    ev_s5 = scale_from_slot(3, 0, s_s5, s_rs5, ev_bc5)

    d_psfree = [None, None, None, None]
    dq_done = {}
    out_sp = {}
    d_ev_a = {}
    for tt in range(NTM):
        oh = ohh[tt % 2]
        ol = oll[tt % 2]
        xr = yx1[0]
        tc512, sub = tt // 4, tt % 4

        ob_in = oin[tt % 2]

        def d_ld(sync, tt=tt, tc512=tc512, sub=sub, xr=xr, ob_in=ob_in):
            if tt == 0:
                sync.wait_ge(sGD.h, ev_r2_sp)
            if tt >= 2:
                sync.wait_ge(sVE.h, dq_done[tt - 2])
            if tt >= 1:
                sync.wait_ge(sVE.h, d_y2done[tt - 1])
            for ic in range(OI):
                sync.dma_start(
                    out=ob_in[:, ic * 128:(ic + 1) * 128],
                    in_=or_dram.ap()[ic, tc512, :, sub * 128:(sub + 1) * 128]
                ).then_inc(sLD.h, 16)
            sync.dma_start(out=xr[:, :], in_=x1_dram.ap()[tt]).then_inc(sLD.h, 16)
        em("sync", d_ld)
        ev_ld = sLD.inc(16 * (OI + 1))

        def d_v(vector, tt=tt, ev_ld=ev_ld, ob_in=ob_in):
            vector.wait_ge(sLD.h, ev_ld)
            if tt == 0:
                vector.wait_ge(sVE.h, ev_s5)
            vector.tensor_scalar(ob_in[:, :], ob_in[:, :], s_rs5[:, 0:1], MAGIC,
                                 op0=ALU.mult, op1=ALU.add).then_inc(sVE.h, 1)
        if tt == 0:
            d_mm_done = {}
            d_y2done = {}
        em("vector", d_v)
        ev_a = sVE.inc(1)
        d_ev_a[tt] = ev_a

        def d_s(scalar, ev_a=ev_a, ob_in=ob_in):
            scalar.wait_ge(sVE.h, ev_a)
            scalar.activation(ob_in[:, :], ob_in[:, :], ACT.Copy,
                              bias=-MAGIC, scale=1.0).then_inc(sSC.h, 1)
        em("scalar", d_s)
        ev_o16 = sSC.inc(1)

        def d_v2(vector, oh=oh, ol=ol, tt=tt, ev_o16=ev_o16, ob_in=ob_in):
            vector.wait_ge(sSC.h, ev_o16)
            if tt >= 2:
                vector.wait_ge(sMM.h, d_mm_done[tt - 2])
            vector.tensor_copy(oh[:, :], ob_in[:, :])
            vector.tensor_tensor(ol[:, :], ob_in[:, :], oh[:, :],
                                 op=ALU.subtract).then_inc(sVE.h, 1)
        em("vector", d_v2)
        ev_ohl = sVE.inc(1)
        dq_done[tt] = ev_ohl

        for half in range(2):
            pb = (tt * 2 + half) % 4
            ps = pMM[pb]
            psfree = d_psfree[pb]

            def dmm(tensor, tt=tt, half=half, ps=ps, ev_ohl=ev_ohl,
                    psfree=psfree, oh=oh, ol=ol):
                if tt == 0 and half == 0:
                    tensor.wait_ge(sLD.h, ev_wd_ld)
                tensor.wait_ge(sVE.h, ev_ohl)
                if psfree is not None:
                    tensor.wait_ge(sVE.h, psfree)
                mm = None
                for ic in range(OI):
                    tensor.matmul(ps[:, :], oh[:, ic * 128:(ic + 1) * 128],
                                  wdh[ic][:, half * TCH:(half + 1) * TCH],
                                  start=(ic == 0), stop=False)
                    tensor.matmul(ps[:, :], oh[:, ic * 128:(ic + 1) * 128],
                                  wdl[ic][:, half * TCH:(half + 1) * TCH],
                                  start=False, stop=False)
                    mm = tensor.matmul(ps[:, :], ol[:, ic * 128:(ic + 1) * 128],
                                       wdh[ic][:, half * TCH:(half + 1) * TCH],
                                       start=False, stop=(ic == OI - 1))
                mm.then_inc(sMM.h, 1)
            em("tensor", dmm)
            ev_mm = sMM.inc(1)
            if half == 1:
                d_mm_done[tt] = ev_mm

            def dev(vector, half=half, ps=ps, ev_mm=ev_mm, xr=xr, tt=tt):
                vector.wait_ge(sMM.h, ev_mm)
                if half == 0 and tt >= 1:
                    vector.wait_ge(sST.h, out_sp[tt - 1])
                vector.scalar_tensor_tensor(
                    y2[:, half * TCH:(half + 1) * TCH], ps[:, :], s_s5[:, 0:1],
                    xr[:, half * TCH:(half + 1) * TCH],
                    op0=ALU.mult, op1=ALU.add).then_inc(sVE.h, 1)
            em("vector", dev)
            d_psfree[pb] = sVE.inc(1)
        ev_y2 = sVE.n
        d_y2done[tt] = ev_y2

        _l1 = sVE.n + 1

        def f_v(vector, ev_y2=ev_y2, _l1=_l1):
            vector.wait_ge(sVE.h, ev_y2)
            vector.tensor_reduce(lnsum[:, :], y2[:, :], axis=AX.X,
                                 op=ALU.add).then_inc(sVE.h, 1)
            vector.wait_ge(sVE.h, _l1)
            vector.tensor_scalar(negmu[:, :], lnsum[:, :], -INV1024, None,
                                 op0=ALU.mult).then_inc(sVE.h, 1)
            vector.wait_ge(sVE.h, _l1 + 1)
            vector.tensor_scalar(y2[:, :], y2[:, :], negmu[:, 0:1], None,
                                 op0=ALU.add).then_inc(sVE.h, 1)
        em("vector", f_v)
        sVE.inc(2)
        ev_t3 = sVE.inc(1)

        def f_s(scalar, ev_t3=ev_t3):
            scalar.wait_ge(sVE.h, ev_t3)
            scalar.activation(sqscr[:, :], y2[:, :], ACT.Square,
                              accum_out=ssq[:, :])
            scalar.activation(lnsd[:, :], ssq[:, :], ACT.Sqrt,
                              bias=cst[:, 3:4], scale=cst[:, 6:7]).then_inc(sSC.h, 1)
        em("scalar", f_s)
        ev_sd3 = sSC.inc(1)

        _m1 = sVE.n + 1

        def f_v2(vector, tt=tt, ev_sd3=ev_sd3, _m1=_m1):
            vector.wait_ge(sSC.h, ev_sd3)
            vector.reciprocal(lnr[:, :], lnsd[:, :]).then_inc(sVE.h, 1)
            vector.wait_ge(sVE.h, _m1)
            vector.tensor_scalar(y2[:, :], y2[:, :], lnr[:, 0:1], None,
                                 op0=ALU.mult).then_inc(sVE.h, 1)
        em("vector", f_v2)
        sVE.inc(1)
        ev_oc = sVE.inc(1)

        def f_st(scalar, tt=tt, ev_oc=ev_oc):
            scalar.wait_ge(sVE.h, ev_oc)
            scalar.dma_start(out=out[tt * TMC:(tt + 1) * TMC, :],
                             in_=y2[:, :]).then_inc(sST.h, 16)
        em("scalar", f_st)
        out_sp[tt] = sST.inc(16)
    ev_out_all = sST.n

    def fin(sync):
        sync.wait_ge(sST.h, ev_out_all)
    em("sync", fin)

    ctxWD.close()
    ctxRB.close()
    ctxM.close()

    with nc.Block() as block:
        @block.sync
        def _(sync):
            for fn in EQ["sync"]:
                fn(sync)

        @block.tensor
        def _(tensor):
            for fn in EQ["tensor"]:
                fn(tensor)

        @block.vector
        def _(vector):
            for fn in EQ["vector"]:
                fn(vector)

        @block.scalar
        def _(scalar):
            for fn in EQ["scalar"]:
                fn(scalar)

        @block.gpsimd
        def _(gpsimd):
            for fn in EQ["gpsimd"]:
                fn(gpsimd)

    ctxR.close()
    return nc


_NC_CACHE = None


def kernel(L0_attn_h, L0_attn_Wl, L0_attn_Wr, L0_attn_att_l, L0_attn_att_r,
           L0_attn_Wf, L0_attn_ln_gamma, L0_attn_ln_beta, L0_ffn_w_gate,
           L0_ffn_w_up, L0_ffn_w_down):
    global _NC_CACHE
    h = np.ascontiguousarray(np.asarray(L0_attn_h, dtype=np.float32))
    wr = np.ascontiguousarray(np.asarray(L0_attn_Wr, dtype=np.float32))
    wf = np.ascontiguousarray(np.asarray(L0_attn_Wf, dtype=np.float32))
    attr = np.ascontiguousarray(np.asarray(L0_attn_att_r, dtype=np.float32))
    gamma = np.asarray(L0_attn_ln_gamma, dtype=np.float32).reshape(1, D)
    beta = np.asarray(L0_attn_ln_beta, dtype=np.float32).reshape(1, D)
    wg = np.ascontiguousarray(np.asarray(L0_ffn_w_gate, dtype=np.float32))
    wu = np.ascontiguousarray(np.asarray(L0_ffn_w_up, dtype=np.float32))
    wd = np.ascontiguousarray(np.asarray(L0_ffn_w_down, dtype=np.float32))
    ident = np.eye(128, dtype=np.float32)

    if _NC_CACHE is None:
        _NC_CACHE = build()
    nc = _NC_CACHE

    in_maps = []
    for c in range(N_CORES):
        b, halfseq = c // 2, c % 2
        tok = slice(halfseq * T, (halfseq + 1) * T)
        h_tm = np.ascontiguousarray(h[b, tok, :])
        h_fm = np.ascontiguousarray(h_tm.T)
        wg_sh = np.ascontiguousarray(
            np.concatenate([wg[128 * (c + 8 * j):128 * (c + 8 * j) + 128]
                            for j in range(3)], axis=0))
        wu_sh = np.ascontiguousarray(
            np.concatenate([wu[128 * (c + 8 * j):128 * (c + 8 * j) + 128]
                            for j in range(3)], axis=0))
        wd_sh = np.ascontiguousarray(wd[128 * c:128 * (c + 1)])
        bsel_np = np.zeros((4, 1), dtype=np.float32)
        bsel_np[b, 0] = 1.0
        in_maps.append({
            "h_fm": h_fm, "h_tm": h_tm, "wr_raw": wr, "wf_raw": wf,
            "attr": attr, "gamma": gamma, "beta": beta,
            "wg_sh": wg_sh, "wu_sh": wu_sh, "wd_sh": wd_sh,
            "bsel": bsel_np, "ident_in": ident,
        })

    res = run_bass_kernel_spmd(nc, in_maps, core_ids=list(range(N_CORES)))
    full = np.empty((B, N, D), dtype=np.float32)
    for c in range(N_CORES):
        b, halfseq = c // 2, c % 2
        full[b, halfseq * T:(halfseq + 1) * T, :] = res.results[c]["out"]
    return full


# version 15
# speedup vs baseline: 1.2360x; 1.1803x over previous
import numpy as np
from contextlib import ExitStack

import concourse.bass as bass
import concourse.mybir as mybir
from concourse.bass_utils import run_bass_kernel_spmd

F32 = mybir.dt.float32
BF16 = mybir.dt.bfloat16
ACT = mybir.ActivationFunctionType
ALU = mybir.AluOpType
AX = mybir.AxisListType

MAGIC = 12582912.0
INV1024 = 0.0009765625
INV65536 = 0.0000152587890625
N_CORES = 8
B, N, D, H = 4, 8192, 1024, 8
I3 = 3072
T = 4096
TH = 2048
KD = 8
TCH = 512
NTC = T // TCH
NTCH = TH // TCH
TMC = 128
NTM = T // TMC
OI = 24
LN_EPS = 1e-5

GU_TILE = 128 * 1024
DN_TILE = 128 * 3072
PACK = 12 * GU_TILE + 2 * DN_TILE


def _off_gu(mat, j, part):
    return ((mat * 3 + j) * 2 + part) * GU_TILE


def _off_dn(part):
    return 12 * GU_TILE + part * DN_TILE


class Sem:
    def __init__(self, h):
        self.h = h
        self.n = 0

    def inc(self, k=1):
        self.n += k
        return self.n


def build(detect_races=True):
    nc = bass.Bass("TRN2", target_bir_lowering=False,
                   detect_race_conditions=detect_races)
    dt = nc.dram_tensor
    h_fm = dt("h_fm", [D, T], F32, kind="ExternalInput").ap()
    h_tm = dt("h_tm", [T, D], F32, kind="ExternalInput").ap()
    wr_raw = dt("wr_raw", [D, D], F32, kind="ExternalInput").ap()
    wf_raw = dt("wf_raw", [D, D], F32, kind="ExternalInput").ap()
    attr = dt("attr", [1, 128], F32, kind="ExternalInput").ap()
    gamma = dt("gamma", [1, D], F32, kind="ExternalInput").ap()
    beta = dt("beta", [1, D], F32, kind="ExternalInput").ap()
    wg_sh = dt("wg_sh", [3 * 128, D], F32, kind="ExternalInput").ap()
    wu_sh = dt("wu_sh", [3 * 128, D], F32, kind="ExternalInput").ap()
    wd_sh = dt("wd_sh", [128, I3], F32, kind="ExternalInput").ap()
    bsel = dt("bsel", [4, 1], F32, kind="ExternalInput").ap()
    ident_in = dt("ident_in", [128, 128], F32, kind="ExternalInput").ap()
    out = dt("out", [T, D], F32, kind="ExternalOutput").ap()

    cc0_in = dt("cc0_in", [PACK], BF16)
    cc0_out = dt("cc0_out", [N_CORES, PACK], BF16)
    cc1_in = dt("cc1_in", [H, 130], F32)
    cc1_out = dt("cc1_out", [N_CORES, H, 130], F32)
    ccs_in = [dt(f"ccs_in{i}", [1, 16], F32) for i in range(4)]
    ccs_out = [dt(f"ccs_out{i}", [N_CORES, 16], F32) for i in range(4)]
    e_dram = dt("e_dram", [H, T], F32)
    fh_dram = dt("fh_dram", [1, D], F32)
    x1_dram = dt("x1_dram", [NTM, TMC, D], F32)
    gate_dram = dt("gate_dram", [OI, NTC, 128, TCH], F32)
    up_dram = dt("up_dram", [OI, NTC, 128, TCH], F32)
    g2_dram = dt("g2_dram", [OI, NTC, 128, TCH], F32)
    or_dram = dt("or_dram", [OI, NTC, 128, TCH], F32)

    ctxR = ExitStack()

    def mk(stack):
        sbf = lambda *a, **k: stack.enter_context(nc.sbuf_tensor(*a, **k))
        return sbf

    sbR = mk(ctxR)
    psumR = lambda *a, **k: ctxR.enter_context(nc.psum_tensor(*a, **k))
    mksem = lambda n: Sem(ctxR.enter_context(nc.semaphore(n)))

    sLD = mksem("sLD")
    sST = mksem("sST")
    sGD = mksem("sGD")
    sCC = mksem("sCC")
    sMM = mksem("sMM")
    sVE = mksem("sVE")
    sSC = mksem("sSC")
    sGP = mksem("sGP")
    sWT = [mksem("sWT0"), mksem("sWT1")]
    sR2 = mksem("sR2")

    cst = sbR("cst", [128, 8], F32)
    r7 = sbR("r7", [128, 1], F32)
    r32767 = sbR("r32767", [128, 1], F32)
    ident = sbR("ident", [128, 128], F32)
    attr_c = sbR("attr_c", [128, 1], F32)
    bsel_c = sbR("bsel_c", [4, 1], F32)
    s_s1 = sbR("s_s1", [128, 1], F32)
    s_rs1 = sbR("s_rs1", [128, 1], F32)
    s_s3 = sbR("s_s3", [128, 1], F32)
    s_rs3 = sbR("s_rs3", [128, 1], F32)
    s_s3m = sbR("s_s3m", [128, 1], F32)
    s_s2 = sbR("s_s2", [128, 1], F32)
    s_rs2 = sbR("s_rs2", [128, 1], F32)
    s_s4 = sbR("s_s4", [128, 1], F32)
    s_rs4 = sbR("s_rs4", [128, 1], F32)
    s_s42 = sbR("s_s42", [128, 1], F32)
    s_s5 = sbR("s_s5", [128, 1], F32)
    s_rs5 = sbR("s_rs5", [128, 1], F32)
    s_tmp = sbR("s_tmp", [128, 2], F32)
    s_bc = [sbR(f"s_bc{i}", [128, 128], F32) for i in range(4)]
    numh = [sbR(f"numh{i}", [128, H], F32) for i in range(2)]
    numT_h = [sbR(f"numTh{i}", [H, 128], F32) for i in range(2)]
    m_all = sbR("m_all", [1, 16], F32)
    d_all = sbR("d_all", [1, 16], F32)
    sr_negm = sbR("sr_negm", [1, 1], F32)
    mm8 = sbR("mm8", [1, 8], F32)
    t16 = sbR("t16", [1, 16], F32)
    w01 = sbR("w01", [1, 16], F32)
    dmrg = sbR("dmrg", [1, 8], F32)
    t8a = sbR("t8a", [1, 8], F32)
    mcol = sbR("mcol", [8, 1], F32)
    dcol = sbR("dcol", [8, 1], F32)
    w0c = sbR("w0c", [8, 1], F32)
    w1c = sbR("w1c", [8, 1], F32)
    numT = sbR("numT", [H, 128], F32)
    pack_sb = sbR("pack_sb", [H, 130], F32)
    gA = sbR("gA", [32, 130], F32)
    gB = sbR("gB", [32, 130], F32)
    mx32 = sbR("mx32", [32, 1], F32)
    ngm32 = sbR("ngm32", [32, 1], F32)
    wA32 = sbR("wA32", [32, 1], F32)
    wB32 = sbR("wB32", [32, 1], F32)
    t129 = sbR("t129", [32, 129], F32)
    cn129 = sbR("cn129", [32, 129], F32)
    dcb = sbR("dcb", [32, 1], F32)
    rdcb = sbR("rdcb", [32, 1], F32)
    ctx32 = sbR("ctx32", [32, 128], F32)
    ctxT = sbR("ctxT", [128, 32], F32)
    ctxT_h = sbR("ctxT_h", [128, 32], BF16)
    ctxT_l = sbR("ctxT_l", [128, 32], BF16)
    lnsum = sbR("lnsum", [128, 1], F32)
    negmu = sbR("negmu", [128, 1], F32)
    ssq = sbR("ssq", [128, 1], F32)
    lnsd = sbR("lnsd", [128, 1], F32)
    lnr = sbR("lnr", [128, 1], F32)
    sqscr = sbR("sqscr", [128, D], F32)
    xm32 = sbR("xm32", [128, NTM], F32)
    gmax = sbR("gmax", [128, OI * NTC], F32)
    umax = sbR("umax", [128, OI * NTC], F32)
    g2max = sbR("g2max", [128, OI * NTC], F32)
    omax = sbR("omax", [128, OI * NTC], F32)
    red1 = sbR("red1", [128, 1], F32)
    redT = sbR("redT", [1, 128], F32)
    red0 = sbR("red0", [1, 1], F32)
    zz16 = sbR("zz16", [1, 16], F32)

    nc.const_aps.register(F32, MAGIC, cst[:, 0:1])
    nc.const_aps.register(F32, -MAGIC, cst[:, 1:2])

    pMM = [psumR(f"pMM{i}", [128, TCH], F32) for i in range(4)]
    pTR = [psumR(f"pTR{i}", [128, TCH], F32) for i in range(2)]
    pSR = pTR[0][0:1, 0:TCH]
    pSEL = pTR[1][0:1, 0:TCH]
    pFH = [pMM[0][0:4, 0:TCH], pMM[1][0:4, 0:TCH]]

    EQ = {k: [] for k in ("sync", "tensor", "vector", "scalar", "gpsimd")}

    def em(eng, fn):
        EQ[eng].append(fn)

    def p0(sync):
        sync.dma_start(out=ident[:, :], in_=ident_in[:, :]).then_inc(sLD.h, 16)
        sync.dma_start(out=attr_c[:, 0:1],
                       in_=attr.rearrange("a f -> f a")).then_inc(sLD.h, 16)
        sync.dma_start(out=bsel_c[:, :], in_=bsel[:, :]).then_inc(sLD.h, 16)
    em("sync", p0)
    sLD.inc(16 * 3)

    def p0v(vector):
        vector.memset(cst[:, 0:1], MAGIC).then_inc(sVE.h, 1)
        vector.memset(cst[:, 1:2], -MAGIC).then_inc(sVE.h, 1)
        vector.memset(cst[:, 2:3], 65536.0).then_inc(sVE.h, 1)
        vector.memset(cst[:, 3:4], LN_EPS).then_inc(sVE.h, 1)
        vector.memset(cst[:, 4:5], 7.0).then_inc(sVE.h, 1)
        vector.memset(cst[:, 5:6], 32767.0).then_inc(sVE.h, 1)
        vector.memset(cst[:, 6:7], INV1024).then_inc(sVE.h, 1)
        vector.wait_ge(sVE.h, 7)
        vector.reciprocal(r7[:, :], cst[:, 4:5])
        vector.reciprocal(r32767[:, :], cst[:, 5:6]).then_inc(sVE.h, 1)
    em("vector", p0v)
    sVE.inc(7)
    ev_consts = sVE.inc(1)

    def p0z(vector):
        vector.memset(zz16[:, :], 0.0).then_inc(sVE.h, 1)
    em("vector", p0z)
    ev_zz = sVE.inc(1)

    def p0zd(gpsimd):
        gpsimd.wait_ge(sVE.h, ev_zz)
        for i in range(4):
            gpsimd.dma_start(out=ccs_in[i][:, :], in_=zz16[:, :]).then_inc(sGD.h, 16)
    em("gpsimd", p0zd)
    ev_ccz = sGD.inc(64)

    ctxA = ExitStack()
    sbA = mk(ctxA)
    wr_stage = sbA("wr_stage", [128, D], F32)
    sr0 = sbA("sr0", [1, TH], F32)
    fh4 = sbA("fh4", [4, D], F32)
    fh1 = sbA("fh1", [1, D], F32)
    wrt_hi = [sbA(f"wrt_hi{k}", [128, D], BF16) for k in range(KD)]
    wrt_lo = [sbA(f"wrt_lo{k}", [128, D], BF16) for k in range(KD)]
    hst = [sbA("hst0", [128, KD * TCH], F32)]
    hh = sbA("hh", [128, KD * TCH], BF16)
    hl = sbA("hl", [128, KD * TCH], BF16)
    fr_sb = [sbA(f"fr{k}", [128, TH], F32) for k in range(KD)]
    lf_sb = [sbA(f"lf_sb{i}", [128, TCH], F32) for i in range(2)]
    ebc = sbA("ebc", [128, TH], F32)

    pt_free = [None, None]
    wr_v_done = {}
    for i in range(KD):
        def ldf(sync, i=i):
            if i >= 1:
                sync.wait_ge(sVE.h, wr_v_done[i - 1])
            sync.dma_start(out=wr_stage[:, :],
                           in_=wr_raw[i * 128:(i + 1) * 128, :]).then_inc(sLD.h, 16)
        em("sync", ldf)
        ev_ld = sLD.inc(16)
        for j in range(KD):
            ps = pTR[j % 2]
            pf = pt_free[j % 2]

            def tfn(tensor, i=i, j=j, ps=ps, ev_ld=ev_ld, pf=pf):
                tensor.wait_ge(sLD.h, ev_ld)
                if i == 0 and j == 0:
                    tensor.wait_ge(sVE.h, ev_consts)
                if pf is not None:
                    tensor.wait_ge(sVE.h, pf)
                tensor.transpose(ps[0:128, 0:128],
                                 wr_stage[:, j * 128:(j + 1) * 128],
                                 ident[:, :]).then_inc(sMM.h, 1)
            em("tensor", tfn)
            ev_mm = sMM.inc(1)

            def vfn(vector, i=i, j=j, ps=ps, ev_mm=ev_mm):
                vector.wait_ge(sMM.h, ev_mm)
                vector.tensor_copy(wrt_hi[j][:, i * 128:(i + 1) * 128],
                                   ps[0:128, 0:128])
                vector.tensor_tensor(wrt_lo[j][:, i * 128:(i + 1) * 128],
                                     ps[0:128, 0:128],
                                     wrt_hi[j][:, i * 128:(i + 1) * 128],
                                     op=ALU.subtract).then_inc(sVE.h, 1)
            em("vector", vfn)
            pt_free[j % 2] = sVE.inc(1)
        wr_v_done[i] = sVE.n
    ev_wrt_v = sVE.n

    wsh_stage = sbA("wsh_stage", [128, D], F32)
    sblk = sbA("sblk", [128, 32], F32)
    sinv = sbA("sinv", [128, 32], F32)
    wdq = sbA("wdq", [128, D], F32)
    pk_hi = sbA("pk_hi", [128, D], BF16)
    pk_lo = sbA("pk_lo", [128, D], BF16)

    sh_ptfree = [None, None]
    ev_pk_prev = [0]

    def shard_tile(raw_ap, off_hi, off_lo):
        def ldf(sync, prev=ev_pk_prev[0]):
            sync.wait_ge(sGD.h, prev)
            sync.dma_start(out=wsh_stage[:, :], in_=raw_ap).then_inc(sLD.h, 16)
        em("sync", ldf)
        ev_ld = sLD.inc(16)

        e1 = sVE.n + 1
        e2 = sVE.n + 2
        e3 = sVE.n + 3

        def vfn(vector, ev_ld=ev_ld, e1=e1, e2=e2, e3=e3):
            vector.wait_ge(sLD.h, ev_ld)
            vector.wait_ge(sVE.h, ev_consts)
            vector.tensor_reduce(sblk[:, :],
                                 wsh_stage[:, :].rearrange("p (b c) -> p b c", c=32),
                                 axis=AX.X, op=ALU.max,
                                 apply_absolute_value=True).then_inc(sVE.h, 1)
            vector.wait_ge(sVE.h, e1)
            vector.tensor_scalar(sblk[:, :], sblk[:, :], r7[:, 0:1], 1e-12,
                                 op0=ALU.mult, op1=ALU.max).then_inc(sVE.h, 1)
            vector.wait_ge(sVE.h, e2)
            vector.reciprocal(sinv[:, :], sblk[:, :]).then_inc(sVE.h, 1)
            vector.wait_ge(sVE.h, e3)
            vector.tensor_tensor(wdq[:, :].rearrange("p (b c) -> p b c", c=32),
                                 wsh_stage[:, :].rearrange("p (b c) -> p b c", c=32),
                                 sinv[:, :].unsqueeze(2).broadcast_to((128, 32, 32)), op=ALU.mult)
            vector.tensor_scalar(wdq[:, :], wdq[:, :], MAGIC, MAGIC,
                                 op0=ALU.add, op1=ALU.subtract)
            vector.tensor_tensor(wdq[:, :].rearrange("p (b c) -> p b c", c=32),
                                 wdq[:, :].rearrange("p (b c) -> p b c", c=32),
                                 sblk[:, :].unsqueeze(2).broadcast_to((128, 32, 32)), op=ALU.mult).then_inc(sVE.h, 1)
        em("vector", vfn)
        sVE.inc(3)
        ev_dq = sVE.inc(1)

        for j in range(KD):
            ps = pTR[j % 2]
            pf = sh_ptfree[j % 2]

            def tfn(tensor, j=j, ps=ps, ev_dq=ev_dq, pf=pf):
                tensor.wait_ge(sVE.h, ev_dq)
                if pf is not None:
                    tensor.wait_ge(sVE.h, pf)
                tensor.transpose(ps[0:128, 0:128], wdq[:, j * 128:(j + 1) * 128],
                                 ident[:, :]).then_inc(sMM.h, 1)
            em("tensor", tfn)
            ev_mm = sMM.inc(1)

            def vfn2(vector, j=j, ps=ps, ev_mm=ev_mm):
                vector.wait_ge(sMM.h, ev_mm)
                vector.tensor_copy(pk_hi[:, j * 128:(j + 1) * 128], ps[0:128, 0:128])
                vector.tensor_tensor(pk_lo[:, j * 128:(j + 1) * 128],
                                     ps[0:128, 0:128],
                                     pk_hi[:, j * 128:(j + 1) * 128],
                                     op=ALU.subtract).then_inc(sVE.h, 1)
            em("vector", vfn2)
            sh_ptfree[j % 2] = sVE.inc(1)
        ev_tiles = sVE.n

        def gdf(gpsimd, off_hi=off_hi, off_lo=off_lo, ev_tiles=ev_tiles):
            gpsimd.wait_ge(sVE.h, ev_tiles)
            gpsimd.dma_start(
                out=cc0_in.ap()[off_hi:off_hi + GU_TILE].rearrange("(p f) -> p f", p=128),
                in_=pk_hi[:, :]).then_inc(sGD.h, 16)
            gpsimd.dma_start(
                out=cc0_in.ap()[off_lo:off_lo + GU_TILE].rearrange("(p f) -> p f", p=128),
                in_=pk_lo[:, :]).then_inc(sGD.h, 16)
        em("gpsimd", gdf)
        ev_pk_prev[0] = sGD.inc(32)

    for mat, raw in ((0, wg_sh), (1, wu_sh)):
        for j in range(3):
            shard_tile(raw[j * 128:(j + 1) * 128, :],
                       _off_gu(mat, j, 0), _off_gu(mat, j, 1))
    for jj in range(3):
        shard_tile(wd_sh[:, jj * 1024:(jj + 1) * 1024],
                   _off_dn(0) + jj * 8 * 128 * 128,
                   _off_dn(1) + jj * 8 * 128 * 128)
    ev_pk_all = ev_pk_prev[0]

    def cc0(gpsimd):
        gpsimd.wait_ge(sGD.h, ev_pk_all)
        gpsimd.collective_compute(
            "AllGather", ALU.bypass, replica_groups=[list(range(N_CORES))],
            ins=[cc0_in.ap()], outs=[cc0_out.ap()]).then_inc(sCC.h, 1)
    em("gpsimd", cc0)
    ev_cc0 = sCC.inc(1)

    h_cast_done = {}
    fr_mm_done = {}
    fr_ps_free = [None, None, None, None]
    sr_ps_free = [None]
    lf_mm_prev = [None, None]
    hstat_prev = [None]
    num_prev = [None]
    ev_half_stats = {}

    for half in range(2):
        for tci in range(NTCH):
            tc = half * NTCH + tci
            buf = hst[0]

            def ldh(sync, tc=tc, buf=buf):
                if tc >= 1:
                    sync.wait_ge(sVE.h, h_cast_done[tc - 1])
                for k in range(KD):
                    sync.dma_start(
                        out=buf[:, k * TCH:(k + 1) * TCH],
                        in_=h_fm[k * 128:(k + 1) * 128, tc * TCH:(tc + 1) * TCH]
                    ).then_inc(sLD.h, 16)
            em("sync", ldh)
            ev_hld = sLD.inc(16 * KD)

            def castf(vector, tc=tc, tci=tci, buf=buf, ev_hld=ev_hld):
                vector.wait_ge(sLD.h, ev_hld)
                if tc >= 1:
                    vector.wait_ge(sMM.h, fr_mm_done[tc - 1])
                vector.tensor_copy(hh[:, :], buf[:, :])
                vector.tensor_tensor(hl[:, :], buf[:, :], hh[:, :],
                                     op=ALU.subtract).then_inc(sVE.h, 1)
            em("vector", castf)
            h_cast_done[tc] = sVE.inc(1)

            for oc in range(KD):
                ps = pMM[oc % 4]
                pf = fr_ps_free[oc % 4]

                def mmf(tensor, tc=tc, tci=tci, oc=oc, ps=ps, pf=pf):
                    if tc == 0 and oc == 0:
                        tensor.wait_ge(sVE.h, ev_wrt_v)
                    tensor.wait_ge(sVE.h, h_cast_done[tc])
                    if pf is not None:
                        tensor.wait_ge(sVE.h, pf)
                    mm = None
                    for k in range(KD):
                        tensor.matmul(ps[:, :], wrt_hi[k][:, oc * 128:(oc + 1) * 128],
                                      hh[:, k * TCH:(k + 1) * TCH],
                                      start=(k == 0), stop=False)
                        tensor.matmul(ps[:, :], wrt_hi[k][:, oc * 128:(oc + 1) * 128],
                                      hl[:, k * TCH:(k + 1) * TCH],
                                      start=False, stop=False)
                        mm = tensor.matmul(ps[:, :], wrt_lo[k][:, oc * 128:(oc + 1) * 128],
                                           hh[:, k * TCH:(k + 1) * TCH],
                                           start=False, stop=(k == KD - 1))
                    mm.then_inc(sMM.h, 1)
                em("tensor", mmf)
                ev_mm = sMM.inc(1)
                if oc == KD - 1:
                    fr_mm_done[tc] = ev_mm

                def evf(vector, tci=tci, oc=oc, ps=ps, ev_mm=ev_mm):
                    vector.wait_ge(sMM.h, ev_mm)
                    vector.tensor_copy(fr_sb[oc][:, tci * TCH:(tci + 1) * TCH],
                                       ps[:, :]).then_inc(sVE.h, 1)
                em("vector", evf)
                fr_ps_free[oc % 4] = sVE.inc(1)
        ev_fr_half_v = sVE.n

        for hc in range(H):
            for tci in range(NTCH):
                gi = hc * NTCH + tci
                lb = lf_sb[gi % 2]

                def lff(vector, hc=hc, tci=tci, lb=lb,
                        ev_fr_half_v=ev_fr_half_v, pmm=lf_mm_prev[gi % 2]):
                    vector.wait_ge(sVE.h, ev_fr_half_v)
                    if pmm is not None:
                        vector.wait_ge(sMM.h, pmm)
                    vector.scalar_tensor_tensor(
                        lb[:, :], fr_sb[hc][:, tci * TCH:(tci + 1) * TCH], 0.01,
                        fr_sb[hc][:, tci * TCH:(tci + 1) * TCH],
                        op0=ALU.mult, op1=ALU.max).then_inc(sVE.h, 1)
                em("vector", lff)
                ev_lf = sVE.inc(1)

                def srm(tensor, ev_lf=ev_lf, lb=lb, pf=sr_ps_free[0]):
                    tensor.wait_ge(sVE.h, ev_lf)
                    if pf is not None:
                        tensor.wait_ge(sSC.h, pf)
                    tensor.matmul(pSR, attr_c[:, 0:1], lb[:, :],
                                  start=True, stop=True).then_inc(sMM.h, 1)
                em("tensor", srm)
                ev_srm = sMM.inc(1)
                lf_mm_prev[gi % 2] = ev_srm

                def sre(scalar, tci=tci, ev_srm=ev_srm, prev_head=hstat_prev[0]):
                    scalar.wait_ge(sMM.h, ev_srm)
                    if tci == 0 and prev_head is not None:
                        scalar.wait_ge(sGD.h, prev_head)
                    scalar.copy(sr0[0:1, tci * TCH:(tci + 1) * TCH],
                                pSR).then_inc(sSC.h, 1)
                em("scalar", sre)
                sr_ps_free[0] = sSC.inc(1)
            ev_sr_head = sSC.n
            slot = half * 8 + hc

            ea = sVE.n + 1

            def hstat_v(vector, slot=slot, ev_sr_head=ev_sr_head, ea=ea):
                vector.wait_ge(sSC.h, ev_sr_head)
                vector.tensor_reduce(m_all[0:1, slot:slot + 1], sr0[:, :],
                                     axis=AX.X, op=ALU.max).then_inc(sVE.h, 1)
                vector.wait_ge(sVE.h, ea)
                vector.tensor_scalar(sr_negm[:, :], m_all[0:1, slot:slot + 1],
                                     -1.0, None, op0=ALU.mult).then_inc(sVE.h, 1)
            em("vector", hstat_v)
            sVE.inc(1)
            ev_hm = sVE.inc(1)

            def hstat_s(scalar, ev_hm=ev_hm):
                scalar.wait_ge(sVE.h, ev_hm)
                scalar.activation(sr0[:, :], sr0[:, :], ACT.Exp,
                                  bias=sr_negm[0:1, 0:1], scale=1.0).then_inc(sSC.h, 1)
            em("scalar", hstat_s)
            ev_he = sSC.inc(1)

            def hstat_v2(vector, slot=slot, ev_he=ev_he):
                vector.wait_ge(sSC.h, ev_he)
                vector.tensor_reduce(d_all[0:1, slot:slot + 1], sr0[:, :],
                                     axis=AX.X, op=ALU.add).then_inc(sVE.h, 1)
            em("vector", hstat_v2)
            sVE.inc(1)

            def e_out(gpsimd, hc=hc, half=half, ev_he=ev_he):
                gpsimd.wait_ge(sSC.h, ev_he)
                gpsimd.dma_start(out=e_dram[hc:hc + 1, half * TH:(half + 1) * TH],
                                 in_=sr0[:, :]).then_inc(sGD.h, 16)
            em("gpsimd", e_out)
            hstat_prev[0] = sGD.inc(16)
            ev_eh_dram = hstat_prev[0]

            def ebl(gpsimd, hc=hc, half=half, ev_eh_dram=ev_eh_dram,
                    prev=num_prev[0]):
                gpsimd.wait_ge(sGD.h, ev_eh_dram)
                if prev is not None:
                    gpsimd.wait_ge(sVE.h, prev)
                gpsimd.dma_start(
                    out=ebc[:, :],
                    in_=e_dram[hc:hc + 1, half * TH:(half + 1) * TH]
                    .partition_broadcast(128)).then_inc(sGD.h, 16)
            em("gpsimd", ebl)
            ev_ebl = sGD.inc(16)

            def numf(vector, hc=hc, half=half, ev_ebl=ev_ebl):
                vector.wait_ge(sGD.h, ev_ebl)
                vector.scalar_tensor_tensor(
                    ebc[:, :], fr_sb[hc][:, :], 1.0, ebc[:, :],
                    op0=ALU.mult, op1=ALU.mult,
                    accum_out=numh[half][:, hc:hc + 1]).then_inc(sVE.h, 1)
            em("vector", numf)
            num_prev[0] = sVE.inc(1)
        ev_half_stats[half] = sVE.n

        def numt_t(tensor, half=half, ev=ev_half_stats[half]):
            tensor.wait_ge(sVE.h, ev)
            tensor.transpose(pTR[0][0:H, 0:128], numh[half][:, :],
                             ident[:, :]).then_inc(sMM.h, 1)
        em("tensor", numt_t)
        ev_ntm = sMM.inc(1)

        def numt_v(vector, half=half, ev_ntm=ev_ntm):
            vector.wait_ge(sMM.h, ev_ntm)
            vector.tensor_copy(numT_h[half][:, :], pTR[0][0:H, 0:128]).then_inc(sVE.h, 1)
        em("vector", numt_v)
        sVE.inc(1)
    ev_both_halves = sVE.n

    _ea = sVE.n + 1

    def merge_v(vector, _ea=_ea):
        vector.wait_ge(sVE.h, ev_both_halves)
        vector.tensor_tensor(mm8[:, :], m_all[0:1, 0:8], m_all[0:1, 8:16],
                             op=ALU.max).then_inc(sVE.h, 1)
        vector.wait_ge(sVE.h, _ea)
        vector.tensor_tensor(t16[0:1, 0:8], m_all[0:1, 0:8], mm8[:, :],
                             op=ALU.subtract)
        vector.tensor_tensor(t16[0:1, 8:16], m_all[0:1, 8:16], mm8[:, :],
                             op=ALU.subtract).then_inc(sVE.h, 1)
    em("vector", merge_v)
    sVE.inc(1)
    ev_mM = sVE.inc(1)

    def merge_s(scalar):
        scalar.wait_ge(sVE.h, ev_mM)
        scalar.activation(w01[:, :], t16[:, :], ACT.Exp, bias=0.0,
                          scale=1.0).then_inc(sSC.h, 1)
    em("scalar", merge_s)
    ev_w2 = sSC.inc(1)

    _eb = sVE.n + 1

    def merge_d(vector, _eb=_eb):
        vector.wait_ge(sSC.h, ev_w2)
        vector.tensor_tensor(t8a[:, :], w01[0:1, 0:8], d_all[0:1, 0:8],
                             op=ALU.mult)
        vector.tensor_tensor(dmrg[:, :], w01[0:1, 8:16], d_all[0:1, 8:16],
                             op=ALU.mult).then_inc(sVE.h, 1)
        vector.wait_ge(sVE.h, _eb)
        vector.tensor_tensor(dmrg[:, :], dmrg[:, :], t8a[:, :],
                             op=ALU.add).then_inc(sVE.h, 1)
    em("vector", merge_d)
    sVE.inc(1)
    ev_dm = sVE.inc(1)

    mg_tp_free = [None]
    mg_cols = []
    for src_ap, dst in ((mm8[0:1, 0:8], mcol), (dmrg[0:1, 0:8], dcol),
                        (w01[0:1, 0:8], w0c), (w01[0:1, 8:16], w1c)):
        def mgt(tensor, src_ap=src_ap, pf=mg_tp_free[0]):
            tensor.wait_ge(sVE.h, ev_dm)
            if pf is not None:
                tensor.wait_ge(sVE.h, pf)
            tensor.transpose(pTR[0][0:8, 0:1], src_ap,
                             ident[0:1, 0:1]).then_inc(sMM.h, 1)
        em("tensor", mgt)
        ev_tp = sMM.inc(1)

        def mge(vector, dst=dst, ev_tp=ev_tp):
            vector.wait_ge(sMM.h, ev_tp)
            vector.tensor_copy(dst[:, :], pTR[0][0:8, 0:1]).then_inc(sVE.h, 1)
        em("vector", mge)
        mg_tp_free[0] = sVE.inc(1)
    ev_cols = sVE.n

    _e1 = sVE.n + 1
    _e2 = sVE.n + 2

    def merge_num(vector, _e1=_e1, _e2=_e2):
        vector.wait_ge(sVE.h, ev_cols)
        vector.tensor_scalar(numT[:, :], numT_h[0][:, :], w0c[:, 0:1], None,
                             op0=ALU.mult).then_inc(sVE.h, 1)
        vector.wait_ge(sVE.h, _e1)
        vector.scalar_tensor_tensor(numT[:, :], numT_h[1][:, :], w1c[:, 0:1],
                                    numT[:, :], op0=ALU.mult,
                                    op1=ALU.add).then_inc(sVE.h, 1)
        vector.wait_ge(sVE.h, _e2)
        vector.tensor_copy(pack_sb[:, 0:1], mcol[:, :])
        vector.tensor_copy(pack_sb[:, 1:2], dcol[:, :])
        vector.tensor_copy(pack_sb[:, 2:130], numT[:, :]).then_inc(sVE.h, 1)
    em("vector", merge_num)
    sVE.inc(2)
    ev_pack = sVE.inc(1)

    def cc1_pack(gpsimd):
        gpsimd.wait_ge(sVE.h, ev_pack)
        gpsimd.dma_start(out=cc1_in.ap(), in_=pack_sb[:, :]).then_inc(sGD.h, 16)
    em("gpsimd", cc1_pack)
    ev_cc1_in = sGD.inc(16)

    def cc1_run(gpsimd):
        gpsimd.wait_ge(sGD.h, ev_cc1_in)
        gpsimd.collective_compute(
            "AllGather", ALU.bypass, replica_groups=[list(range(N_CORES))],
            ins=[cc1_in.ap()], outs=[cc1_out.ap()]).then_inc(sCC.h, 1)
    em("gpsimd", cc1_run)
    ev_cc1 = sCC.inc(1)

    wf_ptfree = [None, None]
    wf_v_done = {}
    for i in range(KD):
        def ldwf(sync, i=i):
            if i >= 1:
                sync.wait_ge(sVE.h, wf_v_done[i - 1])
            else:
                sync.wait_ge(sVE.h, ev_both_halves)
            sync.dma_start(out=wr_stage[:, :],
                           in_=wf_raw[i * 128:(i + 1) * 128, :]).then_inc(sLD.h, 16)
        em("sync", ldwf)
        ev_ld = sLD.inc(16)
        for j in range(KD):
            ps = pTR[(j % 2)]
            pf = wf_ptfree[j % 2]

            def tfn(tensor, i=i, j=j, ps=ps, ev_ld=ev_ld, pf=pf):
                tensor.wait_ge(sLD.h, ev_ld)
                if pf is not None:
                    tensor.wait_ge(sVE.h, pf)
                tensor.transpose(ps[0:128, 0:128],
                                 wr_stage[:, j * 128:(j + 1) * 128],
                                 ident[:, :]).then_inc(sMM.h, 1)
            em("tensor", tfn)
            ev_mm = sMM.inc(1)

            def vfn(vector, i=i, j=j, ps=ps, ev_mm=ev_mm):
                vector.wait_ge(sMM.h, ev_mm)
                vector.tensor_copy(wrt_hi[j][:, i * 128:(i + 1) * 128],
                                   ps[0:128, 0:128])
                vector.tensor_tensor(wrt_lo[j][:, i * 128:(i + 1) * 128],
                                     ps[0:128, 0:128],
                                     wrt_hi[j][:, i * 128:(i + 1) * 128],
                                     op=ALU.subtract).then_inc(sVE.h, 1)
            em("vector", vfn)
            wf_ptfree[j % 2] = sVE.inc(1)
        wf_v_done[i] = sVE.n
    ev_wft = sVE.n

    def comb_ld(gpsimd):
        gpsimd.wait_ge(sCC.h, ev_cc1)
        v = cc1_out.ap().rearrange("(b a) h c -> b a h c", a=2)
        gpsimd.dma_start(out=gA[:, :], in_=v[:, 0:1]).then_inc(sGD.h, 16)
        gpsimd.dma_start(out=gB[:, :], in_=v[:, 1:2]).then_inc(sGD.h, 16)
    em("gpsimd", comb_ld)
    ev_gAB = sGD.inc(32)

    _ec = sVE.n + 1

    def comb_v(vector, _ec=_ec):
        vector.wait_ge(sGD.h, ev_gAB)
        vector.tensor_tensor(mx32[:, :], gA[:, 0:1], gB[:, 0:1],
                             op=ALU.max).then_inc(sVE.h, 1)
        vector.wait_ge(sVE.h, _ec)
        vector.tensor_scalar(ngm32[:, :], mx32[:, :], -1.0, None,
                             op0=ALU.mult).then_inc(sVE.h, 1)
    em("vector", comb_v)
    sVE.inc(1)
    ev_mx = sVE.inc(1)

    def comb_s(scalar):
        scalar.wait_ge(sVE.h, ev_mx)
        scalar.activation(wA32[:, :], gA[:, 0:1], ACT.Exp, bias=ngm32[:, 0:1],
                          scale=1.0)
        scalar.activation(wB32[:, :], gB[:, 0:1], ACT.Exp, bias=ngm32[:, 0:1],
                          scale=1.0).then_inc(sSC.h, 1)
    em("scalar", comb_s)
    ev_w32 = sSC.inc(1)

    _g = [sVE.n + k for k in range(1, 5)]

    def comb_v2(vector, _g=_g):
        vector.wait_ge(sSC.h, ev_w32)
        vector.tensor_scalar(t129[:, :], gA[:, 1:130], wA32[:, 0:1], None,
                             op0=ALU.mult).then_inc(sVE.h, 1)
        vector.wait_ge(sVE.h, _g[0])
        vector.scalar_tensor_tensor(cn129[:, :], gB[:, 1:130], wB32[:, 0:1],
                                    t129[:, :], op0=ALU.mult,
                                    op1=ALU.add).then_inc(sVE.h, 1)
        vector.wait_ge(sVE.h, _g[1])
        vector.tensor_copy(dcb[:, :], cn129[:, 0:1]).then_inc(sVE.h, 1)
        vector.wait_ge(sVE.h, _g[2])
        vector.reciprocal(rdcb[:, :], dcb[:, :]).then_inc(sVE.h, 1)
        vector.wait_ge(sVE.h, _g[3])
        vector.tensor_scalar(ctx32[:, :], cn129[:, 1:129], rdcb[:, 0:1], None,
                             op0=ALU.mult).then_inc(sVE.h, 1)
    em("vector", comb_v2)
    sVE.inc(4)
    ev_ctx = sVE.inc(1)

    def ctxt_t(tensor):
        tensor.wait_ge(sVE.h, ev_ctx)
        tensor.transpose(pTR[0][0:128, 0:32], ctx32[:, :], ident[0:32, 0:32]).then_inc(sMM.h, 1)
    em("tensor", ctxt_t)
    ev_ctxT_mm = sMM.inc(1)

    _h = [sVE.n + 1, sVE.n + 2]

    def ctxt_v(vector, _h=_h):
        vector.wait_ge(sMM.h, ev_ctxT_mm)
        vector.tensor_copy(ctxT[:, :], pTR[0][0:128, 0:32]).then_inc(sVE.h, 1)
        vector.wait_ge(sVE.h, _h[0])
        vector.tensor_copy(ctxT_h[:, :], ctxT[:, :]).then_inc(sVE.h, 1)
        vector.wait_ge(sVE.h, _h[1])
        vector.tensor_tensor(ctxT_l[:, :], ctxT[:, :], ctxT_h[:, :],
                             op=ALU.subtract).then_inc(sVE.h, 1)
    em("vector", ctxt_v)
    sVE.inc(2)
    ev_ctxT = sVE.inc(1)

    fh_mm_evts = []
    for half in range(2):
        def fhm(tensor, half=half):
            tensor.wait_ge(sVE.h, max(ev_ctxT, ev_wft))
            mm = None
            for kc in range(KD):
                tensor.matmul(pFH[half], ctxT_h[:, kc:kc + 25:8],
                              wrt_hi[kc][:, half * TCH:(half + 1) * TCH],
                              start=(kc == 0), stop=False)
                tensor.matmul(pFH[half], ctxT_h[:, kc:kc + 25:8],
                              wrt_lo[kc][:, half * TCH:(half + 1) * TCH],
                              start=False, stop=False)
                mm = tensor.matmul(pFH[half], ctxT_l[:, kc:kc + 25:8],
                                   wrt_hi[kc][:, half * TCH:(half + 1) * TCH],
                                   start=False, stop=(kc == KD - 1))
            mm.then_inc(sMM.h, 1)
        em("tensor", fhm)
        ev_fhm = sMM.inc(1)
        fh_mm_evts.append(ev_fhm)

        def fhe(vector, half=half, ev_fhm=ev_fhm):
            vector.wait_ge(sMM.h, ev_fhm)
            vector.tensor_copy(fh4[:, half * TCH:(half + 1) * TCH],
                               pFH[half]).then_inc(sVE.h, 1)
        em("vector", fhe)
        sVE.inc(1)
    ev_fh4 = sVE.n

    sel_prev = [None]
    for half in range(2):
        def self_m(tensor, half=half, pf=sel_prev[0]):
            tensor.wait_ge(sVE.h, ev_fh4)
            if pf is not None:
                tensor.wait_ge(sSC.h, pf)
            tensor.matmul(pSEL, bsel_c[:, 0:1],
                          fh4[:, half * TCH:(half + 1) * TCH],
                          start=True, stop=True).then_inc(sMM.h, 1)
        em("tensor", self_m)
        ev_selm = sMM.inc(1)

        def sel_e(scalar, half=half, ev_selm=ev_selm):
            scalar.wait_ge(sMM.h, ev_selm)
            scalar.copy(fh1[0:1, half * TCH:(half + 1) * TCH], pSEL).then_inc(sSC.h, 1)
        em("scalar", sel_e)
        sel_prev[0] = sSC.inc(1)
    ev_fh1 = sSC.n
    ev_fh_mm_last = sMM.n

    ctxA.close()

    ctxM = ExitStack()
    sbM = mk(ctxM)
    x1c = [sbM(f"x1c{i}", [128, D], F32) for i in range(2)]
    fhb = sbM("fhb", [128, D], F32)

    def fh_out(gpsimd):
        gpsimd.wait_ge(sSC.h, ev_fh1)
        gpsimd.dma_start(out=fh_dram[:, :], in_=fh1[:, :]).then_inc(sGD.h, 16)
    em("gpsimd", fh_out)
    ev_fh_dram = sGD.inc(16)

    def fh_bc(gpsimd):
        gpsimd.wait_ge(sGD.h, ev_fh_dram)
        gpsimd.wait_ge(sMM.h, ev_fh_mm_last)
        gpsimd.dma_start(out=fhb[:, :],
                         in_=fh_dram[0:1, :].partition_broadcast(128)).then_inc(sGD.h, 16)
    em("gpsimd", fh_bc)
    ev_fhb = sGD.inc(16)

    ctxP2 = ExitStack()
    sbP = mk(ctxP2)
    htm_st = [sbP(f"htm{i}", [128, D], F32) for i in range(2)]
    p2y = sbP("p2y", [128, D], F32)
    p2t = sbP("p2t", [128, D], F32)
    p2a = sbP("p2a", [128, D], F32)
    gam_b = sbP("gam_b", [128, D], F32)
    bet_b = sbP("bet_b", [128, D], F32)

    def p2_gb_ld(sync):
        sync.wait_ge(sMM.h, ev_fh_mm_last)
        sync.wait_ge(sSC.h, ev_fh1)
        sync.dma_start(out=gam_b[:, :],
                       in_=gamma[0:1, :].partition_broadcast(128)).then_inc(sLD.h, 16)
        sync.dma_start(out=bet_b[:, :],
                       in_=beta[0:1, :].partition_broadcast(128)).then_inc(sLD.h, 16)
    em("sync", p2_gb_ld)
    ev_gb_ld = sLD.inc(32)

    p2_done = {}
    x1_spill_done = {}
    for tcm in range(NTM):
        buf = htm_st[tcm % 2]
        xb = x1c[tcm % 2]

        def ldh2(sync, tcm=tcm, buf=buf):
            if tcm < 2:
                sync.wait_ge(sMM.h, ev_fh_mm_last)
                sync.wait_ge(sSC.h, ev_fh1)
                sync.wait_ge(sGD.h, ev_fhb)
            else:
                sync.wait_ge(sVE.h, p2_done[tcm - 2])
            sync.dma_start(out=buf[:, :],
                           in_=h_tm[tcm * TMC:(tcm + 1) * TMC, :]).then_inc(sLD.h, 16)
        em("sync", ldh2)
        ev_ld = sLD.inc(16)

        _i1 = sVE.n + 1

        def p2v(vector, tcm=tcm, buf=buf, ev_ld=ev_ld, _i1=_i1):
            vector.wait_ge(sLD.h, ev_ld)
            if tcm == 0:
                vector.wait_ge(sGD.h, ev_fhb)
                vector.wait_ge(sLD.h, ev_gb_ld)
            if tcm >= 2:
                vector.wait_ge(sST.h, x1_spill_done[tcm - 2])
            vector.tensor_tensor(p2y[:, :], buf[:, :], fhb[:, :], op=ALU.add)
            vector.tensor_reduce(lnsum[:, :], p2y[:, :], axis=AX.X,
                                 op=ALU.add).then_inc(sVE.h, 1)
            vector.wait_ge(sVE.h, _i1)
            vector.tensor_scalar(negmu[:, :], lnsum[:, :], -INV1024, None,
                                 op0=ALU.mult).then_inc(sVE.h, 1)
            vector.wait_ge(sVE.h, _i1 + 1)
            vector.tensor_scalar(p2t[:, :], p2y[:, :], negmu[:, 0:1], None,
                                 op0=ALU.add).then_inc(sVE.h, 1)
        em("vector", p2v)
        sVE.inc(2)
        ev_t1 = sVE.inc(1)

        def p2s(scalar, ev_t1=ev_t1):
            scalar.wait_ge(sVE.h, ev_t1)
            scalar.activation(sqscr[:, :], p2t[:, :], ACT.Square,
                              accum_out=ssq[:, :])
            scalar.activation(lnsd[:, :], ssq[:, :], ACT.Sqrt,
                              bias=cst[:, 3:4], scale=cst[:, 6:7]).then_inc(sSC.h, 1)
        em("scalar", p2s)
        ev_sd1 = sSC.inc(1)

        _j1 = sVE.n + 1

        def p2v2(vector, buf=buf, ev_sd1=ev_sd1, _j1=_j1):
            vector.wait_ge(sSC.h, ev_sd1)
            vector.reciprocal(lnr[:, :], lnsd[:, :]).then_inc(sVE.h, 1)
            vector.wait_ge(sVE.h, _j1)
            vector.scalar_tensor_tensor(p2a[:, :], p2t[:, :], lnr[:, 0:1],
                                        gam_b[:, :], op0=ALU.mult, op1=ALU.mult)
            vector.tensor_tensor(p2a[:, :], p2a[:, :], bet_b[:, :], op=ALU.add)
            vector.tensor_tensor(p2y[:, :], buf[:, :], p2a[:, :], op=ALU.add)
            vector.tensor_reduce(lnsum[:, :], p2y[:, :], axis=AX.X,
                                 op=ALU.add).then_inc(sVE.h, 1)
            vector.wait_ge(sVE.h, _j1 + 1)
            vector.tensor_scalar(negmu[:, :], lnsum[:, :], -INV1024, None,
                                 op0=ALU.mult).then_inc(sVE.h, 1)
            vector.wait_ge(sVE.h, _j1 + 2)
            vector.tensor_scalar(p2t[:, :], p2y[:, :], negmu[:, 0:1], None,
                                 op0=ALU.add).then_inc(sVE.h, 1)
        em("vector", p2v2)
        sVE.inc(3)
        ev_t2 = sVE.inc(1)

        def p2s2(scalar, ev_t2=ev_t2):
            scalar.wait_ge(sVE.h, ev_t2)
            scalar.activation(sqscr[:, :], p2t[:, :], ACT.Square,
                              accum_out=ssq[:, :])
            scalar.activation(lnsd[:, :], ssq[:, :], ACT.Sqrt,
                              bias=cst[:, 3:4], scale=cst[:, 6:7]).then_inc(sSC.h, 1)
        em("scalar", p2s2)
        ev_sd2 = sSC.inc(1)

        _k1 = sVE.n + 1

        def p2v3(vector, tcm=tcm, xb=xb, ev_sd2=ev_sd2, _k1=_k1):
            vector.wait_ge(sSC.h, ev_sd2)
            vector.reciprocal(lnr[:, :], lnsd[:, :]).then_inc(sVE.h, 1)
            vector.wait_ge(sVE.h, _k1)
            vector.tensor_scalar(xb[:, :], p2t[:, :], lnr[:, 0:1], None,
                                 op0=ALU.mult)
            vector.tensor_reduce(xm32[:, tcm:tcm + 1], xb[:, :], axis=AX.X,
                                 op=ALU.max, apply_absolute_value=True).then_inc(sVE.h, 1)
        em("vector", p2v3)
        sVE.inc(1)
        p2_done[tcm] = sVE.inc(1)

        def x1sp(scalar, tcm=tcm, xb=xb):
            scalar.wait_ge(sVE.h, p2_done[tcm])
            scalar.dma_start(out=x1_dram.ap()[tcm], in_=xb[:, :]).then_inc(sST.h, 16)
        em("scalar", x1sp)
        x1_spill_done[tcm] = sST.inc(16)
    ev_p2_all = sVE.n
    ev_x1_sp_all = sST.n

    ctxP2.close()

    last_gf = [0]
    last_red_user = [None]

    def lmax_emit(max_cols_ap, cc_idx, slot, ready_sem, ev_ready):
        def vf(vector, prev_gf=last_gf[0]):
            vector.wait_ge(ready_sem.h, ev_ready)
            if prev_gf:
                vector.wait_ge(sGD.h, prev_gf)
            vector.tensor_reduce(red1[:, :], max_cols_ap, axis=AX.X,
                                 op=ALU.max).then_inc(sVE.h, 1)
        em("vector", vf)
        ev_r1 = sVE.inc(1)

        def tf(tensor):
            tensor.wait_ge(sVE.h, ev_r1)
            tensor.transpose(pTR[0][0:1, 0:128], red1[:, 0:1],
                             ident[:, :]).then_inc(sMM.h, 1)
        em("tensor", tf)
        ev_tm = sMM.inc(1)

        _o1 = sVE.n + 1

        def vf2(vector, ev_tm=ev_tm, _o1=_o1):
            vector.wait_ge(sMM.h, ev_tm)
            vector.tensor_copy(redT[:, :], pTR[0][0:1, 0:128]).then_inc(sVE.h, 1)
            vector.wait_ge(sVE.h, _o1)
            vector.tensor_reduce(red0[:, :], redT[:, :], axis=AX.X,
                                 op=ALU.max).then_inc(sVE.h, 1)
        em("vector", vf2)
        sVE.inc(1)
        ev_r0 = sVE.inc(1)

        def gf(gpsimd, cc_idx=cc_idx, slot=slot, ev_r0=ev_r0):
            gpsimd.wait_ge(sVE.h, ev_r0)
            gpsimd.wait_ge(sGD.h, ev_ccz)
            gpsimd.dma_start(out=ccs_in[cc_idx][0:1, slot:slot + 1],
                             in_=red0[:, :]).then_inc(sGD.h, 16)
        em("gpsimd", gf)
        last_gf[0] = sGD.inc(16)
        return last_gf[0]

    def cc_run(cc_idx, ev_in_ready):
        def cf(gpsimd, cc_idx=cc_idx, ev_in_ready=ev_in_ready):
            gpsimd.wait_ge(sGD.h, ev_in_ready)
            gpsimd.collective_compute(
                "AllGather", ALU.bypass, replica_groups=[list(range(N_CORES))],
                ins=[ccs_in[cc_idx].ap()], outs=[ccs_out[cc_idx].ap()]).then_inc(sCC.h, 1)
        em("gpsimd", cf)
        return sCC.inc(1)

    def cc_bcast(cc_idx, ev_cc):
        def bf(gpsimd, cc_idx=cc_idx, ev_cc=ev_cc):
            gpsimd.wait_ge(sCC.h, ev_cc)
            gpsimd.dma_start(
                out=s_bc[cc_idx][:, :],
                in_=ccs_out[cc_idx].ap().rearrange("r c -> (r c)")
                .rearrange("(a f) -> a f", a=1).partition_broadcast(128)
            ).then_inc(sGD.h, 16)
        em("gpsimd", bf)
        return sGD.inc(16)

    def scale_from_slot(cc_idx, slot, s_tile, rs_tile, ev_bc):
        _n1 = sVE.n + 1

        def vf(vector, cc_idx=cc_idx, slot=slot, ev_bc=ev_bc, _n1=_n1):
            vector.wait_ge(sGD.h, ev_bc)
            vector.wait_ge(sVE.h, ev_consts)
            vector.tensor_reduce(s_tmp[:, 0:1],
                                 s_bc[cc_idx][:, slot:slot + 113:16],
                                 axis=AX.X, op=ALU.max).then_inc(sVE.h, 1)
            vector.wait_ge(sVE.h, _n1)
            vector.tensor_scalar(s_tile[:, :], s_tmp[:, 0:1], r32767[:, 0:1],
                                 1e-12, op0=ALU.mult, op1=ALU.max).then_inc(sVE.h, 1)
            vector.wait_ge(sVE.h, _n1 + 1)
            vector.reciprocal(rs_tile[:, :], s_tile[:, :]).then_inc(sVE.h, 1)
        em("vector", vf)
        sVE.inc(2)
        return sVE.inc(1)

    ev_in2 = lmax_emit(xm32[:, :], 0, 0, sVE, ev_p2_all)
    ev_cc2 = cc_run(0, ev_in2)
    ev_bc2 = cc_bcast(0, ev_cc2)
    ev_s1 = scale_from_slot(0, 0, s_s1, s_rs1, ev_bc2)

    ctxRB = ExitStack()
    sbRB = mk(ctxRB)
    rb_in = [sbRB(f"rb_in{i}", [128, TCH], F32) for i in range(2)]
    rb_ig = sbRB("rb_ig", [128, TCH], F32)
    rb_t1 = sbRB("rb_t1", [128, TCH], F32)
    rb_t2 = sbRB("rb_t2", [128, TCH], F32)
    rb_o = [sbRB(f"rb_o{i}", [128, TCH], F32) for i in range(2)]


    ctxF1 = ExitStack()
    sbF = mk(ctxF1)
    q16 = sbF("q16", [128, D], F32)
    qh_fm = [sbF(f"qh{k}", [128, T], BF16) for k in range(KD)]
    ql_fm = [sbF(f"ql{k}", [128, T], BF16) for k in range(KD)]
    wt_hi = [sbF(f"wt_hi{i}", [128, D], BF16) for i in range(2)]
    wt_lo = [sbF(f"wt_lo{i}", [128, D], BF16) for i in range(2)]
    ev_tile = [sbF(f"ev_tile{i}", [128, TCH], F32) for i in range(4)]

    q_ptfree = [None, None]
    q16_done = {}
    for tcm in range(NTM):
        xb = x1c[tcm % 2]

        def x1rd(sync, tcm=tcm, xb=xb):
            if tcm == 0:
                sync.wait_ge(sST.h, ev_x1_sp_all)
            if tcm >= 2:
                sync.wait_ge(sVE.h, q16_done[tcm - 2])
            sync.dma_start(out=xb[:, :], in_=x1_dram.ap()[tcm]).then_inc(sLD.h, 16)
        em("sync", x1rd)
        ev_ld = sLD.inc(16)

        def q16v(vector, tcm=tcm, xb=xb, ev_ld=ev_ld):
            vector.wait_ge(sLD.h, ev_ld)
            if tcm == 0:
                vector.wait_ge(sVE.h, ev_s1)
            if tcm >= 1:
                vector.wait_ge(sMM.h, q_mm_done[tcm - 1])
            vector.tensor_scalar(q16[:, :], xb[:, :], s_rs1[:, 0:1], MAGIC,
                                 op0=ALU.mult, op1=ALU.add).then_inc(sVE.h, 1)
        if tcm == 0:
            q16s_done = {}
            q_mm_done = {}
        em("vector", q16v)
        ev_q16a = sVE.inc(1)

        def q16s(scalar, ev_q16a=ev_q16a):
            scalar.wait_ge(sVE.h, ev_q16a)
            scalar.activation(q16[:, :], q16[:, :], ACT.Copy,
                              bias=-MAGIC, scale=1.0).then_inc(sSC.h, 1)
        em("scalar", q16s)
        q16s_done[tcm] = sSC.inc(1)

        for j in range(KD):
            ps = pTR[j % 2]
            pf = q_ptfree[j % 2]

            def qtt(tensor, tcm=tcm, j=j, ps=ps, pf=pf):
                tensor.wait_ge(sSC.h, q16s_done[tcm])
                if pf is not None:
                    tensor.wait_ge(sVE.h, pf)
                tensor.transpose(ps[0:128, 0:128], q16[:, j * 128:(j + 1) * 128],
                                 ident[:, :]).then_inc(sMM.h, 1)
            em("tensor", qtt)
            ev_mm = sMM.inc(1)

            def qte(vector, tcm=tcm, j=j, ps=ps, ev_mm=ev_mm):
                vector.wait_ge(sMM.h, ev_mm)
                vector.tensor_copy(qh_fm[j][:, tcm * TMC:(tcm + 1) * TMC],
                                   ps[0:128, 0:128])
                vector.tensor_tensor(ql_fm[j][:, tcm * TMC:(tcm + 1) * TMC],
                                     ps[0:128, 0:128],
                                     qh_fm[j][:, tcm * TMC:(tcm + 1) * TMC],
                                     op=ALU.subtract).then_inc(sVE.h, 1)
            em("vector", qte)
            q_ptfree[j % 2] = sVE.inc(1)
        q_mm_done[tcm] = sMM.n
        q16_done[tcm] = sVE.n
    ev_q_all = sVE.n

    gu_wfree = [None, None]
    gu_psfree = [None, None, None, None]
    gu_ebfree = [None, None, None, None]

    def emit_gu_phase(mat, dst_dram, max_cols, tail_cb=None):
        ev_wld_map = {}

        def emit_wld(oc, mat=mat):
            wb = oc % 2
            rank, j = oc % 8, oc // 8
            wfree = gu_wfree[wb]

            def wld(sync, mat=mat, wb=wb, rank=rank, j=j, wfree=wfree):
                if mat == 0 and j == 0 and rank < 2:
                    sync.wait_ge(sCC.h, ev_cc0)
                if wfree is not None:
                    sync.wait_ge(sMM.h, wfree)
                off_hi = _off_gu(mat, j, 0)
                off_lo = _off_gu(mat, j, 1)
                sync.dma_start(
                    out=wt_hi[wb][:, :],
                    in_=cc0_out.ap()[rank, off_hi:off_hi + GU_TILE]
                    .rearrange("(p f) -> p f", p=128)).then_inc(sWT[wb].h, 16)
                sync.dma_start(
                    out=wt_lo[wb][:, :],
                    in_=cc0_out.ap()[rank, off_lo:off_lo + GU_TILE]
                    .rearrange("(p f) -> p f", p=128)).then_inc(sWT[wb].h, 16)
            em("sync", wld)
            ev_wld_map[oc] = (wb, sWT[wb].inc(32))

        emit_wld(0)
        for oc in range(OI):
            wb = oc % 2
            if oc + 1 < OI:
                emit_wld(oc + 1)
            ev_wld = ev_wld_map[oc]

            for tc in range(NTC):
                pb = (oc * NTC + tc) % 4
                ps = pMM[pb]
                idx = oc * NTC + tc
                psfree = gu_psfree[pb]

                def gmm(tensor, mat=mat, oc=oc, tc=tc, wb=wb, ps=ps,
                        ev_wld=ev_wld, psfree=psfree):
                    if mat == 0 and oc == 0 and tc == 0:
                        tensor.wait_ge(sVE.h, ev_q_all)
                    tensor.wait_ge(sWT[ev_wld[0]].h, ev_wld[1])
                    if psfree is not None:
                        tensor.wait_ge(sSC.h, psfree)
                    mm = None
                    for k in range(KD):
                        tensor.matmul(ps[:, :], wt_hi[wb][:, k * 128:(k + 1) * 128],
                                      qh_fm[k][:, tc * TCH:(tc + 1) * TCH],
                                      start=(k == 0), stop=False)
                        tensor.matmul(ps[:, :], wt_hi[wb][:, k * 128:(k + 1) * 128],
                                      ql_fm[k][:, tc * TCH:(tc + 1) * TCH],
                                      start=False, stop=False)
                        mm = tensor.matmul(ps[:, :], wt_lo[wb][:, k * 128:(k + 1) * 128],
                                           qh_fm[k][:, tc * TCH:(tc + 1) * TCH],
                                           start=False, stop=(k == KD - 1))
                    mm.then_inc(sMM.h, 1)
                em("tensor", gmm)
                ev_mm = sMM.inc(1)
                if tc == NTC - 1:
                    gu_wfree[wb] = ev_mm

                eb = ev_tile[idx % 4]
                ebfree = gu_ebfree[idx % 4]

                def gev(scalar, ps=ps, eb=eb, ev_mm=ev_mm, ebfree=ebfree):
                    scalar.wait_ge(sMM.h, ev_mm)
                    if ebfree is not None:
                        scalar.wait_ge(sST.h, ebfree[0])
                        scalar.wait_ge(sVE.h, ebfree[1])
                    scalar.activation(eb[:, :], ps[:, :], ACT.Copy,
                                      scale=s_s1[:, 0:1]).then_inc(sSC.h, 1)
                em("scalar", gev)
                ev_ev = sSC.inc(1)
                gu_psfree[pb] = ev_ev

                def gsp(scalar, oc=oc, tc=tc, eb=eb, dst=dst_dram, ev_ev=ev_ev):
                    scalar.wait_ge(sSC.h, ev_ev)
                    scalar.dma_start(out=dst.ap()[oc, tc], in_=eb[:, :]).then_inc(sST.h, 16)
                em("scalar", gsp)
                ev_sp = sST.inc(16)

                def gmx(vector, idx=idx, eb=eb, mc=max_cols, ev_ev=ev_ev):
                    vector.wait_ge(sSC.h, ev_ev)
                    vector.tensor_reduce(mc[:, idx:idx + 1], eb[:, :], axis=AX.X,
                                         op=ALU.max,
                                         apply_absolute_value=True).then_inc(sVE.h, 1)
                em("vector", gmx)
                ev_mx = sVE.inc(1)
                gu_ebfree[idx % 4] = (ev_sp, ev_mx)
            if tail_cb is not None:
                tail_cb(oc)
        return sST.n, sVE.n

    ev_gate_sp, ev_gate_mx = emit_gu_phase(0, gate_dram, gmax)

    ev_in3 = lmax_emit(gmax[:, :], 1, 0, sVE, ev_gate_mx)
    ev_cc3 = cc_run(1, ev_in3)
    ev_bc3 = cc_bcast(1, ev_cc3)
    ev_s3 = scale_from_slot(1, 0, s_s3, s_rs3, ev_bc3)

    def s3m_f(vector):
        vector.wait_ge(sVE.h, ev_s3)
        vector.tensor_scalar(s_s3m[:, :], s_s3[:, :], INV65536, None,
                             op0=ALU.mult).then_inc(sVE.h, 1)
    em("vector", s3m_f)
    ev_s3m = sVE.inc(1)

    r1_done = {}
    r1_s_done = {}
    r1_sp_done = {}

    def emit_r1(idx):
        oc, tc = idx // NTC, idx % NTC
        ib = rb_in[idx % 2]
        ob = rb_o[idx % 2]

        def r1ld(sync, oc=oc, tc=tc, ib=ib, idx=idx):
            if idx == 0:
                sync.wait_ge(sST.h, ev_gate_sp)
            if idx >= 2:
                sync.wait_ge(sVE.h, r1_done[idx - 2])
            sync.dma_start(out=ib[:, :], in_=gate_dram.ap()[oc, tc]).then_inc(sLD.h, 16)
        em("sync", r1ld)
        ev_ld = sLD.inc(16)

        def r1v(vector, ib=ib, idx=idx, ev_ld=ev_ld):
            vector.wait_ge(sLD.h, ev_ld)
            if idx == 0:
                vector.wait_ge(sVE.h, ev_s3m)
            if idx >= 1:
                vector.wait_ge(sSC.h, r1_s_done[idx - 1])
            vector.tensor_scalar(rb_t1[:, :], ib[:, :], s_rs3[:, 0:1], MAGIC,
                                 op0=ALU.mult, op1=ALU.add).then_inc(sVE.h, 1)
        em("vector", r1v)
        ev_a = sVE.inc(1)

        def r1s(scalar, ev_a=ev_a, idx=idx):
            scalar.wait_ge(sVE.h, ev_a)
            if idx >= 1:
                scalar.wait_ge(sVE.h, r1_done.get(idx - 1, 0))
            scalar.activation(rb_ig[:, :], rb_t1[:, :], ACT.Copy,
                              bias=-MAGIC, scale=1.0)
            scalar.activation(rb_t2[:, :], rb_ig[:, :], ACT.Sigmoid,
                              bias=0.0, scale=s_s3[:, 0:1])
            scalar.activation(rb_t2[:, :], rb_t2[:, :], ACT.Copy,
                              bias=MAGIC, scale=cst[:, 2:3]).then_inc(sSC.h, 1)
        em("scalar", r1s)
        r1_s_done[idx] = sSC.inc(1)

        def r1v2(vector, ob=ob, idx=idx):
            vector.wait_ge(sSC.h, r1_s_done[idx])
            if idx >= 2:
                vector.wait_ge(sR2.h, r1_sp_done[idx - 2])
            vector.tensor_scalar(rb_t2[:, :], rb_t2[:, :], MAGIC, 65535.0,
                                 op0=ALU.subtract, op1=ALU.min)
            vector.scalar_tensor_tensor(ob[:, :], rb_ig[:, :], s_s3m[:, 0:1],
                                        rb_t2[:, :], op0=ALU.mult, op1=ALU.mult)
            vector.tensor_reduce(g2max[:, idx:idx + 1], ob[:, :], axis=AX.X,
                                 op=ALU.max, apply_absolute_value=True).then_inc(sVE.h, 1)
        em("vector", r1v2)
        r1_done[idx] = sVE.inc(1)

        def r1sp(gpsimd, oc=oc, tc=tc, ob=ob, idx=idx):
            gpsimd.wait_ge(sVE.h, r1_done[idx])
            gpsimd.dma_start(out=g2_dram.ap()[oc, tc], in_=ob[:, :]).then_inc(sR2.h, 16)
        em("gpsimd", r1sp)
        r1_sp_done[idx] = sR2.inc(16)

    def up_tail(oc):
        for tc in range(NTC):
            emit_r1(oc * NTC + tc)

    ev_up_sp, ev_up_mx = emit_gu_phase(1, up_dram, umax, tail_cb=up_tail)
    ev_r1_all = sVE.n
    ev_r1_sp = sR2.n

    lmax_emit(umax[:, :], 2, 0, sVE, ev_up_mx)
    ev_in4 = lmax_emit(g2max[:, :], 2, 1, sVE, ev_r1_all)
    ev_cc4 = cc_run(2, ev_in4)
    ev_bc4 = cc_bcast(2, ev_cc4)
    ev_s2 = scale_from_slot(2, 0, s_s2, s_rs2, ev_bc4)
    ev_s4 = scale_from_slot(2, 1, s_s4, s_rs4, ev_bc4)

    def s42_f(vector):
        vector.wait_ge(sVE.h, max(ev_s2, ev_s4))
        vector.tensor_tensor(s_s42[:, :], s_s4[:, :], s_s2[:, :],
                             op=ALU.mult).then_inc(sVE.h, 1)
    em("vector", s42_f)
    ev_s42 = sVE.inc(1)

    ctxF1.close()

    ctxWD = ExitStack()
    sbW = mk(ctxWD)
    rb2_up = rb_in
    rb2_g2 = [sbW(f"rb2g{i}", [128, TCH], F32) for i in range(2)]
    rb2_i2 = [rb_t1, rb_t2]
    rb2_i4 = [sbW(f"rb2_i4{i}", [128, TCH], F32) for i in range(2)]
    wdh = [sbW(f"wdh{ic}", [128, D], BF16) for ic in range(OI)]
    wdl = [sbW(f"wdl{ic}", [128, D], BF16) for ic in range(OI)]
    oin = [sbW(f"oin{i}", [128, I3], F32) for i in range(2)]
    ohh = [sbW(f"ohh{i}", [128, I3], BF16) for i in range(2)]
    oll = [sbW(f"oll{i}", [128, I3], BF16) for i in range(2)]
    yx1 = [sbW("yx10", [128, D], F32)]
    y2b = [sbW(f"y2_{i}", [128, D], F32) for i in range(2)]

    wd_load_list = []
    for ic in range(OI):
        jj, j = ic // 8, ic % 8
        for r in range(8):
            for part, dst in ((0, wdh), (1, wdl)):
                wd_load_list.append((dst[ic], ic, r, j, jj, part))

    def emit_wd_chunk(k):
        chunk = wd_load_list[4 * k:4 * k + 4]

        def wdc(sync, chunk=chunk, k=k):
            if k == 0:
                sync.wait_ge(sVE.h, ev_up_mx)
            for dtile, ic, r, j, jj, part in chunk:
                base = _off_dn(part) + jj * (128 * 1024)
                sync.dma_start(
                    out=dtile[:, r * 128:(r + 1) * 128],
                    in_=cc0_out.ap()[r, base:base + 128 * 1024]
                    .rearrange("(p j o) -> p j o", j=8, o=128)[:, j:j + 1, :]
                ).then_inc(sWT[0].h, 16)
        em("sync", wdc)
        return sWT[0].inc(16 * len(chunk))

    r2_done = {}
    r2_s_done = {}
    r2_v_done = {}
    r2_g_done = {}
    r2_sp_done = {}
    for idx in range(OI * NTC):
        oc, tc = idx // NTC, idx % NTC
        ub = rb2_up[idx % 2]
        gb = rb2_g2[idx % 2]
        ob = rb_o[idx % 2]
        i2 = rb2_i2[idx % 2]
        i4 = rb2_i4[idx % 2]

        def r2ld(sync, oc=oc, tc=tc, ub=ub, gb=gb, idx=idx):
            if idx == 0:
                sync.wait_ge(sST.h, ev_up_sp)
                sync.wait_ge(sR2.h, ev_r1_sp)
            if idx >= 2:
                sync.wait_ge(sVE.h, r2_v_done[idx - 2])
                sync.wait_ge(sSC.h, r2_s_done[idx - 2])
            sync.dma_start(out=ub[:, :], in_=up_dram.ap()[oc, tc]).then_inc(sLD.h, 16)
            sync.dma_start(out=gb[:, :], in_=g2_dram.ap()[oc, tc]).then_inc(sLD.h, 16)
        em("sync", r2ld)
        ev_ld = sLD.inc(32)
        ev_wd_ld = emit_wd_chunk(idx)

        def r2v(vector, gb=gb, i4=i4, idx=idx, ev_ld=ev_ld):
            vector.wait_ge(sLD.h, ev_ld)
            if idx == 0:
                vector.wait_ge(sVE.h, ev_s42)
            if idx >= 2:
                vector.wait_ge(sGP.h, r2_g_done[idx - 2])
            vector.tensor_scalar(i4[:, :], gb[:, :], s_rs4[:, 0:1], MAGIC,
                                 op0=ALU.mult, op1=ALU.add)
            vector.tensor_scalar(i4[:, :], i4[:, :], MAGIC, s_s42[:, 0:1],
                                 op0=ALU.subtract,
                                 op1=ALU.mult).then_inc(sVE.h, 1)
        em("vector", r2v)
        r2_v_done[idx] = sVE.inc(1)

        def r2s(scalar, ub=ub, i2=i2, idx=idx, ev_ld=ev_ld):
            scalar.wait_ge(sLD.h, ev_ld)
            if idx == 0:
                scalar.wait_ge(sVE.h, ev_s42)
            if idx >= 2:
                scalar.wait_ge(sGP.h, r2_g_done[idx - 2])
            scalar.activation(i2[:, :], ub[:, :], ACT.Copy,
                              bias=MAGIC, scale=s_rs2[:, 0:1])
            scalar.activation(i2[:, :], i2[:, :], ACT.Copy,
                              bias=-MAGIC, scale=1.0).then_inc(sSC.h, 1)
        em("scalar", r2s)
        r2_s_done[idx] = sSC.inc(1)

        def r2g(gpsimd, ob=ob, oc=oc, tc=tc, i2=i2, i4=i4, idx=idx,
                self_ev=sGP.n + 1):
            gpsimd.wait_ge(sSC.h, r2_s_done[idx])
            gpsimd.wait_ge(sVE.h, r2_v_done[idx])
            if idx >= 2:
                gpsimd.wait_ge(sVE.h, r2_done[idx - 2])
                gpsimd.wait_ge(sR2.h, r2_sp_done[idx - 2])
            gpsimd.tensor_tensor(ob[:, :], i4[:, :], i2[:, :],
                                 op=ALU.mult).then_inc(sGP.h, 1)
            gpsimd.wait_ge(sGP.h, self_ev)
            gpsimd.dma_start(out=or_dram.ap()[oc, tc], in_=ob[:, :]).then_inc(sR2.h, 16)
        em("gpsimd", r2g)
        r2_g_done[idx] = sGP.inc(1)
        r2_sp_done[idx] = sR2.inc(16)

        def r2vr(vector, ob=ob, idx=idx):
            vector.wait_ge(sGP.h, r2_g_done[idx])
            vector.tensor_reduce(omax[:, idx:idx + 1], ob[:, :], axis=AX.X,
                                 op=ALU.max, apply_absolute_value=True).then_inc(sVE.h, 1)
        em("vector", r2vr)
        r2_done[idx] = sVE.inc(1)
    ev_r2_all = sVE.n
    ev_r2_sp = sR2.n

    ev_in5 = lmax_emit(omax[:, :], 3, 0, sVE, ev_r2_all)
    ev_cc5 = cc_run(3, ev_in5)
    ev_bc5 = cc_bcast(3, ev_cc5)
    ev_s5 = scale_from_slot(3, 0, s_s5, s_rs5, ev_bc5)

    d_psfree = [None, None, None, None]
    dq_done = {}
    out_sp = {}
    d_ev_a = {}
    for tt in range(NTM):
        oh = ohh[tt % 2]
        ol = oll[tt % 2]
        y2 = y2b[tt % 2]
        xr = yx1[0]
        tc512, sub = tt // 4, tt % 4

        ob_in = oin[tt % 2]

        def d_ld(sync, tt=tt, tc512=tc512, sub=sub, ob_in=ob_in):
            if tt == 0:
                sync.wait_ge(sR2.h, ev_r2_sp)
            if tt >= 2:
                sync.wait_ge(sVE.h, dq_done[tt - 2])
            for ic in range(OI):
                sync.dma_start(
                    out=ob_in[:, ic * 128:(ic + 1) * 128],
                    in_=or_dram.ap()[ic, tc512, :, sub * 128:(sub + 1) * 128]
                ).then_inc(sLD.h, 16)
        em("sync", d_ld)
        ev_ld = sLD.inc(16 * OI)

        def d_ldx(sync, tt=tt, xr=xr):
            if tt >= 1:
                sync.wait_ge(sVE.h, d_y2done[tt - 1])
            sync.dma_start(out=xr[:, :], in_=x1_dram.ap()[tt]).then_inc(sWT[1].h, 16)
        em("sync", d_ldx)
        ev_ldx = sWT[1].inc(16)

        def d_v(vector, tt=tt, ev_ld=ev_ld, ob_in=ob_in):
            vector.wait_ge(sLD.h, ev_ld)
            if tt == 0:
                vector.wait_ge(sVE.h, ev_s5)
            vector.tensor_scalar(ob_in[:, :], ob_in[:, :], s_rs5[:, 0:1], MAGIC,
                                 op0=ALU.mult, op1=ALU.add).then_inc(sVE.h, 1)
        if tt == 0:
            d_mm_done = {}
            d_y2done = {}
        em("vector", d_v)
        ev_a = sVE.inc(1)
        d_ev_a[tt] = ev_a

        def d_s(scalar, ev_a=ev_a, ob_in=ob_in):
            scalar.wait_ge(sVE.h, ev_a)
            scalar.activation(ob_in[:, :], ob_in[:, :], ACT.Copy,
                              bias=-MAGIC, scale=1.0).then_inc(sSC.h, 1)
        em("scalar", d_s)
        ev_o16 = sSC.inc(1)

        def d_v2(vector, oh=oh, ol=ol, tt=tt, ev_o16=ev_o16, ob_in=ob_in):
            vector.wait_ge(sSC.h, ev_o16)
            if tt >= 2:
                vector.wait_ge(sMM.h, d_mm_done[tt - 2])
            vector.tensor_copy(oh[:, :], ob_in[:, :])
            vector.tensor_tensor(ol[:, :], ob_in[:, :], oh[:, :],
                                 op=ALU.subtract).then_inc(sVE.h, 1)
        em("vector", d_v2)
        ev_ohl = sVE.inc(1)
        dq_done[tt] = ev_ohl

        for half in range(2):
            pb = (tt * 2 + half) % 4
            ps = pMM[pb]
            psfree = d_psfree[pb]

            def dmm(tensor, tt=tt, half=half, ps=ps, ev_ohl=ev_ohl,
                    psfree=psfree, oh=oh, ol=ol):
                if tt == 0 and half == 0:
                    tensor.wait_ge(sWT[0].h, ev_wd_ld)
                tensor.wait_ge(sVE.h, ev_ohl)
                if psfree is not None:
                    tensor.wait_ge(sVE.h, psfree)
                mm = None
                for ic in range(OI):
                    tensor.matmul(ps[:, :], oh[:, ic * 128:(ic + 1) * 128],
                                  wdh[ic][:, half * TCH:(half + 1) * TCH],
                                  start=(ic == 0), stop=False)
                    tensor.matmul(ps[:, :], oh[:, ic * 128:(ic + 1) * 128],
                                  wdl[ic][:, half * TCH:(half + 1) * TCH],
                                  start=False, stop=False)
                    mm = tensor.matmul(ps[:, :], ol[:, ic * 128:(ic + 1) * 128],
                                       wdh[ic][:, half * TCH:(half + 1) * TCH],
                                       start=False, stop=(ic == OI - 1))
                mm.then_inc(sMM.h, 1)
            em("tensor", dmm)
            ev_mm = sMM.inc(1)
            if half == 1:
                d_mm_done[tt] = ev_mm

            def dev(vector, half=half, ps=ps, ev_mm=ev_mm, xr=xr, tt=tt,
                    ev_ldx=ev_ldx, y2=y2):
                vector.wait_ge(sMM.h, ev_mm)
                vector.wait_ge(sWT[1].h, ev_ldx)
                if half == 0 and tt >= 2:
                    vector.wait_ge(sST.h, out_sp[tt - 2])
                vector.scalar_tensor_tensor(
                    y2[:, half * TCH:(half + 1) * TCH], ps[:, :], s_s5[:, 0:1],
                    xr[:, half * TCH:(half + 1) * TCH],
                    op0=ALU.mult, op1=ALU.add).then_inc(sVE.h, 1)
            em("vector", dev)
            d_psfree[pb] = sVE.inc(1)
        ev_y2 = sVE.n
        d_y2done[tt] = ev_y2

        _l1 = sVE.n + 1

        def f_v(vector, ev_y2=ev_y2, _l1=_l1, y2=y2):
            vector.wait_ge(sVE.h, ev_y2)
            vector.tensor_reduce(lnsum[:, :], y2[:, :], axis=AX.X,
                                 op=ALU.add).then_inc(sVE.h, 1)
            vector.wait_ge(sVE.h, _l1)
            vector.tensor_scalar(negmu[:, :], lnsum[:, :], -INV1024, None,
                                 op0=ALU.mult).then_inc(sVE.h, 1)
            vector.wait_ge(sVE.h, _l1 + 1)
            vector.tensor_scalar(y2[:, :], y2[:, :], negmu[:, 0:1], None,
                                 op0=ALU.add).then_inc(sVE.h, 1)
        em("vector", f_v)
        sVE.inc(2)
        ev_t3 = sVE.inc(1)

        def f_s(scalar, ev_t3=ev_t3, y2=y2):
            scalar.wait_ge(sVE.h, ev_t3)
            scalar.activation(sqscr[:, :], y2[:, :], ACT.Square,
                              accum_out=ssq[:, :])
            scalar.activation(lnsd[:, :], ssq[:, :], ACT.Sqrt,
                              bias=cst[:, 3:4], scale=cst[:, 6:7]).then_inc(sSC.h, 1)
        em("scalar", f_s)
        ev_sd3 = sSC.inc(1)

        _m1 = sVE.n + 1

        def f_v2(vector, tt=tt, ev_sd3=ev_sd3, _m1=_m1, y2=y2):
            vector.wait_ge(sSC.h, ev_sd3)
            vector.reciprocal(lnr[:, :], lnsd[:, :]).then_inc(sVE.h, 1)
            vector.wait_ge(sVE.h, _m1)
            vector.tensor_scalar(y2[:, :], y2[:, :], lnr[:, 0:1], None,
                                 op0=ALU.mult).then_inc(sVE.h, 1)
        em("vector", f_v2)
        sVE.inc(1)
        ev_oc = sVE.inc(1)

        def f_st(scalar, tt=tt, ev_oc=ev_oc, y2=y2):
            scalar.wait_ge(sVE.h, ev_oc)
            scalar.dma_start(out=out[tt * TMC:(tt + 1) * TMC, :],
                             in_=y2[:, :]).then_inc(sST.h, 16)
        em("scalar", f_st)
        out_sp[tt] = sST.inc(16)
    ev_out_all = sST.n

    def fin(sync):
        sync.wait_ge(sST.h, ev_out_all)
    em("sync", fin)

    ctxWD.close()
    ctxRB.close()
    ctxM.close()

    with nc.Block() as block:
        @block.sync
        def _(sync):
            for fn in EQ["sync"]:
                fn(sync)

        @block.tensor
        def _(tensor):
            for fn in EQ["tensor"]:
                fn(tensor)

        @block.vector
        def _(vector):
            for fn in EQ["vector"]:
                fn(vector)

        @block.scalar
        def _(scalar):
            for fn in EQ["scalar"]:
                fn(scalar)

        @block.gpsimd
        def _(gpsimd):
            for fn in EQ["gpsimd"]:
                fn(gpsimd)

    ctxR.close()
    return nc


_NC_CACHE = None


def kernel(L0_attn_h, L0_attn_Wl, L0_attn_Wr, L0_attn_att_l, L0_attn_att_r,
           L0_attn_Wf, L0_attn_ln_gamma, L0_attn_ln_beta, L0_ffn_w_gate,
           L0_ffn_w_up, L0_ffn_w_down):
    global _NC_CACHE
    h = np.ascontiguousarray(np.asarray(L0_attn_h, dtype=np.float32))
    wr = np.ascontiguousarray(np.asarray(L0_attn_Wr, dtype=np.float32))
    wf = np.ascontiguousarray(np.asarray(L0_attn_Wf, dtype=np.float32))
    attr = np.ascontiguousarray(np.asarray(L0_attn_att_r, dtype=np.float32))
    gamma = np.asarray(L0_attn_ln_gamma, dtype=np.float32).reshape(1, D)
    beta = np.asarray(L0_attn_ln_beta, dtype=np.float32).reshape(1, D)
    wg = np.ascontiguousarray(np.asarray(L0_ffn_w_gate, dtype=np.float32))
    wu = np.ascontiguousarray(np.asarray(L0_ffn_w_up, dtype=np.float32))
    wd = np.ascontiguousarray(np.asarray(L0_ffn_w_down, dtype=np.float32))
    ident = np.eye(128, dtype=np.float32)

    if _NC_CACHE is None:
        _NC_CACHE = build()
    nc = _NC_CACHE

    in_maps = []
    for c in range(N_CORES):
        b, halfseq = c // 2, c % 2
        tok = slice(halfseq * T, (halfseq + 1) * T)
        h_tm = np.ascontiguousarray(h[b, tok, :])
        h_fm = np.ascontiguousarray(h_tm.T)
        wg_sh = np.ascontiguousarray(
            np.concatenate([wg[128 * (c + 8 * j):128 * (c + 8 * j) + 128]
                            for j in range(3)], axis=0))
        wu_sh = np.ascontiguousarray(
            np.concatenate([wu[128 * (c + 8 * j):128 * (c + 8 * j) + 128]
                            for j in range(3)], axis=0))
        wd_sh = np.ascontiguousarray(wd[128 * c:128 * (c + 1)])
        bsel_np = np.zeros((4, 1), dtype=np.float32)
        bsel_np[b, 0] = 1.0
        in_maps.append({
            "h_fm": h_fm, "h_tm": h_tm, "wr_raw": wr, "wf_raw": wf,
            "attr": attr, "gamma": gamma, "beta": beta,
            "wg_sh": wg_sh, "wu_sh": wu_sh, "wd_sh": wd_sh,
            "bsel": bsel_np, "ident_in": ident,
        })

    res = run_bass_kernel_spmd(nc, in_maps, core_ids=list(range(N_CORES)))
    full = np.empty((B, N, D), dtype=np.float32)
    for c in range(N_CORES):
        b, halfseq = c // 2, c % 2
        full[b, halfseq * T:(halfseq + 1) * T, :] = res.results[c]["out"]
    return full
